# version 02
# baseline (speedup 1.0000x reference)
import math
from contextlib import ExitStack

import numpy as np
import ml_dtypes

import concourse.bass as bass
import concourse.mybir as mybir
from concourse.bass_types import AP
from concourse.bass_core_rust import IndirectOffsetOnAxis
from concourse.bass_utils import run_bass_kernel_spmd

F32 = mybir.dt.float32
BF16 = mybir.dt.bfloat16
I32 = mybir.dt.int32
AF = mybir.ActivationFunctionType
ALU = mybir.AluOpType
AX = mybir.AxisListType

NCORES = 8
B = 2
S = 2048
E = 1024
T = B * S
EC = E // 128
NT = T // 128
FMAX = 3.4028235e38

ENGINES = ("sync", "gpsimd", "scalar", "vector", "tensor")

DEBUG_TAPS = []
L1F_RANGE = list(range(18))


class Buf:
    __slots__ = ("w", "rs", "name", "dsem")

    def __init__(self, name=""):
        self.w = None
        self.rs = []
        self.name = name
        self.dsem = None


class Op:
    __slots__ = ("eng", "fn", "dma", "seq", "waits", "awaited", "val", "flush",
                 "dsem")


class Prog:
    SERIAL_DMA = False
    NDSEM = 48

    def __init__(self, nc):
        self.nc = nc
        self.streams = {e: [] for e in ENGINES}
        self.seq = 0
        self.fence = []
        self.last_dma = {e: None for e in ENGINES}
        self.dsem_count = [0] * (Prog.NDSEM + len(ENGINES))
        self.dsem_free = list(range(Prog.NDSEM))
        self.dsem_assigned = []

    def _slot_sem(self, buf):
        if buf is None:
            return None
        if buf.dsem is None:
            if not self.dsem_free:
                raise RuntimeError("out of DMA slot sems; add a barrier or bump NDSEM")
            buf.dsem = self.dsem_free.pop()
            self.dsem_assigned.append(buf)
        return buf.dsem

    def _mk(self, eng, fn, dma, reads, writes, after, flush=False):
        deps = set()
        for b in reads:
            if b.w is not None:
                deps.add(b.w)
        for b in writes:
            if b.w is not None:
                deps.add(b.w)
            deps.update(b.rs)
        deps.update(after)
        deps.update(self.fence)
        o = Op()
        o.eng, o.fn, o.dma = eng, fn, dma
        o.flush = flush
        o.seq = self.seq
        self.seq += 1
        o.waits = list(deps)
        o.awaited = dma
        o.val = None
        self.streams[eng].append(o)
        for b in reads:
            b.rs.append(o)
        for b in writes:
            b.w = o
            b.rs = []
        return o

    def op(self, eng, fn, reads=(), writes=(), after=(), flush=False):
        return self._mk(eng, fn, False, reads, writes, after, flush)

    def dma(self, queue, out=None, in_=None, reads=(), writes=(), after=(),
            raw=None, slot=None, **kw):
        if raw is None:
            def fn(e, o=out, i=in_, kw=kw):
                return e.dma_start(out=o, in_=i, **kw)
        else:
            fn = raw
        if Prog.SERIAL_DMA and self.last_dma[queue] is not None:
            after = tuple(after) + (self.last_dma[queue],)
        o = self._mk(queue, fn, True, reads, writes, after)
        sid = self._slot_sem(slot)
        if sid is None:
            sid = Prog.NDSEM + ENGINES.index(queue)
        o.dsem = sid
        self.dsem_count[sid] += 16
        o.val = self.dsem_count[sid]
        self.last_dma[queue] = o
        return o

    def barrier(self):
        self.fence = [self.streams[e][-1] for e in ENGINES if self.streams[e]]
        for b in self.dsem_assigned:
            self.dsem_free.append(b.dsem)
            b.dsem = None
        self.dsem_assigned = []

    def playback(self, block, sems):
        for e in ENGINES:
            for o in self.streams[e]:
                for d in o.waits:
                    if d.eng != o.eng or d.dma:
                        d.awaited = True
        for e in ENGINES:
            c = 0
            for o in self.streams[e]:
                if not o.dma and o.awaited:
                    c += 1
                    o.val = c
        def emit(e):
            def run(engobj):
                seen = {}
                for o in self.streams[e]:
                    for d in sorted(o.waits, key=lambda d: d.seq):
                        if d.eng == e and not d.dma:
                            continue
                        key = ("s", d.dsem) if d.dma else ("c", d.eng)
                        if seen.get(key, 0) >= d.val:
                            continue
                        engobj.wait_ge(sems[key], d.val)
                        seen[key] = d.val
                    ins = o.fn(engobj)
                    if not o.dma and (o.flush or
                                      (o.awaited and e in ("vector", "scalar",
                                                          "gpsimd"))):
                        ins = engobj.drain(fusable=False)
                    if o.dma:
                        ins.then_inc(sems[("s", o.dsem)], 16)
                    elif o.awaited:
                        ins.then_inc(sems[("c", e)], 1)
            return run

        for e in ENGINES:
            if self.streams[e]:
                getattr(block, e)(emit(e))


class Ctx:

    def __init__(self, nc, P, es):
        self.nc = nc
        self.P = P
        self.es = es
        self.W = {}
        self.dram_cache = {}
        self.sb_scopes = []
        self._names = {}

    def dram(self, name, shape, dtype):
        key = name
        if key not in self.dram_cache:
            t = self.nc.dram_tensor(name, list(shape), dtype, kind="Internal")
            self.dram_cache[key] = (t.ap(), Buf(name))
        return self.dram_cache[key]

    def _uniq(self, name):
        n = self._names.get(name, 0)
        self._names[name] = n + 1
        return name if n == 0 else f"{name}__{n}"

    def sb(self, name, shape, dtype, es=None, side=None):
        name = self._uniq(name)
        t = (es or self.es).enter_context(
            self.nc.sbuf_tensor(name, list(shape), dtype, side=side))
        return t, Buf(name)

    def ps(self, name, shape, dtype=F32, es=None, side=None):
        name = self._uniq(name)
        t = (es or self.es).enter_context(
            self.nc.psum_tensor(name, list(shape), dtype, side=side))
        return t, Buf(name)


def mkap(base, offset_add, freedims):
    return AP(base.tensor, base.offset + offset_add,
              [list(base.ap[0])] + [list(f) for f in freedims])


class Shared:

    def __init__(self, C):
        nc, P = C.nc, C.P
        self.identb, self.identb_b = C.sb("g_identb", [128, 128], BF16)
        self.identf, self.identf_b = C.sb("g_identf", [128, 128], F32)
        self.onesrow, self.onesrow_b = C.sb("g_onesrow", [1, 128], F32)
        self.onesrow_bf, self.onesrow_bf_b = C.sb("g_onesrow_bf", [1, 128], BF16)
        self.onescol, self.onescol_b = C.sb("g_onescol", [128, 1], F32)
        self.eps, self.eps_b = C.sb("g_eps", [128, 4], F32)
        self.zero512, self.zero512_b = C.sb("g_zero512", [128, 512], F32)

        def setup(g):
            g.memset(self.identb[:], 1.0)
            g.affine_select(out=self.identb[:], in_=self.identb[:],
                            compare_op=ALU.is_equal, fill=0.0, base=0,
                            pattern=[[-1, 128]], channel_multiplier=1)
            g.memset(self.identf[:], 1.0)
            g.affine_select(out=self.identf[:], in_=self.identf[:],
                            compare_op=ALU.is_equal, fill=0.0, base=0,
                            pattern=[[-1, 128]], channel_multiplier=1)
            g.memset(self.onesrow[:], 1.0)
            g.memset(self.onesrow_bf[:], 1.0)
            g.memset(self.onescol[:], 1.0)
            g.memset(self.eps[:, 0:1], 1e-5)
            g.memset(self.eps[:, 1:2], 1e-12)
            g.memset(self.eps[:, 2:3], 0.0)
            g.memset(self.eps[:, 3:4], 1e-20)
            return g.memset(self.zero512[:], 0.0)

        wr = [self.identb_b, self.identf_b, self.onesrow_b, self.onesrow_bf_b,
              self.onescol_b, self.eps_b, self.zero512_b]
        P.op("gpsimd", setup, writes=wr)


def ln_tile(C, G, v_x, b_x, v_out, b_out, stat, b_stat, sq, b_sq, width=E,
            out_dtype_note=None):
    P = C.P
    P.op("vector", lambda v: v.reduce_sum(stat[:, 0:1], v_x, axis=AX.X),
         reads=[b_x], writes=[b_stat], flush=True)

    def sqacc(s):
        s.activation(sq, v_x, AF.Square, accum_out=stat[:, 2:3])
        s.drain()
        return s.mul(stat[:, 3:4], stat[:, 2:3], 1.0 / width)

    P.op("scalar", sqacc, reads=[b_x], writes=[b_sq, b_stat], flush=True)

    def meansq(v):
        v.tensor_scalar_mul(stat[:, 1:2], stat[:, 0:1], 1.0 / width)
        return v.scalar_tensor_tensor(stat[:, 7:8], stat[:, 0:1],
                                      1.0 / (width * width), stat[:, 0:1],
                                      ALU.mult, ALU.mult)

    P.op("vector", meansq, reads=[b_stat], writes=[b_stat], flush=True)
    P.op("vector", lambda v: v.tensor_sub(stat[:, 4:5], stat[:, 3:4],
                                          stat[:, 7:8]),
         reads=[b_stat], writes=[b_stat], flush=True)
    P.op("scalar", lambda s: s.activation(stat[:, 5:6], stat[:, 4:5], AF.Sqrt,
                                          bias=G.eps[:, 0:1]),
         reads=[b_stat, G.eps_b], writes=[b_stat], flush=True)
    P.op("vector", lambda v: v.reciprocal(stat[:, 6:7], stat[:, 5:6]),
         reads=[b_stat], writes=[b_stat], flush=True)
    P.op("vector", lambda v: v.tensor_scalar(v_out, v_x, stat[:, 1:2],
                                             stat[:, 6:7], ALU.subtract,
                                             ALU.mult),
         reads=[b_x, b_stat], writes=[b_out])


def transpose_to_fm(C, G, v_src_bf, b_src, pst, b_pst, v_dst, b_dst, nchunks=8):
    P = C.P

    def tr(t):
        last = None
        for c in range(nchunks):
            last = t.transpose(pst[:, c * 128:(c + 1) * 128],
                               v_src_bf[:, c * 128:(c + 1) * 128], G.identb[:])
        return last

    P.op("tensor", tr, reads=[b_src, G.identb_b], writes=[b_pst])
    P.op("vector", lambda v: v.tensor_copy(v_dst, pst[:, 0:nchunks * 128]),
         reads=[b_pst], writes=[b_dst])


def prep_wT(C, G, name, w_ap, din, dout, es, dtype=BF16, es_tmp=None):
    P = C.P
    wt, b_wt = C.sb(name, [128, din // 128, dout], dtype, es=es)
    stg, b_stg = C.sb(name + "_stg", [128, 512], dtype, es=es_tmp or es,
                      side="right" if es_tmp else None)
    pst, b_pst = C.ps(name + "_pst", [128, 512], dtype, es=es_tmp or es,
                      side="right" if es_tmp else None)
    ident = G.identb if dtype == BF16 else G.identf
    b_ident = G.identb_b if dtype == BF16 else G.identf_b
    queue = "gpsimd" if dtype == BF16 else "sync"
    for o0 in range(0, dout, 128):
        on = min(128, dout - o0)
        for c0 in range(0, din, 512):
            cn = min(512, din - c0)
            P.dma(queue, out=stg[0:on, 0:cn], in_=w_ap[o0:o0 + on, c0:c0 + cn],
                  writes=[b_stg], slot=b_stg)

            def tr(t, on=on, cn=cn):
                last = None
                for cc in range(0, cn, 128):
                    last = t.transpose(pst[:, cc:cc + on],
                                       stg[0:on, cc:cc + 128],
                                       ident[0:on, 0:on])
                return last
            P.op("tensor", tr, reads=[b_stg, b_ident], writes=[b_pst])

            def ev(v, o0=o0, on=on, c0=c0, cn=cn):
                last = None
                for cc in range(0, cn, 128):
                    last = v.tensor_copy(wt[:, (c0 + cc) // 128, o0:o0 + on],
                                         pst[:, cc:cc + on])
                return last
            P.op("vector", ev, reads=[b_pst], writes=[b_wt])
    return wt, b_wt


def load_w_natural(C, name, w_ap, din, dout, es, dtype=BF16):
    P = C.P
    wt, b_wt = C.sb(name, [128, din // 128, dout], dtype, es=es)
    P.dma("gpsimd", out=wt[:],
          in_=w_ap.rearrange("(c p) o -> p c o", p=128), writes=[b_wt], slot=b_wt)
    return wt, b_wt


def bcast_row(C, G, name, row_ap, n, es, dtype=F32, es_tmp=None):
    P = C.P
    rw, b_rw = C.sb(name + "_r", [1, n], F32, es=es_tmp or es,
                    side="right" if es_tmp else None)
    out, b_out = C.sb(name, [128, n], dtype, es=es)
    P.dma("sync", out=rw[:], in_=row_ap.unsqueeze(0) if row_ap.ndim == 1 else row_ap,
          writes=[b_rw], slot=b_rw)
    pst, b_pst = C.ps(name + "_ps", [128, 512], F32, es=es_tmp or es,
                      side="right" if es_tmp else None)
    for c0 in range(0, n, 512):
        cn = min(512, n - c0)
        P.op("tensor", lambda t, c0=c0, cn=cn: t.matmul(
            pst[:, 0:cn], G.onesrow[0:1, :], rw[0:1, c0:c0 + cn],
            start=True, stop=True),
            reads=[b_rw, G.onesrow_b], writes=[b_pst])
        P.op("vector", lambda v, c0=c0, cn=cn: v.tensor_copy(
            out[:, c0:c0 + cn], pst[:, 0:cn]), reads=[b_pst], writes=[b_out])
    return out, b_out


def stage_prep_xn(C, G, resid, b_resid, want_fm=None, want_tm32=None,
                  want_tmbf=None, want_fm32=None):
    P = C.P
    with ExitStack() as es:
        x, b_x = [list(z) for z in zip(C.sb("px_x0", [128, E], F32, es=es),
                                       C.sb("px_x1", [128, E], F32, es=es))]
        xn, b_xn = [list(z) for z in zip(C.sb("px_n0", [128, E], F32, es=es),
                                         C.sb("px_n1", [128, E], F32, es=es))]
        xb, b_xb = [list(z) for z in zip(C.sb("px_b0", [128, E], BF16, es=es),
                                         C.sb("px_b1", [128, E], BF16, es=es))]
        xf, b_xf = [list(z) for z in zip(C.sb("px_f0", [128, EC, 128], BF16, es=es),
                                         C.sb("px_f1", [128, EC, 128], BF16, es=es))]
        sq, b_sq = C.sb("px_sq", [128, E], F32, es=es)
        st, b_st = [list(z) for z in zip(C.sb("px_s0", [128, 8], F32, es=es),
                                         C.sb("px_s1", [128, 8], F32, es=es))]
        pst, b_pst = [list(z) for z in zip(C.ps("px_p0", [128, E], BF16, es=es),
                                           C.ps("px_p1", [128, E], BF16, es=es))]
        if want_fm32 is not None:
            xf32, b_xf32 = [list(z) for z in zip(
                C.sb("px_g0", [128, EC, 128], F32, es=es),
                C.sb("px_g1", [128, EC, 128], F32, es=es))]
            pst32, b_pst32 = [list(z) for z in zip(
                C.ps("px_q0", [128, E], F32, es=es),
                C.ps("px_q1", [128, E], F32, es=es))]
        for i in range(NT):
            j = i % 2
            P.dma("sync", out=x[j][:], in_=resid[i * 128:(i + 1) * 128, :],
                  reads=[b_resid], writes=[b_x[j]], slot=b_x[j])
            ln_tile(C, G, x[j][:], b_x[j], xn[j][:], b_xn[j], st[j], b_st[j],
                    sq[:], b_sq)
            P.op("vector", lambda v, j=j: v.tensor_copy(xb[j][:], xn[j][:]),
                 reads=[b_xn[j]], writes=[b_xb[j]])
            if want_tm32 is not None:
                P.dma("gpsimd", out=want_tm32[0][i * 128:(i + 1) * 128, :],
                      in_=xn[j][:], reads=[b_xn[j]], writes=[want_tm32[1]], slot=b_xn[j])
            if want_tmbf is not None:
                P.dma("gpsimd", out=want_tmbf[0][i * 128:(i + 1) * 128, :],
                      in_=xb[j][:], reads=[b_xb[j]], writes=[want_tmbf[1]], slot=b_xb[j])
            bb = i // (S // 128)
            t0 = (i % (S // 128)) * 128
            if want_fm is not None:
                transpose_to_fm(C, G, xb[j][:], b_xb[j], pst[j][:], b_pst[j],
                                xf[j][:], b_xf[j])
                P.dma("gpsimd",
                      out=want_fm[0][bb, :, :, t0:t0 + 128].rearrange("c p t -> p c t"),
                      in_=xf[j][:], reads=[b_xf[j]], writes=[want_fm[1]], slot=b_xf[j])
            if want_fm32 is not None:
                def trf(t, j=j):
                    last = None
                    for c in range(EC):
                        last = t.transpose(pst32[j][:, c * 128:(c + 1) * 128],
                                           xn[j][:, c * 128:(c + 1) * 128],
                                           G.identf[:])
                    return last
                P.op("tensor", trf, reads=[b_xn[j], G.identf_b],
                     writes=[b_pst32[j]])
                P.op("vector", lambda v, j=j: v.tensor_copy(
                    xf32[j][:], pst32[j][:]), reads=[b_pst32[j]],
                    writes=[b_xf32[j]])
                P.dma("gpsimd",
                      out=want_fm32[0][bb, :, :, t0:t0 + 128].rearrange("c p t -> p c t"),
                      in_=xf32[j][:], reads=[b_xf32[j]], writes=[want_fm32[1]],
                      slot=b_xf32[j])


def stage_post_ln_add(C, G, y, b_y, resid, b_resid):
    P = C.P
    with ExitStack() as es:
        yv, b_yv = [list(z) for z in zip(C.sb("po_y0", [128, E], F32, es=es),
                                         C.sb("po_y1", [128, E], F32, es=es))]
        rv, b_rv = [list(z) for z in zip(C.sb("po_r0", [128, E], F32, es=es),
                                         C.sb("po_r1", [128, E], F32, es=es))]
        yn, b_yn = [list(z) for z in zip(C.sb("po_n0", [128, E], F32, es=es),
                                         C.sb("po_n1", [128, E], F32, es=es))]
        rs, b_rs = [list(z) for z in zip(C.sb("po_o0", [128, E], F32, es=es),
                                         C.sb("po_o1", [128, E], F32, es=es))]
        sq, b_sq = C.sb("po_sq", [128, E], F32, es=es)
        st, b_st = [list(z) for z in zip(C.sb("po_s0", [128, 8], F32, es=es),
                                         C.sb("po_s1", [128, 8], F32, es=es))]
        for i in range(NT):
            j = i % 2
            P.dma("sync", out=yv[j][:], in_=y[i * 128:(i + 1) * 128, :],
                  reads=[b_y], writes=[b_yv[j]], slot=b_yv[j])
            P.dma("sync", out=rv[j][:], in_=resid[i * 128:(i + 1) * 128, :],
                  reads=[b_resid], writes=[b_rv[j]], slot=b_rv[j])
            ln_tile(C, G, yv[j][:], b_yv[j], yn[j][:], b_yn[j], st[j], b_st[j],
                    sq[:], b_sq)
            P.op("vector", lambda v, j=j: v.tensor_add(rs[j][:], rv[j][:], yn[j][:]),
                 reads=[b_yn[j], b_rv[j]], writes=[b_rs[j]])
            P.dma("gpsimd", out=resid[i * 128:(i + 1) * 128, :], in_=rs[j][:],
                  reads=[b_rs[j]], writes=[b_resid], slot=b_rs[j])


def emit_L0_attn(C, G, resid, b_resid):
    P = C.P
    H, HD = 16, 64
    xnfm = C.dram("l0a_xnfm", (B, EC, 128, S), BF16)
    qfm = C.dram("l0a_q", (B, EC, 128, S), BF16)
    kfm = C.dram("l0a_k", (B, EC, 128, S), BF16)
    vtm = C.dram("l0a_v", (B, S, E), BF16)
    ofm = C.dram("l0a_o", (B, EC, 128, S), BF16)
    y = C.dram("l0a_y", (T, E), F32)

    stage_prep_xn(C, G, resid, b_resid, want_fm=xnfm)
    P.barrier()

    with ExitStack() as es:
        qwT, b_qwT = prep_wT(C, G, "l0a_qwT", C.W["L0_attn_q_w"], E, E, es)
        kwT, b_kwT = prep_wT(C, G, "l0a_kwT", C.W["L0_attn_k_w"], E, E, es)
        vwT, b_vwT = prep_wT(C, G, "l0a_vwT", C.W["L0_attn_v_w"], E, E, es)
        xsb, b_xsb = C.sb("l0a_xsb", [128, EC, S], BF16, es=es)
        stg, b_stg = [list(z) for z in zip(C.sb("l0a_st0", [128, 512], BF16, es=es),
                                           C.sb("l0a_st1", [128, 512], BF16, es=es))]
        psq, b_psq = [list(z) for z in zip(C.ps("l0a_ps0", [128, 512], F32, es=es),
                                           C.ps("l0a_ps1", [128, 512], F32, es=es))]
        for bb in range(B):
            P.dma("sync", out=xsb[:],
                  in_=xnfm[0][bb].rearrange("c p t -> p c t"),
                  reads=[xnfm[1]], writes=[b_xsb], slot=b_xsb)
            for which, wt, b_wt, dst in (("q", qwT, b_qwT, qfm),
                                         ("k", kwT, b_kwT, kfm)):
                for ic in range(EC):
                    for ts in range(S // 512):
                        j = (ic * (S // 512) + ts) % 2

                        def mm(t, ic=ic, ts=ts, j=j, wt=wt, psq=psq):
                            last = None
                            for ec in range(EC):
                                last = t.matmul(
                                    psq[j][:],
                                    wt[:, ec, ic * 128:(ic + 1) * 128],
                                    xsb[:, ec, ts * 512:(ts + 1) * 512],
                                    start=(ec == 0), stop=(ec == EC - 1))
                            return last
                        P.op("tensor", mm, reads=[b_wt, b_xsb], writes=[b_psq[j]])
                        P.op("vector", lambda v, j=j, stg=stg, psq=psq: v.tensor_copy(
                            stg[j][:], psq[j][:]), reads=[b_psq[j]],
                            writes=[b_stg[j]])
                        P.dma("gpsimd",
                              out=dst[0][bb, ic, :, ts * 512:(ts + 1) * 512],
                              in_=stg[j][:], reads=[b_stg[j]], writes=[dst[1]],
                              slot=b_stg[j])
            for tt in range(S // 128):
                for ds in range(E // 512):
                    j = (tt * (E // 512) + ds) % 2

                    def mmv(t, tt=tt, ds=ds, j=j, psq=psq):
                        last = None
                        for ec in range(EC):
                            last = t.matmul(
                                psq[j][:],
                                xsb[:, ec, tt * 128:(tt + 1) * 128],
                                vwT[:, ec, ds * 512:(ds + 1) * 512],
                                start=(ec == 0), stop=(ec == EC - 1))
                        return last
                    P.op("tensor", mmv, reads=[b_vwT, b_xsb], writes=[b_psq[j]])
                    P.op("vector", lambda v, j=j, stg=stg, psq=psq: v.tensor_copy(
                        stg[j][:], psq[j][:]), reads=[b_psq[j]], writes=[b_stg[j]])
                    P.dma("gpsimd",
                          out=vtm[0][bb, tt * 128:(tt + 1) * 128,
                                     ds * 512:(ds + 1) * 512],
                          in_=stg[j][:], reads=[b_stg[j]], writes=[vtm[1]],
                          slot=b_stg[j])
    P.barrier()

    NKC = S // 512
    with ExitStack() as es:
        qsb, b_qsb = C.sb("l0a_at_q", [64, S], BF16, es=es)
        ksb, b_ksb = C.sb("l0a_at_k", [64, S], BF16, es=es)
        vsb, b_vsb = C.sb("l0a_at_v", [128, S // 128, HD], BF16, es=es)
        ssb, b_ssb = C.sb("l0a_at_s", [128, S], F32, es=es)
        pbf, b_pbf = C.sb("l0a_at_p", [128, S], BF16, es=es)
        ptb, b_ptb = C.sb("l0a_at_pt", [128, S // 128, 128], BF16, es=es)
        osb, b_osb = C.sb("l0a_at_o", [64, 128], BF16, es=es)
        stt, b_stt = C.sb("l0a_at_st", [128, 8], F32, es=es)
        psS = [C.ps(f"l0a_at_psS{i}", [128, 512], F32, es=es) for i in range(4)]
        psT = [C.ps(f"l0a_at_psT{i}", [128, 1024], BF16, es=es) for i in range(2)]
        psO, b_psO = C.ps("l0a_at_psO", [64, 128], F32, es=es)
        for bb in range(B):
            for h in range(H):
                ic, off = h // 2, (h % 2) * 64
                P.dma("sync", out=qsb[:], in_=qfm[0][bb, ic, off:off + 64, :],
                      reads=[qfm[1]], writes=[b_qsb], slot=b_qsb)
                P.dma("sync", out=ksb[:], in_=kfm[0][bb, ic, off:off + 64, :],
                      reads=[kfm[1]], writes=[b_ksb], slot=b_ksb)
                P.dma("sync", out=vsb[:],
                      in_=vtm[0][bb, :, h * 64:(h + 1) * 64].rearrange(
                          "(c p) d -> p c d", p=128),
                      reads=[vtm[1]], writes=[b_vsb], slot=b_vsb)
                for qt in range(S // 128):
                    for kc in range(NKC):
                        P.op("tensor", lambda t, kc=kc, qt=qt: t.matmul(
                            psS[kc][0][:], qsb[0:64, qt * 128:(qt + 1) * 128],
                            ksb[0:64, kc * 512:(kc + 1) * 512],
                            start=True, stop=True),
                            reads=[b_qsb, b_ksb], writes=[psS[kc][1]])
                    def ev(s):
                        last = None
                        for kc in range(4):
                            last = s.copy(ssb[:, kc * 512:(kc + 1) * 512],
                                          psS[kc][0][:])
                        return last
                    P.op("scalar", ev, reads=[psS[i][1] for i in range(4)],
                         writes=[b_ssb])
                    def mx1(v):
                        v.reduce_max(stt[:, 5:6], ssb[:, 0:1984], axis=AX.X)
                        return v.reduce_max(stt[:, 4:5], ssb[:, 1984:2048],
                                            axis=AX.X)
                    P.op("vector", mx1, reads=[b_ssb], writes=[b_stt], flush=True)
                    P.op("vector", lambda v: v.tensor_tensor(
                        stt[:, 6:7], stt[:, 5:6], stt[:, 4:5], ALU.max),
                        reads=[b_stt], writes=[b_stt], flush=True)
                    def pre(v):
                        v.tensor_scalar_mul(stt[:, 7:8], stt[:, 6:7], -0.125)
                        return v.tensor_sub(stt[:, 1:2], stt[:, 5:6],
                                            stt[:, 6:7])
                    P.op("vector", pre, reads=[b_stt], writes=[b_stt], flush=True)

                    def exp64(s):
                        s.activation(pbf[:, 1984:2048], ssb[:, 1984:2048],
                                     AF.Exp, bias=stt[:, 7:8], scale=0.125,
                                     accum_out=stt[:, 0:1])
                        s.drain()
                        s.mul(stt[:, 3:4], stt[:, 0:1], 1.0)
                        return s.activation(stt[:, 2:3], stt[:, 1:2], AF.Exp,
                                            scale=0.125)
                    P.op("scalar", exp64, reads=[b_ssb, b_stt],
                         writes=[b_pbf, b_stt], flush=True)
                    P.op("vector", lambda v: v.tensor_scalar_add(
                        stt[:, 5:6], stt[:, 2:3], 1.0),
                        reads=[b_stt], writes=[b_stt], flush=True)
                    P.op("vector", lambda v: v.tensor_tensor(
                        stt[:, 6:7], stt[:, 3:4], stt[:, 5:6], ALU.mult),
                        reads=[b_stt], writes=[b_stt], flush=True)
                    P.op("scalar", lambda s: s.activation(
                        stt[:, 4:5], stt[:, 6:7], AF.Ln),
                        reads=[b_stt], writes=[b_stt], flush=True)
                    P.op("vector", lambda v: v.tensor_sub(
                        stt[:, 7:8], stt[:, 7:8], stt[:, 4:5]),
                        reads=[b_stt], writes=[b_stt], flush=True)
                    P.op("scalar", lambda s: s.activation(
                        pbf[:, 0:1984], ssb[:, 0:1984], AF.Exp,
                        bias=stt[:, 7:8], scale=0.125),
                        reads=[b_ssb, b_stt], writes=[b_pbf])
                    P.op("scalar", lambda s: s.activation(
                        pbf[:, 1984:2048], ssb[:, 1984:2048], AF.Exp,
                        bias=stt[:, 7:8], scale=0.125),
                        reads=[b_ssb, b_stt], writes=[b_pbf])

                    for half in range(2):
                        def trp(t, half=half):
                            last = None
                            for c in range(8):
                                last = t.transpose(
                                    psT[half][0][:, c * 128:(c + 1) * 128],
                                    pbf[:, (half * 8 + c) * 128:(half * 8 + c + 1) * 128],
                                    G.identb[:])
                            return last
                        P.op("tensor", trp, reads=[b_pbf, G.identb_b],
                             writes=[psT[half][1]])
                        P.op("vector", lambda v, half=half: v.tensor_copy(
                            ptb[:, half * 8:(half + 1) * 8, :], psT[half][0][:]),
                            reads=[psT[half][1]], writes=[b_ptb])

                    def pv(t):
                        last = None
                        for kc in range(S // 128):
                            last = t.matmul(psO[:], vsb[:, kc, :],
                                            ptb[:, kc, :],
                                            start=(kc == 0),
                                            stop=(kc == S // 128 - 1))
                        return last
                    P.op("tensor", pv, reads=[b_vsb, b_ptb], writes=[b_psO])
                    P.op("vector", lambda v: v.tensor_copy(osb[0:64, :], psO[0:64, :]),
                         reads=[b_psO], writes=[b_osb])
                    P.dma("gpsimd",
                          out=ofm[0][bb, ic, off:off + 64,
                                     qt * 128:(qt + 1) * 128],
                          in_=osb[:], reads=[b_osb], writes=[ofm[1]],
                          slot=b_osb)
    P.barrier()

    with ExitStack() as es:
        owT, b_owT = prep_wT(C, G, "l0a_owT", C.W["L0_attn_out_w"], E, E, es)
        ob_bc, b_ob_bc = bcast_row(C, G, "l0a_ob", C.W["L0_attn_out_b"], E, es)
        poxb, b_poxb = C.sb("l0a_po_x", [128, EC, S], BF16, es=es)
        yst, b_yst = [list(z) for z in zip(C.sb("l0a_po_y0", [128, 512], F32, es=es),
                                           C.sb("l0a_po_y1", [128, 512], F32, es=es))]
        pops, b_pops = [list(z) for z in zip(C.ps("l0a_po_p0", [128, 512], F32, es=es),
                                             C.ps("l0a_po_p1", [128, 512], F32, es=es))]
        for bb in range(B):
            P.dma("sync", out=poxb[:], in_=ofm[0][bb].rearrange("c p t -> p c t"),
                  reads=[ofm[1]], writes=[b_poxb], slot=b_poxb)
            for tt in range(S // 128):
                for ds in range(E // 512):
                    j = (tt * (E // 512) + ds) % 2

                    def mm5(t, tt=tt, ds=ds, j=j, pops=pops, poxb=poxb, owT=owT):
                        last = None
                        for ic in range(EC):
                            last = t.matmul(
                                pops[j][:], poxb[:, ic, tt * 128:(tt + 1) * 128],
                                owT[:, ic, ds * 512:(ds + 1) * 512],
                                start=(ic == 0), stop=(ic == EC - 1))
                        return last
                    P.op("tensor", mm5, reads=[b_owT, b_poxb], writes=[b_pops[j]])
                    P.op("vector", lambda v, j=j, ds=ds, yst=yst, pops=pops, ob_bc=ob_bc: v.tensor_add(
                        yst[j][:], pops[j][:], ob_bc[:, ds * 512:(ds + 1) * 512]),
                        reads=[b_pops[j], b_ob_bc], writes=[b_yst[j]])
                    P.dma("gpsimd",
                          out=y[0][bb * S + tt * 128: bb * S + (tt + 1) * 128,
                                   ds * 512:(ds + 1) * 512],
                          in_=yst[j][:], reads=[b_yst[j]], writes=[y[1]],
                          slot=b_yst[j])
    P.barrier()
    stage_post_ln_add(C, G, y[0], y[1], resid, b_resid)
    P.barrier()



def col_vec(C, G, name, vec_ap, n, es):
    P = C.P
    nchunks = n // 128
    rw, b_rw = C.sb(name + "_r", [1, n], F32, es=es)
    out, b_out = C.sb(name, [128, nchunks], F32, es=es)
    pst, b_pst = C.ps(name + "_ps", [128, nchunks], F32, es=es)
    P.dma("sync", out=rw[:], in_=vec_ap.unsqueeze(0) if vec_ap.ndim == 1 else vec_ap,
          writes=[b_rw], slot=b_rw)

    def tr(t):
        last = None
        for c in range(nchunks):
            last = t.transpose(pst[:, c:c + 1], rw[0:1, c * 128:(c + 1) * 128],
                               G.identf[0:1, 0:1])
        return last
    P.op("tensor", tr, reads=[b_rw, G.identf_b], writes=[b_pst])
    P.op("vector", lambda v: v.tensor_copy(out[:], pst[:]),
         reads=[b_pst], writes=[b_out])
    return out, b_out


def emit_L3_attn(C, G, resid, b_resid):
    P = C.P
    H, HD = 16, 64
    xnfm = C.dram("l3a_xnfm", (B, EC, 128, S), BF16)
    qfm = C.dram("l3a_q", (B, EC, 128, S), BF16)
    kfm = C.dram("l3a_k", (B, EC, 128, S), BF16)
    vtm = C.dram("l3a_v", (B, S, E), BF16)
    ofm = C.dram("l3a_o", (B, EC, 128, S), BF16)
    y = C.dram("l3a_y", (T, E), F32)

    stage_prep_xn(C, G, resid, b_resid, want_fm=xnfm)
    P.barrier()

    with ExitStack() as es:
        qwT, b_qwT = prep_wT(C, G, "l3a_qwT", C.W["L3_attn_Wq"], E, E, es)
        kwT, b_kwT = prep_wT(C, G, "l3a_kwT", C.W["L3_attn_Wk"], E, E, es)
        vwT, b_vwT = prep_wT(C, G, "l3a_vwT", C.W["L3_attn_Wv"], E, E, es)
        bqc, b_bqc = col_vec(C, G, "l3a_bq", C.W["L3_attn_bq"], E, es)
        bkc, b_bkc = col_vec(C, G, "l3a_bk", C.W["L3_attn_bk"], E, es)
        bv_bc, b_bv_bc = bcast_row(C, G, "l3a_bv", C.W["L3_attn_bv"], E, es)
        xsb, b_xsb = C.sb("l3a_xsb", [128, EC, S], BF16, es=es)
        stg, b_stg = [list(z) for z in zip(C.sb("l3a_st0", [128, 512], BF16, es=es),
                                           C.sb("l3a_st1", [128, 512], BF16, es=es))]
        psq, b_psq = [list(z) for z in zip(C.ps("l3a_ps0", [128, 512], F32, es=es),
                                           C.ps("l3a_ps1", [128, 512], F32, es=es))]
        for bb in range(B):
            P.dma("sync", out=xsb[:],
                  in_=xnfm[0][bb].rearrange("c p t -> p c t"),
                  reads=[xnfm[1]], writes=[b_xsb], slot=b_xsb)
            for which, wt, b_wt, bc, b_bc, dst in (
                    ("q", qwT, b_qwT, bqc, b_bqc, qfm),
                    ("k", kwT, b_kwT, bkc, b_bkc, kfm)):
                for ic in range(EC):
                    for ts in range(S // 512):
                        j = (ic * (S // 512) + ts) % 2

                        def mm(t, ic=ic, ts=ts, j=j, wt=wt, psq=psq, xsb=xsb):
                            last = None
                            for ec in range(EC):
                                last = t.matmul(
                                    psq[j][:],
                                    wt[:, ec, ic * 128:(ic + 1) * 128],
                                    xsb[:, ec, ts * 512:(ts + 1) * 512],
                                    start=(ec == 0), stop=(ec == EC - 1))
                            return last
                        P.op("tensor", mm, reads=[b_wt, b_xsb], writes=[b_psq[j]])
                        P.op("vector", lambda v, j=j, ic=ic, stg=stg, psq=psq,
                             bc=bc: v.tensor_scalar_add(
                                 stg[j][:], psq[j][:], bc[:, ic:ic + 1]),
                             reads=[b_psq[j], b_bc], writes=[b_stg[j]])
                        P.dma("gpsimd",
                              out=dst[0][bb, ic, :, ts * 512:(ts + 1) * 512],
                              in_=stg[j][:], reads=[b_stg[j]], writes=[dst[1]],
                              slot=b_stg[j])
            for tt in range(S // 128):
                for ds in range(E // 512):
                    j = (tt * (E // 512) + ds) % 2

                    def mmv(t, tt=tt, ds=ds, j=j, psq=psq, xsb=xsb, vwT=vwT):
                        last = None
                        for ec in range(EC):
                            last = t.matmul(
                                psq[j][:],
                                xsb[:, ec, tt * 128:(tt + 1) * 128],
                                vwT[:, ec, ds * 512:(ds + 1) * 512],
                                start=(ec == 0), stop=(ec == EC - 1))
                        return last
                    P.op("tensor", mmv, reads=[b_vwT, b_xsb], writes=[b_psq[j]])
                    P.op("vector", lambda v, j=j, ds=ds, stg=stg, psq=psq:
                         v.tensor_add(stg[j][:], psq[j][:],
                                      bv_bc[:, ds * 512:(ds + 1) * 512]),
                         reads=[b_psq[j], b_bv_bc], writes=[b_stg[j]])
                    P.dma("gpsimd",
                          out=vtm[0][bb, tt * 128:(tt + 1) * 128,
                                     ds * 512:(ds + 1) * 512],
                          in_=stg[j][:], reads=[b_stg[j]], writes=[vtm[1]],
                          slot=b_stg[j])
    P.barrier()

    with ExitStack() as es:
        qsb, b_qsb = C.sb("l3a_at_q", [64, S], BF16, es=es)
        ksb, b_ksb = C.sb("l3a_at_k", [64, S], BF16, es=es)
        vsb, b_vsb = C.sb("l3a_at_v", [128, S // 128, HD], BF16, es=es)
        ssb, b_ssb = C.sb("l3a_at_s", [128, 512], F32, es=es)
        pbf, b_pbf = C.sb("l3a_at_p", [128, 512], BF16, es=es)
        ptb, b_ptb = C.sb("l3a_at_pt", [128, 4, 128], BF16, es=es)
        osb, b_osb = C.sb("l3a_at_o", [64, 128], BF16, es=es)
        stt, b_stt = C.sb("l3a_at_st", [128, 8], F32, es=es)
        msk, b_msk = C.sb("l3a_msk", [128, 512], F32, es=es)
        psS, b_psS = C.ps("l3a_at_psS", [128, 512], F32, es=es)
        psT, b_psT = C.ps("l3a_at_psT", [128, 512], BF16, es=es)
        psO, b_psO = C.ps("l3a_at_psO", [64, 128], F32, es=es)

        def mkmask(g):
            g.iota(msk[:], pattern=[[1, 512]], base=-384, channel_multiplier=-1,
                   allow_small_or_imprecise_dtypes=True)
            return g.affine_select(out=msk[:], in_=msk[:], compare_op=ALU.is_ge,
                                   fill=-FMAX, base=384,
                                   pattern=[[-1, 512]], channel_multiplier=1)
        P.op("gpsimd", mkmask, writes=[b_msk], flush=True)

        NQ = S // 128
        for bb in range(B):
            for h in range(H):
                ic, off = h // 2, (h % 2) * 64
                P.dma("sync", out=qsb[:], in_=qfm[0][bb, ic, off:off + 64, :],
                      reads=[qfm[1]], writes=[b_qsb], slot=b_qsb)
                P.dma("sync", out=ksb[:], in_=kfm[0][bb, ic, off:off + 64, :],
                      reads=[kfm[1]], writes=[b_ksb], slot=b_ksb)
                P.dma("sync", out=vsb[:],
                      in_=vtm[0][bb, :, h * 64:(h + 1) * 64].rearrange(
                          "(c p) d -> p c d", p=128),
                      reads=[vtm[1]], writes=[b_vsb], slot=b_vsb)
                for qt in range(NQ):
                    nk = min(qt + 1, 4)
                    k0 = qt + 1 - nk
                    w = nk * 128
                    moff = 512 - w
                    P.op("tensor", lambda t, qt=qt, k0=k0, w=w:
                         t.matmul(psS[:, 0:w],
                                  qsb[0:64, qt * 128:(qt + 1) * 128],
                                  ksb[0:64, k0 * 128:k0 * 128 + w],
                                  start=True, stop=True),
                         reads=[b_qsb, b_ksb], writes=[b_psS])
                    P.op("vector", lambda v, w=w, moff=moff:
                         v.scalar_tensor_tensor(ssb[:, 0:w], psS[:, 0:w],
                                                0.125, msk[:, moff:moff + w],
                                                ALU.mult, ALU.add),
                         reads=[b_psS, b_msk], writes=[b_ssb])
                    P.op("vector", lambda v, w=w: v.reduce_max(
                        stt[:, 0:1], ssb[:, 0:w], axis=AX.X),
                        reads=[b_ssb], writes=[b_stt], flush=True)
                    P.op("vector", lambda v: v.tensor_scalar_mul(
                        stt[:, 1:2], stt[:, 0:1], -1.0),
                        reads=[b_stt], writes=[b_stt], flush=True)

                    def expacc(s, w=w):
                        s.activation(pbf[:, 0:w], ssb[:, 0:w], AF.Exp,
                                     bias=stt[:, 1:2], accum_out=stt[:, 2:3])
                        s.drain()
                        return s.activation(stt[:, 3:4], stt[:, 2:3], AF.Ln)
                    P.op("scalar", expacc, reads=[b_ssb, b_stt],
                         writes=[b_pbf, b_stt], flush=True)
                    P.op("vector", lambda v: v.tensor_sub(
                        stt[:, 1:2], stt[:, 1:2], stt[:, 3:4]),
                        reads=[b_stt], writes=[b_stt], flush=True)
                    P.op("scalar", lambda s, w=w: s.activation(
                        pbf[:, 0:w], ssb[:, 0:w], AF.Exp, bias=stt[:, 1:2]),
                        reads=[b_ssb, b_stt], writes=[b_pbf])

                    def trp(t, nk=nk):
                        last = None
                        for c in range(nk):
                            last = t.transpose(
                                psT[:, c * 128:(c + 1) * 128],
                                pbf[:, c * 128:(c + 1) * 128], G.identb[:])
                        return last
                    P.op("tensor", trp, reads=[b_pbf, G.identb_b],
                         writes=[b_psT])
                    P.op("vector", lambda v, nk=nk: v.tensor_copy(
                        ptb[:, 0:nk, :], psT[:, 0:nk * 128]),
                        reads=[b_psT], writes=[b_ptb])

                    def pv(t, nk=nk, k0=k0):
                        last = None
                        for c in range(nk):
                            last = t.matmul(psO[:], vsb[:, k0 + c, :],
                                            ptb[:, c, :], start=(c == 0),
                                            stop=(c == nk - 1))
                        return last
                    P.op("tensor", pv, reads=[b_vsb, b_ptb], writes=[b_psO])
                    P.op("vector", lambda v: v.tensor_copy(osb[0:64, :],
                                                           psO[0:64, :]),
                         reads=[b_psO], writes=[b_osb])
                    P.dma("gpsimd",
                          out=ofm[0][bb, ic, off:off + 64,
                                     qt * 128:(qt + 1) * 128],
                          in_=osb[:], reads=[b_osb], writes=[ofm[1]],
                          slot=b_osb)
    P.barrier()

    with ExitStack() as es:
        owT, b_owT = prep_wT(C, G, "l3a_owT", C.W["L3_attn_Wo"], E, E, es)
        ob_bc, b_ob_bc = bcast_row(C, G, "l3a_ob", C.W["L3_attn_bo"], E, es)
        poxb, b_poxb = C.sb("l3a_po_x", [128, EC, S], BF16, es=es)
        yst, b_yst = [list(z) for z in zip(C.sb("l3a_po_y0", [128, 512], F32, es=es),
                                           C.sb("l3a_po_y1", [128, 512], F32, es=es))]
        pops, b_pops = [list(z) for z in zip(C.ps("l3a_po_p0", [128, 512], F32, es=es),
                                             C.ps("l3a_po_p1", [128, 512], F32, es=es))]
        for bb in range(B):
            P.dma("sync", out=poxb[:], in_=ofm[0][bb].rearrange("c p t -> p c t"),
                  reads=[ofm[1]], writes=[b_poxb], slot=b_poxb)
            for tt in range(S // 128):
                for ds in range(E // 512):
                    j = (tt * (E // 512) + ds) % 2

                    def mm5(t, tt=tt, ds=ds, j=j, pops=pops, poxb=poxb, owT=owT):
                        last = None
                        for icc in range(EC):
                            last = t.matmul(
                                pops[j][:], poxb[:, icc, tt * 128:(tt + 1) * 128],
                                owT[:, icc, ds * 512:(ds + 1) * 512],
                                start=(icc == 0), stop=(icc == EC - 1))
                        return last
                    P.op("tensor", mm5, reads=[b_owT, b_poxb], writes=[b_pops[j]])
                    P.op("vector", lambda v, j=j, ds=ds, yst=yst, pops=pops,
                         ob_bc=ob_bc: v.tensor_add(
                             yst[j][:], pops[j][:],
                             ob_bc[:, ds * 512:(ds + 1) * 512]),
                         reads=[b_pops[j], b_ob_bc], writes=[b_yst[j]])
                    P.dma("gpsimd",
                          out=y[0][bb * S + tt * 128: bb * S + (tt + 1) * 128,
                                   ds * 512:(ds + 1) * 512],
                          in_=yst[j][:], reads=[b_yst[j]], writes=[y[1]],
                          slot=b_yst[j])
    P.barrier()
    stage_post_ln_add(C, G, y[0], y[1], resid, b_resid)
    P.barrier()



def emit_L2_ffn(C, G, resid, b_resid):
    P = C.P
    NE, HH = 8, 1408
    xnfm = C.dram("l2f_xnfm", (B, EC, 128, S), BF16)
    xnfm32 = C.dram("l2f_xnfm32", (B, EC, 128, S), F32)
    y = C.dram("l2f_y", (T, E), F32)

    stage_prep_xn(C, G, resid, b_resid, want_fm=xnfm, want_fm32=xnfm32)
    P.barrier()

    NST = 8
    TPS = 4
    with ExitStack() as es:
        wg, b_wg = C.sb("l2f_wg", [128, EC, NE], F32, es=es)
        P.dma("sync", out=wg[:], in_=C.W["L2_ffn_Wg"].rearrange(
            "(c p) e -> p c e", p=128), writes=[b_wg], slot=b_wg)
        negbig, b_negbig = C.sb("l2f_nb", [128, NE], F32, es=es)
        P.op("gpsimd", lambda g: g.memset(negbig[:], -1e30), writes=[b_negbig])

        xt, b_xt = [list(z) for z in zip(
            C.sb("l2f_x0", [128, EC, 512], BF16, es=es),
            C.sb("l2f_x1", [128, EC, 512], BF16, es=es))]
        xt32, b_xt32 = C.sb("l2f_x32", [128, EC, 512], F32, es=es)
        wmat, b_wmat = C.sb("l2f_wmat", [128, TPS, NE], F32, es=es)
        rt, b_rt = C.sb("l2f_rt", [128, 8], F32, es=es)
        oh1, b_oh1 = C.sb("l2f_oh1", [128, 8], F32, es=es)
        oh2, b_oh2 = C.sb("l2f_oh2", [128, 8], F32, es=es)
        m8, b_m8 = C.sb("l2f_m8", [128, 8], F32, es=es)
        sstat, b_sstat = C.sb("l2f_ss", [128, 8], F32, es=es)
        yacc, b_yacc = C.sb("l2f_yacc", [128, TPS, E], F32, es=es)
        hbf, b_hbf = C.sb("l2f_hbf", [128, HH], BF16, es=es)
        hfm, b_hfm = C.sb("l2f_hfm", [128, 11, 128], BF16, es=es)
        silu_t, b_silu = C.sb("l2f_silu", [128, 512], F32, es=es)
        ew = []
        for sl in range(2):
            w1, bw1 = C.sb(f"l2f_w1_{sl}", [128, EC, HH], BF16, es=es)
            w2, bw2 = C.sb(f"l2f_w2_{sl}", [128, EC, HH], BF16, es=es)
            w3, bw3 = C.sb(f"l2f_w3_{sl}", [128, 11, E], BF16, es=es)
            ew.append((w1, bw1, w2, bw2, w3, bw3))
        psr, b_psr = C.ps("l2f_psr", [128, NE], F32, es=es)
        psg, b_psg = C.ps("l2f_psg", [128, 512], F32, es=es)
        psu, b_psu = C.ps("l2f_psu", [128, 512], F32, es=es)
        psT, b_psT = C.ps("l2f_psT", [128, 1024], BF16, es=es)
        psy, b_psy = [list(z) for z in zip(C.ps("l2f_py0", [128, 512], F32, es=es),
                                           C.ps("l2f_py1", [128, 512], F32, es=es))]

        def wsrc(e):
            if e < NE:
                return (C.W["L2_ffn_W1"][e], C.W["L2_ffn_W2"][e],
                        C.W["L2_ffn_W3"][e])
            h0 = (e - NE) * HH
            return (C.W["L2_ffn_Ws1"][:, h0:h0 + HH],
                    C.W["L2_ffn_Ws2"][:, h0:h0 + HH],
                    C.W["L2_ffn_Ws3"][h0:h0 + HH, :])

        def load_expert(e, sl):
            w1, bw1, w2, bw2, w3, bw3 = ew[sl]
            a1, a2, a3 = wsrc(e)
            P.dma("gpsimd", out=w1[:], in_=a1.rearrange("(c p) h -> p c h", p=128),
                  writes=[bw1], slot=bw1)
            P.dma("gpsimd", out=w2[:], in_=a2.rearrange("(c p) h -> p c h", p=128),
                  writes=[bw2], slot=bw2)
            P.dma("gpsimd", out=w3[:], in_=a3.rearrange("(c p) o -> p c o", p=128),
                  writes=[bw3], slot=bw3)

        for st in range(NST):
            stoff = st * 512
            jx = st % 2
            bb, t0 = (stoff // S), (stoff % S)
            P.dma("sync", out=xt[jx][:],
                  in_=xnfm[0][bb, :, :, t0:t0 + 512].rearrange("c p t -> p c t"),
                  reads=[xnfm[1]], writes=[b_xt[jx]], slot=b_xt[jx])
            P.dma("gpsimd", out=xt32[:],
                  in_=xnfm32[0][bb, :, :, t0:t0 + 512].rearrange("c p t -> p c t"),
                  reads=[xnfm32[1]], writes=[b_xt32], slot=b_xt32)
            P.op("gpsimd", lambda g: g.memset(yacc[:], 0.0),
                 reads=[], writes=[b_yacc])
            for tt in range(TPS):
                def mmr(t, tt=tt):
                    last = None
                    for ec in range(EC):
                        last = t.matmul(psr[:], xt32[:, ec, tt * 128:(tt + 1) * 128],
                                        wg[:, ec, :], start=(ec == 0),
                                        stop=(ec == EC - 1))
                    return last
                P.op("tensor", mmr, reads=[b_xt32, b_wg], writes=[b_psr])
                P.op("vector", lambda v: v.tensor_copy(rt[:], psr[:]),
                     reads=[b_psr], writes=[b_rt], flush=True)
                P.op("vector", lambda v: v.max(out=m8[:], in_=rt[:]),
                     reads=[b_rt], writes=[b_m8], flush=True)
                P.op("vector", lambda v: v.tensor_sub(sstat[:, 0:1], m8[:, 0:1],
                                                      m8[:, 1:2]),
                     reads=[b_m8], writes=[b_sstat], flush=True)

                def sig(s):
                    s.activation(sstat[:, 1:2], sstat[:, 0:1], AF.Sigmoid)
                    return s.activation(sstat[:, 2:3], sstat[:, 0:1], AF.Sigmoid,
                                        scale=-1.0)
                P.op("scalar", sig, reads=[b_sstat], writes=[b_sstat], flush=True)
                P.op("vector", lambda v, oh1=oh1: v.tensor_scalar(
                    oh1[:], rt[:], m8[:, 0:1], None, ALU.is_equal),
                    reads=[b_rt, b_m8], writes=[b_oh1], flush=True)
                P.op("vector", lambda v, oh1=oh1, oh2=oh2: v.scalar_tensor_tensor(
                    oh2[:], oh1[:], -1e30, rt[:], ALU.mult, ALU.add),
                    reads=[b_oh1, b_rt], writes=[b_oh2], flush=True)
                P.op("vector", lambda v, oh2=oh2: v.tensor_scalar(
                    oh2[:], oh2[:], m8[:, 1:2], None, ALU.is_equal),
                    reads=[b_oh2, b_m8], writes=[b_oh2], flush=True)
                P.op("vector", lambda v, tt=tt, oh1=oh1: v.tensor_scalar(
                    wmat[:, tt, :], oh1[:], sstat[:, 1:2], None, ALU.mult),
                    reads=[b_oh1, b_sstat], writes=[b_wmat], flush=True)
                P.op("vector", lambda v, tt=tt, oh2=oh2: v.scalar_tensor_tensor(
                    wmat[:, tt, :], oh2[:], sstat[:, 2:3], wmat[:, tt, :],
                    ALU.mult, ALU.add),
                    reads=[b_oh2, b_sstat, b_wmat], writes=[b_wmat], flush=True)

            NSL = [(0, 512), (512, 512), (1024, 384)]
            for e in range(10):
                sl = e % 2
                if st == 0 and e == 0:
                    load_expert(0, 0)
                if e + 1 < 10:
                    load_expert(e + 1, (e + 1) % 2)
                elif st + 1 < NST:
                    pass
                w1, bw1, w2, bw2, w3, bw3 = ew[sl]
                for tt in range(TPS):
                    for (h0, hn) in NSL:
                        def mmg(t, tt=tt, h0=h0, hn=hn, jx=jx, w=w1):
                            last = None
                            for ec in range(EC):
                                last = t.matmul(
                                    psg[:, 0:hn],
                                    xt[jx][:, ec, tt * 128:(tt + 1) * 128],
                                    w[:, ec, h0:h0 + hn],
                                    start=(ec == 0), stop=(ec == EC - 1))
                            return last
                        P.op("tensor", mmg, reads=[b_xt[jx], bw1], writes=[b_psg])

                        def mmu(t, tt=tt, h0=h0, hn=hn, jx=jx, w=w2):
                            last = None
                            for ec in range(EC):
                                last = t.matmul(
                                    psu[:, 0:hn],
                                    xt[jx][:, ec, tt * 128:(tt + 1) * 128],
                                    w[:, ec, h0:h0 + hn],
                                    start=(ec == 0), stop=(ec == EC - 1))
                            return last
                        P.op("tensor", mmu, reads=[b_xt[jx], bw2], writes=[b_psu])
                        P.op("scalar", lambda s, hn=hn: s.activation(
                            silu_t[:, 0:hn], psg[:, 0:hn], AF.Silu),
                            reads=[b_psg], writes=[b_silu])
                        if e < NE:
                            P.op("vector", lambda v, h0=h0, hn=hn, tt=tt, e=e:
                                 v.scalar_tensor_tensor(
                                     hbf[:, h0:h0 + hn], psu[:, 0:hn],
                                     wmat[:, tt, e:e + 1], silu_t[:, 0:hn],
                                     ALU.mult, ALU.mult),
                                 reads=[b_psu, b_wmat, b_silu], writes=[b_hbf])
                        else:
                            P.op("vector", lambda v, h0=h0, hn=hn:
                                 v.tensor_mul(hbf[:, h0:h0 + hn], psu[:, 0:hn],
                                              silu_t[:, 0:hn]),
                                 reads=[b_psu, b_silu], writes=[b_hbf])
                    def trh1(t):
                        last = None
                        for c in range(8):
                            last = t.transpose(psT[:, c * 128:(c + 1) * 128],
                                               hbf[:, c * 128:(c + 1) * 128],
                                               G.identb[:])
                        return last
                    P.op("tensor", trh1, reads=[b_hbf, G.identb_b], writes=[b_psT])
                    P.op("vector", lambda v: v.tensor_copy(hfm[:, 0:8, :], psT[:]),
                         reads=[b_psT], writes=[b_hfm])

                    def trh2(t):
                        last = None
                        for c in range(8, 11):
                            last = t.transpose(psT[:, (c - 8) * 128:(c - 7) * 128],
                                               hbf[:, c * 128:(c + 1) * 128],
                                               G.identb[:])
                        return last
                    P.op("tensor", trh2, reads=[b_hbf, G.identb_b], writes=[b_psT])
                    P.op("vector", lambda v: v.tensor_copy(hfm[:, 8:11, :],
                                                           psT[:, 0:384]),
                         reads=[b_psT], writes=[b_hfm])
                    for ds in range(2):
                        def mmd(t, ds=ds, w=w3):
                            last = None
                            for hc in range(11):
                                last = t.matmul(
                                    psy[ds][:], hfm[:, hc, :],
                                    w[:, hc, ds * 512:(ds + 1) * 512],
                                    start=(hc == 0), stop=(hc == 10))
                            return last
                        P.op("tensor", mmd, reads=[b_hfm, bw3], writes=[b_psy[ds]])
                        P.op("vector", lambda v, tt=tt, ds=ds: v.tensor_add(
                            yacc[:, tt, ds * 512:(ds + 1) * 512],
                            yacc[:, tt, ds * 512:(ds + 1) * 512], psy[ds][:]),
                            reads=[b_psy[ds], b_yacc], writes=[b_yacc])
            if st + 1 < NST:
                load_expert(0, 0)
            for tt in range(TPS):
                P.dma("gpsimd",
                      out=y[0][stoff + tt * 128: stoff + (tt + 1) * 128, :],
                      in_=yacc[:, tt, :], reads=[b_yacc], writes=[y[1]],
                      slot=b_yacc)
    P.barrier()
    stage_post_ln_add(C, G, y[0], y[1], resid, b_resid)
    P.barrier()



def emit_L1_ffn(C, G, resid, b_resid):
    P = C.P
    NE, HR = 16, 256
    xnfm = C.dram("l1f_xnfm", (B, EC, 128, S), BF16)
    xnfm32 = C.dram("l1f_xnfm32", (B, EC, 128, S), F32)
    y = C.dram("l1f_y", (T, E), F32)

    stage_prep_xn(C, G, resid, b_resid, want_fm=xnfm, want_fm32=xnfm32)
    P.barrier()

    NST = 8
    TPS = 4
    with ExitStack() as es:
        with ExitStack() as es_tmp:
            rwT, b_rwT = prep_wT(C, G, "l1f_rwT", C.W["L1_ffn_router_w"], E,
                                 NE, es, dtype=F32, es_tmp=es_tmp)
            sgwT, b_sgwT = prep_wT(C, G, "l1f_sgwT",
                                   C.W["L1_ffn_shared_gate_w"], E, 2048, es,
                                   es_tmp=es_tmp)
            suwT, b_suwT = prep_wT(C, G, "l1f_suwT",
                                   C.W["L1_ffn_shared_up_w"], E, 2048, es,
                                   es_tmp=es_tmp)
            sdwT, b_sdwT = prep_wT(C, G, "l1f_sdwT",
                                   C.W["L1_ffn_shared_down_w"], 2048, E, es,
                                   es_tmp=es_tmp)
            cb_bc, b_cb_bc = bcast_row(C, G, "l1f_cb",
                                       C.W["L1_ffn_correction_bias"], NE, es,
                                       es_tmp=es_tmp)
        P.barrier()

        xt, b_xt = [list(z) for z in zip(
            C.sb("l1f_x0", [128, EC, 512], BF16, es=es),
            C.sb("l1f_x1", [128, EC, 512], BF16, es=es))]
        xt32, b_xt32 = C.sb("l1f_x32", [128, EC, 512], F32, es=es)
        cmb, b_cmb = C.sb("l1f_cmb", [128, TPS, NE], F32, es=es)
        rt, b_rt = C.sb("l1f_rt", [128, NE], F32, es=es)
        sb_, b_sb = C.sb("l1f_sb", [128, NE], F32, es=es)
        t1_, b_t1 = C.sb("l1f_t1", [128, NE], F32, es=es)
        t2_, b_t2 = C.sb("l1f_t2", [128, NE], F32, es=es)
        m8, b_m8 = C.sb("l1f_m8", [128, 8], F32, es=es)
        gstat, b_gstat = C.sb("l1f_gs", [128, 16], F32, es=es)
        yacc, b_yacc = C.sb("l1f_yacc", [128, TPS, E], F32, es=es)
        hbf, b_hbf = C.sb("l1f_hbf", [128, 1024], BF16, es=es)
        hfm, b_hfm = C.sb("l1f_hfm", [128, 8, 128], BF16, es=es)
        silu_t, b_silu = C.sb("l1f_silu", [128, 512], F32, es=es)
        ew = []
        for sl in range(2):
            w1, bw1 = C.sb(f"l1f_w1_{sl}", [128, EC, HR], BF16, es=es)
            w2, bw2 = C.sb(f"l1f_w2_{sl}", [128, EC, HR], BF16, es=es)
            w3, bw3 = C.sb(f"l1f_w3_{sl}", [128, 2, E], BF16, es=es)
            ew.append((w1, bw1, w2, bw2, w3, bw3))
        psr, b_psr = C.ps("l1f_psr", [128, NE], F32, es=es)
        psg, b_psg = C.ps("l1f_psg", [128, 512], F32, es=es)
        psu, b_psu = C.ps("l1f_psu", [128, 512], F32, es=es)
        psT, b_psT = C.ps("l1f_psT", [128, 1024], BF16, es=es)
        psy, b_psy = [list(z) for z in zip(C.ps("l1f_py0", [128, 512], F32, es=es),
                                           C.ps("l1f_py1", [128, 512], F32, es=es))]

        def load_expert(e, sl):
            if e >= NE:
                return
            w1, bw1, w2, bw2, w3, bw3 = ew[sl]
            P.dma("gpsimd", out=w1[:], in_=C.W["L1_ffn_gate_w"][e].rearrange(
                "(c p) h -> p c h", p=128), writes=[bw1], slot=bw1)
            P.dma("gpsimd", out=w2[:], in_=C.W["L1_ffn_up_w"][e].rearrange(
                "(c p) h -> p c h", p=128), writes=[bw2], slot=bw2)
            P.dma("gpsimd", out=w3[:], in_=C.W["L1_ffn_down_w"][e].rearrange(
                "(c p) o -> p c o", p=128), writes=[bw3], slot=bw3)

        for st in range(NST):
            stoff = st * 512
            jx = st % 2
            bb, t0 = (stoff // S), (stoff % S)
            P.dma("sync", out=xt[jx][:],
                  in_=xnfm[0][bb, :, :, t0:t0 + 512].rearrange("c p t -> p c t"),
                  reads=[xnfm[1]], writes=[b_xt[jx]], slot=b_xt[jx])
            P.dma("gpsimd", out=xt32[:],
                  in_=xnfm32[0][bb, :, :, t0:t0 + 512].rearrange("c p t -> p c t"),
                  reads=[xnfm32[1]], writes=[b_xt32], slot=b_xt32)
            P.op("gpsimd", lambda g: g.memset(yacc[:], 0.0), writes=[b_yacc])
            for tt in range(TPS):
                def mmr(t, tt=tt):
                    last = None
                    for ec in range(EC):
                        last = t.matmul(psr[:], xt32[:, ec, tt * 128:(tt + 1) * 128],
                                        rwT[:, ec, :], start=(ec == 0),
                                        stop=(ec == EC - 1))
                    return last
                P.op("tensor", mmr, reads=[b_xt32, b_rwT], writes=[b_psr])
                P.op("scalar", lambda s: s.activation(rt[:], psr[:], AF.Sigmoid),
                     reads=[b_psr], writes=[b_rt], flush=True)
                P.op("vector", lambda v: v.tensor_add(sb_[:], rt[:], cb_bc[:, 0:NE]),
                     reads=[b_rt, b_cb_bc], writes=[b_sb], flush=True)
                sb3 = sb_.rearrange("p (g j) -> p g j", g=4)
                P.op("vector", lambda v, sb3=sb3: v.tensor_reduce(
                    gstat[:, 0:4], sb3, axis=AX.X, op=ALU.max),
                    reads=[b_sb], writes=[b_gstat], flush=True)
                P.op("vector", lambda v, sb3=sb3: v.tensor_tensor(
                    t1_.rearrange("p (g j) -> p g j", g=4), sb3,
                    gstat[:, 0:4].unsqueeze(2).broadcast_to([128, 4, 4]),
                    ALU.is_equal),
                    reads=[b_sb, b_gstat], writes=[b_t1], flush=True)
                P.op("vector", lambda v: v.scalar_tensor_tensor(
                    t2_[:], t1_[:], -1e30, sb_[:], ALU.mult, ALU.add),
                    reads=[b_t1, b_sb], writes=[b_t2], flush=True)
                P.op("vector", lambda v: v.tensor_reduce(
                    gstat[:, 4:8], t2_.rearrange("p (g j) -> p g j", g=4),
                    axis=AX.X, op=ALU.max),
                    reads=[b_t2], writes=[b_gstat], flush=True)
                P.op("vector", lambda v: v.tensor_add(
                    gstat[:, 8:12], gstat[:, 0:4], gstat[:, 4:8]),
                    reads=[b_gstat], writes=[b_gstat], flush=True)
                P.op("vector", lambda v: v.reduce_max(
                    gstat[:, 12:13], gstat[:, 8:12], axis=AX.X),
                    reads=[b_gstat], writes=[b_gstat], flush=True)
                P.op("vector", lambda v: v.tensor_scalar(
                    gstat[:, 0:4], gstat[:, 8:12], gstat[:, 12:13], None,
                    ALU.is_equal),
                    reads=[b_gstat], writes=[b_gstat], flush=True)
                P.op("vector", lambda v: v.scalar_tensor_tensor(
                    gstat[:, 4:8], gstat[:, 0:4], -1e30, gstat[:, 8:12],
                    ALU.mult, ALU.add),
                    reads=[b_gstat], writes=[b_gstat], flush=True)
                P.op("vector", lambda v: v.reduce_max(
                    gstat[:, 13:14], gstat[:, 4:8], axis=AX.X),
                    reads=[b_gstat], writes=[b_gstat], flush=True)
                P.op("vector", lambda v: v.tensor_scalar(
                    gstat[:, 4:8], gstat[:, 4:8], gstat[:, 13:14], None,
                    ALU.is_equal),
                    reads=[b_gstat], writes=[b_gstat], flush=True)
                P.op("vector", lambda v: v.tensor_add(
                    gstat[:, 0:4], gstat[:, 0:4], gstat[:, 4:8]),
                    reads=[b_gstat], writes=[b_gstat], flush=True)
                P.op("vector", lambda v: v.tensor_tensor(
                    t1_.rearrange("p (g j) -> p g j", g=4),
                    sb_.rearrange("p (g j) -> p g j", g=4),
                    gstat[:, 0:4].unsqueeze(2).broadcast_to([128, 4, 4]),
                    ALU.mult),
                    reads=[b_sb, b_gstat], writes=[b_t1], flush=True)
                P.op("vector", lambda v: v.tensor_scalar(
                    gstat[:, 4:8], gstat[:, 0:4], 1.0, 1e30,
                    ALU.subtract, ALU.mult),
                    reads=[b_gstat], writes=[b_gstat], flush=True)
                P.op("vector", lambda v: v.tensor_tensor(
                    t2_.rearrange("p (g j) -> p g j", g=4),
                    t1_.rearrange("p (g j) -> p g j", g=4),
                    gstat[:, 4:8].unsqueeze(2).broadcast_to([128, 4, 4]),
                    ALU.add),
                    reads=[b_t1, b_gstat], writes=[b_t2], flush=True)
                P.op("vector", lambda v: v.max(out=m8[:], in_=t2_[:]),
                     reads=[b_t2], writes=[b_m8], flush=True)
                P.op("vector", lambda v: v.memset(m8[:, 4:8], -1e30),
                     reads=[], writes=[b_m8], flush=True)
                P.op("vector", lambda v: v.match_replace(
                    out=t1_[:], in_to_replace=m8[:], in_values=t2_[:],
                    imm_value=-1e30),
                    reads=[b_m8, b_t2], writes=[b_t1], flush=True)
                P.op("vector", lambda v: v.tensor_tensor(
                    t2_[:], t2_[:], t1_[:], ALU.not_equal),
                    reads=[b_t1, b_t2], writes=[b_t2], flush=True)
                P.op("vector", lambda v: v.tensor_mul(t1_[:], rt[:], t2_[:]),
                     reads=[b_rt, b_t2], writes=[b_t1], flush=True)
                P.op("vector", lambda v: v.reduce_sum(
                    gstat[:, 14:15], t1_[:], axis=AX.X),
                    reads=[b_t1], writes=[b_gstat], flush=True)
                P.op("vector", lambda v: v.tensor_scalar_add(
                    gstat[:, 15:16], gstat[:, 14:15], 1e-20),
                    reads=[b_gstat], writes=[b_gstat], flush=True)
                P.op("vector", lambda v: v.reciprocal(
                    gstat[:, 14:15], gstat[:, 15:16]),
                    reads=[b_gstat], writes=[b_gstat], flush=True)
                P.op("vector", lambda v, tt=tt: v.tensor_scalar(
                    cmb[:, tt, :], t1_[:], gstat[:, 14:15], None, ALU.mult),
                    reads=[b_t1, b_gstat], writes=[b_cmb], flush=True)

            cdump = C.dram("l1f_cmbdump", (NST, 128, TPS, NE), F32)
            P.dma("gpsimd", out=cdump[0][st], in_=cmb[:], reads=[b_cmb],
                  writes=[cdump[1]], slot=b_cmb)
            for e in L1F_RANGE:
                sl = e % 2
                if st == 0 and e == L1F_RANGE[0]:
                    load_expert(L1F_RANGE[0], L1F_RANGE[0] % 2)
                if e + 1 < 18:
                    load_expert(e + 1, (e + 1) % 2)
                w1, bw1, w2, bw2, w3, bw3 = ew[sl]
                routed = e < NE
                if routed:
                    slices = [(0, HR)]
                    nhc = 2
                else:
                    slices = [(0, 512), (512, 512)]
                    nhc = 8
                    p0 = (e - NE) * 1024
                for tt in range(TPS):
                    for (h0, hn) in slices:
                        def mmg(t, tt=tt, h0=h0, hn=hn, jx=jx, routed=routed,
                                w=w1, e=e, p0=(0 if routed else p0)):
                            last = None
                            for ec in range(EC):
                                rhs = (w[:, ec, h0:h0 + hn] if routed else
                                       sgwT[:, ec, p0 + h0:p0 + h0 + hn])
                                last = t.matmul(
                                    psg[:, 0:hn],
                                    xt[jx][:, ec, tt * 128:(tt + 1) * 128],
                                    rhs, start=(ec == 0), stop=(ec == EC - 1))
                            return last
                        P.op("tensor", mmg, reads=[b_xt[jx], bw1, b_sgwT],
                             writes=[b_psg])

                        def mmu(t, tt=tt, h0=h0, hn=hn, jx=jx, routed=routed,
                                w=w2, e=e, p0=(0 if routed else p0)):
                            last = None
                            for ec in range(EC):
                                rhs = (w[:, ec, h0:h0 + hn] if routed else
                                       suwT[:, ec, p0 + h0:p0 + h0 + hn])
                                last = t.matmul(
                                    psu[:, 0:hn],
                                    xt[jx][:, ec, tt * 128:(tt + 1) * 128],
                                    rhs, start=(ec == 0), stop=(ec == EC - 1))
                            return last
                        P.op("tensor", mmu, reads=[b_xt[jx], bw2, b_suwT],
                             writes=[b_psu])
                        P.op("scalar", lambda s, hn=hn: s.activation(
                            silu_t[:, 0:hn], psg[:, 0:hn], AF.Silu),
                            reads=[b_psg], writes=[b_silu])
                        if routed:
                            P.op("vector", lambda v, h0=h0, hn=hn, tt=tt, e=e:
                                 v.scalar_tensor_tensor(
                                     hbf[:, h0:h0 + hn], psu[:, 0:hn],
                                     cmb[:, tt, e:e + 1], silu_t[:, 0:hn],
                                     ALU.mult, ALU.mult),
                                 reads=[b_psu, b_cmb, b_silu], writes=[b_hbf])
                        else:
                            P.op("vector", lambda v, h0=h0, hn=hn:
                                 v.tensor_mul(hbf[:, h0:h0 + hn], psu[:, 0:hn],
                                              silu_t[:, 0:hn]),
                                 reads=[b_psu, b_silu], writes=[b_hbf])

                    def trh(t, nhc=nhc):
                        last = None
                        for c in range(nhc):
                            last = t.transpose(psT[:, c * 128:(c + 1) * 128],
                                               hbf[:, c * 128:(c + 1) * 128],
                                               G.identb[:])
                        return last
                    P.op("tensor", trh, reads=[b_hbf, G.identb_b], writes=[b_psT])
                    P.op("vector", lambda v, nhc=nhc: v.tensor_copy(
                        hfm[:, 0:nhc, :], psT[:, 0:nhc * 128]),
                        reads=[b_psT], writes=[b_hfm])
                    for ds in range(2):
                        def mmd(t, ds=ds, nhc=nhc, routed=routed, w=w3, e=e):
                            last = None
                            for hc in range(nhc):
                                rhs = (w[:, hc, ds * 512:(ds + 1) * 512]
                                       if routed else
                                       sdwT[:, (e - NE) * 8 + hc,
                                            ds * 512:(ds + 1) * 512])
                                last = t.matmul(psy[ds][:], hfm[:, hc, :],
                                                rhs, start=(hc == 0),
                                                stop=(hc == nhc - 1))
                            return last
                        P.op("tensor", mmd, reads=[b_hfm, bw3, b_sdwT],
                             writes=[b_psy[ds]])
                        P.op("vector", lambda v, tt=tt, ds=ds: v.tensor_add(
                            yacc[:, tt, ds * 512:(ds + 1) * 512],
                            yacc[:, tt, ds * 512:(ds + 1) * 512], psy[ds][:]),
                            reads=[b_psy[ds], b_yacc], writes=[b_yacc])
            if st + 1 < NST:
                load_expert(0, 0)
            for tt in range(TPS):
                P.dma("gpsimd",
                      out=y[0][stoff + tt * 128: stoff + (tt + 1) * 128, :],
                      in_=yacc[:, tt, :], reads=[b_yacc], writes=[y[1]],
                      slot=b_yacc)
    P.barrier()
    stage_post_ln_add(C, G, y[0], y[1], resid, b_resid)
    P.barrier()



def emit_L3_ffn(C, G, resid, b_resid):
    P = C.P
    NE, R = 8, 64
    xnfm = C.dram("l3f_xnfm", (B, EC, 128, S), BF16)
    xnfm32 = C.dram("l3f_xnfm32", (B, EC, 128, S), F32)
    y = C.dram("l3f_y", (T, E), F32)

    stage_prep_xn(C, G, resid, b_resid, want_fm=xnfm, want_fm32=xnfm32)
    P.barrier()

    with ExitStack() as es:
        with ExitStack() as es_tmp:
            bwT, b_bwT = prep_wT(C, G, "l3f_bwT", C.W["L3_ffn_base_W"], E, E,
                                 es, es_tmp=es_tmp)
            bb_bc, b_bb_bc = bcast_row(C, G, "l3f_bb", C.W["L3_ffn_base_b"],
                                       E, es, es_tmp=es_tmp)
        P.barrier()

        sg_, b_sg = C.sb("l3f_sig", [8, E], F32, es=es)
        mu_, b_mu = C.sb("l3f_mu", [8, E], F32, es=es)
        iv_, b_iv = C.sb("l3f_iv", [8, E], F32, es=es)
        muiv, b_muiv = C.sb("l3f_muiv", [8, E], F32, es=es)
        m2iv, b_m2iv = C.sb("l3f_m2iv", [8, E], F32, es=es)
        ivT, b_ivT = C.sb("l3f_ivT", [128, EC, 8], F32, es=es)
        muivT, b_muivT = C.sb("l3f_muivT", [128, EC, 8], F32, es=es)
        pstT, b_pstT = C.ps("l3f_pstT", [128, 64], F32, es=es)
        P.dma("sync", out=sg_[:], in_=C.W["L3_ffn_disc_sigma"], writes=[b_sg],
              slot=b_sg)
        P.dma("sync", out=mu_[:], in_=C.W["L3_ffn_disc_mu"], writes=[b_mu],
              slot=b_mu)

        def mkiv(v):
            v.tensor_mul(iv_[:], sg_[:], sg_[:])
            v.reciprocal(iv_[:], iv_[:])
            v.tensor_mul(muiv[:], mu_[:], iv_[:])
            return v.tensor_mul(m2iv[:], mu_[:], muiv[:])
        P.op("vector", mkiv, reads=[b_sg, b_mu], writes=[b_iv, b_muiv, b_m2iv],
             flush=True)
        t3c, b_t3c = C.sb("l3f_t3", [8, 2], F32, es=es)
        P.op("vector", lambda v: v.reduce_sum(t3c[:, 0:1], m2iv[:], axis=AX.X),
             reads=[b_m2iv], writes=[b_t3c], flush=True)

        def trv(t):
            last = None
            for c in range(EC):
                last = t.transpose(pstT[:, c * 8:(c + 1) * 8],
                                   iv_[0:8, c * 128:(c + 1) * 128],
                                   G.identf[0:8, 0:8])
            return last
        P.op("tensor", trv, reads=[b_iv, G.identf_b], writes=[b_pstT])
        P.op("vector", lambda v: v.tensor_copy(
            ivT.rearrange("p c e -> p (c e)"), pstT[:, 0:64]),
            reads=[b_pstT], writes=[b_ivT])

        def trv2(t):
            last = None
            for c in range(EC):
                last = t.transpose(pstT[:, c * 8:(c + 1) * 8],
                                   muiv[0:8, c * 128:(c + 1) * 128],
                                   G.identf[0:8, 0:8])
            return last
        P.op("tensor", trv2, reads=[b_muiv, G.identf_b], writes=[b_pstT])
        P.op("vector", lambda v: v.tensor_copy(
            muivT.rearrange("p c e -> p (c e)"), pstT[:, 0:64]),
            reads=[b_pstT], writes=[b_muivT])

        xc32, b_xc32 = C.sb("l3f_x32", [128, S], F32, es=es)
        sq, b_sq = C.sb("l3f_sq", [128, S], F32, es=es)
        m1c, b_m1c = C.sb("l3f_m1", [128, EC], F32, es=es)
        m2c, b_m2c = C.sb("l3f_m2", [128, EC], F32, es=es)
        lrow, b_lrow = C.sb("l3f_lrow", [1, 8], F32, es=es)
        selr, b_selr = C.sb("l3f_selr", [1, 2], F32, es=es)
        iotar, b_iotar = C.sb("l3f_iotar", [1, 8], F32, es=es)
        idxf, b_idxf = C.sb("l3f_idxf", [1, 1], F32, es=es)
        iotac, b_iotac = C.sb("l3f_iotac", [128, 1], F32, es=es)
        idxb, b_idxb = C.sb("l3f_idxb", [128, 2], F32, es=es)
        idxi, b_idxi = C.sb("l3f_idxi", [128, 10], I32, es=es)
        psL, b_psL = C.ps("l3f_psL", [8, 8], F32, es=es)
        psB, b_psB = C.ps("l3f_psB", [128, 8], F32, es=es)

        def mkio(g):
            g.iota(iotar[:], pattern=[[1, 8]], base=0, channel_multiplier=0,
                   allow_small_or_imprecise_dtypes=True)
            return g.iota(iotac[:], pattern=[[1, 1]], base=0,
                          channel_multiplier=1,
                          allow_small_or_imprecise_dtypes=True)
        P.op("gpsimd", mkio, writes=[b_iotar, b_iotac], flush=True)

        dwg, b_dwg = C.sb("l3f_dwg", [128, EC, R], F32, es=es)
        dwb, b_dwb = C.sb("l3f_dwb", [128, EC, R], BF16, es=es)
        uwg, b_uwg = C.sb("l3f_uwg", [64, E], F32, es=es)
        uwb, b_uwb = C.sb("l3f_uwb", [64, E], BF16, es=es)
        cwg, b_cwg = C.sb("l3f_cwg", [128, EC, E], F32, es=es)
        cwT, b_cwT = C.sb("l3f_cwT", [128, EC, E], BF16, es=es)
        cstg, b_cstg = C.sb("l3f_cstg", [128, E], BF16, es=es)
        rows3, b_rows3 = C.sb("l3f_rows", [2, 3 * E + R], F32, es=es)
        dbb, b_dbb = C.sb("l3f_dbb", [128, R], F32, es=es)
        ubb, b_ubb = C.sb("l3f_ubb", [128, E], F32, es=es)
        ybb, b_ybb = C.sb("l3f_ybb", [128, E], F32, es=es)
        psW, b_psW = C.ps("l3f_psW", [128, 1024], BF16, es=es)
        psh, b_psh = C.ps("l3f_psh", [128, R], F32, es=es)
        psa, b_psa = [list(z) for z in zip(C.ps("l3f_pa0", [128, 512], F32, es=es),
                                           C.ps("l3f_pa1", [128, 512], F32, es=es))]
        xt, b_xtile = C.sb("l3f_xt", [128, EC, 128], BF16, es=es)
        hh, b_hh = C.sb("l3f_hh", [128, R], BF16, es=es)
        hrelu, b_hrelu = C.sb("l3f_hrelu", [128, R], F32, es=es)
        hfmT, b_hfmT = C.sb("l3f_hfmT", [64, 128], BF16, es=es)
        aa, b_aa = C.sb("l3f_aa", [128, E], BF16, es=es)
        afm, b_afm = C.sb("l3f_afm", [128, EC, 128], BF16, es=es)
        yst, b_yst = C.sb("l3f_yst", [128, E], F32, es=es)
        psHT, b_psHT = C.ps("l3f_psHT", [64, 128], BF16, es=es)

        for bb in range(B):
            for c in range(EC):
                P.dma("sync", out=xc32[:], in_=xnfm32[0][bb, c],
                      reads=[xnfm32[1]], writes=[b_xc32], slot=b_xc32)
                P.op("vector", lambda v, c=c: v.reduce_sum(
                    m1c[:, c:c + 1], xc32[:], axis=AX.X),
                    reads=[b_xc32], writes=[b_m1c], flush=True)

                def sqa(s, c=c):
                    s.activation(sq[:], xc32[:], AF.Square,
                                 accum_out=m2c[:, c:c + 1])
                    s.drain()
                    return s.mul(m2c[:, c:c + 1], m2c[:, c:c + 1], 1.0 / S)
                P.op("scalar", sqa, reads=[b_xc32], writes=[b_sq, b_m2c],
                     flush=True)
            P.op("vector", lambda v: v.tensor_scalar(
                m1c[:], m1c[:], 1.0 / S, -2.0, ALU.mult, ALU.mult),
                reads=[b_m1c], writes=[b_m1c], flush=True)

            def mml(t):
                last = None
                for c in range(EC):
                    t.matmul(psL[:, 0:1], ivT[:, c, :], m2c[:, c:c + 1],
                             start=(c == 0), stop=False)
                for c in range(EC):
                    last = t.matmul(psL[:, 0:1], muivT[:, c, :],
                                    m1c[:, c:c + 1], start=False,
                                    stop=(c == EC - 1))
                return last
            P.op("tensor", mml, reads=[b_ivT, b_muivT, b_m1c, b_m2c],
                 writes=[b_psL])
            P.op("vector", lambda v: v.tensor_add(t3c[:, 1:2], psL[:, 0:1],
                                                  t3c[:, 0:1]),
                 reads=[b_psL, b_t3c], writes=[b_t3c], flush=True)
            P.op("tensor", lambda t: t.transpose(psL[0:1, 0:8], t3c[0:8, 1:2],
                                                 G.identf[0:8, 0:8]),
                 reads=[b_t3c, G.identf_b], writes=[b_psL])
            P.op("vector", lambda v: v.tensor_copy(lrow[:], psL[0:1, 0:8]),
                 reads=[b_psL], writes=[b_lrow], flush=True)
            P.op("vector", lambda v: v.tensor_reduce(
                selr[0:1, 0:1], lrow[:], axis=AX.X, op=ALU.min),
                reads=[b_lrow], writes=[b_selr], flush=True)
            P.op("vector", lambda v: v.tensor_scalar(
                lrow[:], lrow[:], selr[0:1, 0:1], None, ALU.is_equal),
                reads=[b_lrow, b_selr], writes=[b_lrow], flush=True)

            def mkidx(v):
                v.tensor_mul(lrow[:], lrow[:], iotar[:])
                v.drain()
                return v.reduce_sum(idxf[:], lrow[:], axis=AX.X)
            P.op("vector", mkidx, reads=[b_lrow, b_iotar],
                 writes=[b_lrow, b_idxf], flush=True)
            P.op("tensor", lambda t: t.matmul(psB[:, 0:1], G.onesrow[0:1, :],
                                              idxf[0:1, 0:1], start=True,
                                              stop=True),
                 reads=[b_idxf, G.onesrow_b], writes=[b_psB])

            def mkib(v):
                v.scalar_tensor_tensor(idxb[:, 0:1], psB[:, 0:1], 1024.0,
                                       iotac[:], ALU.mult, ALU.add)
                return v.scalar_tensor_tensor(idxb[:, 1:2], psB[:, 0:1], 64.0,
                                              iotac[:], ALU.mult, ALU.add)
            P.op("vector", mkib, reads=[b_psB, b_iotac], writes=[b_idxb],
                 flush=True)

            def mkidxi(v):
                for c in range(EC):
                    v.tensor_scalar_add(idxi[:, c:c + 1], idxb[:, 0:1],
                                        float(c * 128))
                v.tensor_copy(idxi[:, 8:9], idxb[:, 1:2])
                return v.tensor_copy(idxi[:, 9:10], psB[:, 0:1])
            P.op("vector", mkidxi, reads=[b_idxb, b_psB], writes=[b_idxi],
                 flush=True)

            dwf = C.W["L3_ffn_down_W"].rearrange("e d r -> (e d) r")
            cwf = C.W["L3_ffn_conv_W"].rearrange("e o d -> (e o) d")
            uwf = C.W["L3_ffn_up_W"].rearrange("e r d -> (e r) d")
            for c in range(EC):
                P.dma("gpsimd", raw=(lambda g, c=c, dwf=dwf: g.indirect_dma_start(
                    dwg[:, c, :], None, dwf,
                    IndirectOffsetOnAxis(idxi[:, c:c + 1], axis=0))),
                    reads=[b_idxi], writes=[b_dwg], slot=b_dwg)
                P.dma("gpsimd", raw=(lambda g, c=c, cwf=cwf: g.indirect_dma_start(
                    cwg[:, c, :], None, cwf,
                    IndirectOffsetOnAxis(idxi[:, c:c + 1], axis=0))),
                    reads=[b_idxi], writes=[b_cwg], slot=b_cwg)
            P.dma("gpsimd", raw=(lambda g, uwf=uwf: g.indirect_dma_start(
                uwg[:, :], None, uwf,
                IndirectOffsetOnAxis(idxi[0:64, 8:9], axis=0))),
                reads=[b_idxi], writes=[b_uwg], slot=b_uwg)
            P.dma("gpsimd", raw=(lambda g: g.indirect_dma_start(
                rows3[0:2, 0:R], None, C.W["L3_ffn_down_b"],
                IndirectOffsetOnAxis(idxi[0:2, 9:10], axis=0))),
                reads=[b_idxi], writes=[b_rows3], slot=b_rows3)
            P.dma("gpsimd", raw=(lambda g: g.indirect_dma_start(
                rows3[0:2, R:R + E], None, C.W["L3_ffn_up_b"],
                IndirectOffsetOnAxis(idxi[0:2, 9:10], axis=0))),
                reads=[b_idxi], writes=[b_rows3], slot=b_rows3)
            P.dma("gpsimd", raw=(lambda g: g.indirect_dma_start(
                rows3[0:2, R + E:R + 2 * E], None, C.W["L3_ffn_conv_b"],
                IndirectOffsetOnAxis(idxi[0:2, 9:10], axis=0))),
                reads=[b_idxi], writes=[b_rows3], slot=b_rows3)

            P.op("tensor", lambda t: t.matmul(psa[0][:, 0:R],
                                              G.onesrow[0:1, :],
                                              rows3[0:1, 0:R], start=True,
                                              stop=True),
                 reads=[b_rows3, G.onesrow_b], writes=[b_psa[0]])
            P.op("vector", lambda v: v.tensor_copy(dbb[:], psa[0][:, 0:R]),
                 reads=[b_psa[0]], writes=[b_dbb], flush=True)
            for half in range(2):
                P.op("tensor", lambda t, half=half: t.matmul(
                    psa[half][:], G.onesrow[0:1, :],
                    rows3[0:1, R + half * 512:R + (half + 1) * 512],
                    start=True, stop=True),
                    reads=[b_rows3, G.onesrow_b], writes=[b_psa[half]])
                P.op("vector", lambda v, half=half: v.tensor_copy(
                    ubb[:, half * 512:(half + 1) * 512], psa[half][:]),
                    reads=[b_psa[half]], writes=[b_ubb], flush=True)
                P.op("tensor", lambda t, half=half: t.matmul(
                    psa[half][:], G.onesrow[0:1, :],
                    rows3[0:1, R + E + half * 512:R + E + (half + 1) * 512],
                    start=True, stop=True),
                    reads=[b_rows3, G.onesrow_b], writes=[b_psa[half]])
                P.op("vector", lambda v, half=half: v.tensor_add(
                    ybb[:, half * 512:(half + 1) * 512], psa[half][:],
                    bb_bc[:, half * 512:(half + 1) * 512]),
                    reads=[b_psa[half], b_bb_bc], writes=[b_ybb], flush=True)

            P.op("vector", lambda v: v.tensor_copy(
                dwb.rearrange("p c r -> p (c r)"),
                dwg.rearrange("p c r -> p (c r)")),
                reads=[b_dwg], writes=[b_dwb])
            P.op("vector", lambda v: v.tensor_copy(uwb[:], uwg[:]),
                 reads=[b_uwg], writes=[b_uwb])
            for c in range(EC):
                P.op("vector", lambda v, c=c: v.tensor_copy(cstg[:], cwg[:, c, :]),
                     reads=[b_cwg], writes=[b_cstg])

                def trc(t, c=c):
                    last = None
                    for oc in range(EC):
                        last = t.transpose(psW[:, oc * 128:(oc + 1) * 128],
                                           cstg[:, oc * 128:(oc + 1) * 128],
                                           G.identb[:])
                    return last
                P.op("tensor", trc, reads=[b_cstg, G.identb_b], writes=[b_psW])
                P.op("vector", lambda v, c=c: v.tensor_copy(
                    mkap(cwT[:, 0, 0:1], c * 128, [[E, EC], [1, 128]]),
                    psW.rearrange("p (c t) -> p c t", c=EC)),
                    reads=[b_psW], writes=[b_cwT])

            for tt in range(S // 128):
                P.dma("sync", out=xt[:],
                      in_=xnfm[0][bb, :, :, tt * 128:(tt + 1) * 128].rearrange(
                          "c p t -> p c t"),
                      reads=[xnfm[1]], writes=[b_xtile], slot=b_xtile)

                def mmh(t):
                    last = None
                    for ec in range(EC):
                        last = t.matmul(psh[:], xt[:, ec, :], dwb[:, ec, :],
                                        start=(ec == 0), stop=(ec == EC - 1))
                    return last
                P.op("tensor", mmh, reads=[b_xtile, b_dwb], writes=[b_psh])

                P.op("vector", lambda v: v.tensor_add(hrelu[:], psh[:], dbb[:]),
                     reads=[b_psh, b_dbb], writes=[b_hrelu], flush=True)
                P.op("vector", lambda v: v.tensor_relu(hh[:], hrelu[:]),
                     reads=[b_hrelu], writes=[b_hh])
                P.op("tensor", lambda t: t.transpose(psHT[0:64, 0:128], hh[:],
                                                     G.identb[:]),
                     reads=[b_hh, G.identb_b], writes=[b_psHT])
                P.op("vector", lambda v: v.tensor_copy(hfmT[0:64, :],
                                                       psHT[0:64, 0:128]),
                     reads=[b_psHT], writes=[b_hfmT])
                for ds in range(2):
                    P.op("tensor", lambda t, ds=ds: t.matmul(
                        psa[ds][:], hfmT[0:64, :],
                        uwb[0:64, ds * 512:(ds + 1) * 512], start=True,
                        stop=True),
                        reads=[b_hfmT, b_uwb], writes=[b_psa[ds]])
                    P.op("vector", lambda v, ds=ds: v.tensor_add(
                        aa[:, ds * 512:(ds + 1) * 512], psa[ds][:],
                        ubb[:, ds * 512:(ds + 1) * 512]),
                        reads=[b_psa[ds], b_ubb], writes=[b_aa])

                def tra(t):
                    last = None
                    for c in range(EC):
                        last = t.transpose(psW[:, c * 128:(c + 1) * 128],
                                           aa[:, c * 128:(c + 1) * 128],
                                           G.identb[:])
                    return last
                P.op("tensor", tra, reads=[b_aa, G.identb_b], writes=[b_psW])
                P.op("vector", lambda v: v.tensor_copy(
                    afm.rearrange("p c t -> p (c t)"), psW[:]),
                    reads=[b_psW], writes=[b_afm])
                for ds in range(2):
                    def mmy(t, ds=ds):
                        last = None
                        for ec in range(EC):
                            t.matmul(psa[ds][:], xt[:, ec, :],
                                     bwT[:, ec, ds * 512:(ds + 1) * 512],
                                     start=(ec == 0), stop=False)
                        for ec in range(EC):
                            last = t.matmul(psa[ds][:], afm[:, ec, :],
                                            cwT[:, ec, ds * 512:(ds + 1) * 512],
                                            start=False, stop=(ec == EC - 1))
                        return last
                    P.op("tensor", mmy, reads=[b_xtile, b_afm, b_bwT, b_cwT],
                         writes=[b_psa[ds]])
                    P.op("vector", lambda v, ds=ds: v.tensor_add(
                        yst[:, ds * 512:(ds + 1) * 512], psa[ds][:],
                        ybb[:, ds * 512:(ds + 1) * 512]),
                        reads=[b_psa[ds], b_ybb], writes=[b_yst])
                P.dma("gpsimd",
                      out=y[0][bb * S + tt * 128: bb * S + (tt + 1) * 128, :],
                      in_=yst[:], reads=[b_yst], writes=[y[1]], slot=b_yst)
    P.barrier()
    stage_post_ln_add(C, G, y[0], y[1], resid, b_resid)
    P.barrier()



def emit_L2_attn(C, G, resid, b_resid):
    P = C.P
    H, DH = 8, 128
    NW = 32
    xnfm = C.dram("l2a_xnfm", (B, EC, 128, S), BF16)
    qfm = C.dram("l2a_q", (B, H, 128, S), BF16)
    kfm = C.dram("l2a_k", (B, H, 128, S), BF16)
    vtm = C.dram("l2a_v", (B, S, E), BF16)
    plA = C.dram("l2a_plA", (B, S, E), F32)
    plB = C.dram("l2a_plB", (B, S, E), F32)
    y = C.dram("l2a_y", (T, E), F32)

    stage_prep_xn(C, G, resid, b_resid, want_fm=xnfm)
    P.barrier()

    with ExitStack() as es:
        wq, b_wq = load_w_natural(C, "l2a_wqkv", C.W["L2_attn_w_qkv"], E,
                                  3072, es)
        xsb, b_xsb = C.sb("l2a_xsb", [128, EC, S], BF16, es=es)
        stg, b_stg = [list(z) for z in zip(C.sb("l2a_st0", [128, 512], BF16, es=es),
                                           C.sb("l2a_st1", [128, 512], BF16, es=es))]
        psq, b_psq = [list(z) for z in zip(C.ps("l2a_ps0", [128, 512], F32, es=es),
                                           C.ps("l2a_ps1", [128, 512], F32, es=es))]
        for bb in range(B):
            P.dma("sync", out=xsb[:],
                  in_=xnfm[0][bb].rearrange("c p t -> p c t"),
                  reads=[xnfm[1]], writes=[b_xsb], slot=b_xsb)
            for which, base, dst in (("q", 0, qfm), ("k", E, kfm)):
                for h in range(H):
                    for ts in range(S // 512):
                        j = (h * (S // 512) + ts) % 2

                        def mm(t, h=h, ts=ts, j=j, base=base, psq=psq, xsb=xsb,
                               wq=wq):
                            last = None
                            for ec in range(EC):
                                last = t.matmul(
                                    psq[j][:, 0:128] if False else psq[j][:, 0:512],
                                    wq[:, ec, base + h * 128: base + (h + 1) * 128],
                                    xsb[:, ec, ts * 512:(ts + 1) * 512],
                                    start=(ec == 0), stop=(ec == EC - 1))
                            return last
                        P.op("tensor", mm, reads=[b_wq, b_xsb], writes=[b_psq[j]])
                        P.op("vector", lambda v, j=j, stg=stg, psq=psq:
                             v.tensor_copy(stg[j][:], psq[j][:]),
                             reads=[b_psq[j]], writes=[b_stg[j]])
                        P.dma("gpsimd",
                              out=dst[0][bb, h, :, ts * 512:(ts + 1) * 512],
                              in_=stg[j][:], reads=[b_stg[j]], writes=[dst[1]],
                              slot=b_stg[j])
            for tt in range(S // 128):
                for ds in range(E // 512):
                    j = (tt * (E // 512) + ds) % 2

                    def mmv(t, tt=tt, ds=ds, j=j, psq=psq, xsb=xsb, wq=wq):
                        last = None
                        for ec in range(EC):
                            last = t.matmul(
                                psq[j][:],
                                xsb[:, ec, tt * 128:(tt + 1) * 128],
                                wq[:, ec, 2 * E + ds * 512:2 * E + (ds + 1) * 512],
                                start=(ec == 0), stop=(ec == EC - 1))
                        return last
                    P.op("tensor", mmv, reads=[b_wq, b_xsb], writes=[b_psq[j]])
                    P.op("vector", lambda v, j=j, stg=stg, psq=psq:
                         v.tensor_copy(stg[j][:], psq[j][:]),
                         reads=[b_psq[j]], writes=[b_stg[j]])
                    P.dma("gpsimd",
                          out=vtm[0][bb, tt * 128:(tt + 1) * 128,
                                     ds * 512:(ds + 1) * 512],
                          in_=stg[j][:], reads=[b_stg[j]], writes=[vtm[1]],
                          slot=b_stg[j])
    P.barrier()

    NTL = 32
    with ExitStack() as es:
        mskA, b_mskA = C.sb("l2a_mA", [128, 128], F32, es=es)
        mskB, b_mskB = C.sb("l2a_mB", [128, 128], F32, es=es)
        P.dma("sync", out=mskA[:], in_=C.W["c_l2a_maskA"], writes=[b_mskA],
              slot=b_mskA)
        P.dma("sync", out=mskB[:], in_=C.W["c_l2a_maskB"], writes=[b_mskB],
              slot=b_mskB)
        qsb, b_qsb = C.sb("l2a_at_q", [128, S], BF16, es=es)
        ksb, b_ksb = C.sb("l2a_at_k", [128, S], BF16, es=es)
        vband, b_vband = [list(z) for z in zip(
            C.sb("l2a_at_v0", [68, 128], BF16, es=es),
            C.sb("l2a_at_v1", [68, 128], BF16, es=es))]
        qdup, b_qdup = C.sb("l2a_at_qd", [128, 128], BF16, es=es)
        ssb, b_ssb = C.sb("l2a_at_s", [128, 128], F32, es=es)
        pbf, b_pbf = C.sb("l2a_at_p", [128, 128], BF16, es=es)
        peff, b_peff = C.sb("l2a_at_pe", [128, 68], BF16, es=es)
        pef2, b_pef2 = C.sb("l2a_at_pe2", [128, 68], BF16, es=es)
        peS, b_peS = C.sb("l2a_at_peS", [128, 68], BF16, es=es)
        petb, b_petb = C.sb("l2a_at_pet", [68, 128], BF16, es=es)
        osb, b_osb = C.sb("l2a_at_o", [128, 128], F32, es=es)
        stt, b_stt = C.sb("l2a_at_st", [128, 4], F32, es=es)
        psS, b_psS = C.ps("l2a_at_psS", [128, 128], F32, es=es)
        psT, b_psT = C.ps("l2a_at_psT", [128, 128], BF16, es=es)
        psO, b_psO = C.ps("l2a_at_psO", [128, 128], F32, es=es)
        zl, b_zl = C.sb("l2a_zl", [128, E], F32, es=es)
        P.op("gpsimd", lambda g: g.memset(zl[:], 0.0), writes=[b_zl])
        for bb in range(B):
            P.dma("gpsimd", out=plA[0][bb, 2046:2048, :], in_=zl[0:2, :],
                  reads=[b_zl], writes=[plA[1]], slot=b_zl)
            P.dma("gpsimd", out=plB[0][bb, 0:2, :], in_=zl[0:2, :],
                  reads=[b_zl], writes=[plB[1]], slot=b_zl)
            for h in range(H):
                P.dma("sync", out=qsb[:], in_=qfm[0][bb, h], reads=[qfm[1]],
                      writes=[b_qsb], slot=b_qsb)
                P.dma("sync", out=ksb[:], in_=kfm[0][bb, h], reads=[kfm[1]],
                      writes=[b_ksb], slot=b_ksb)
                for tl in range(NTL):
                    nw = 31 if tl == NTL - 1 else 32
                    nq = 4 * nw
                    t0 = 64 * tl
                    keyw = 2 * nw + 2
                    keyw = min(keyw, S - t0)
                    jv = tl % 2
                    lq = mkap(qsb[:, 0:1], t0, [[2, 2], [2, nw], [1, 2]])
                    rk = mkap(ksb[:, 0:1], t0, [[2, nw], [1, 4]])
                    P.op("vector", lambda v, lq=lq, nq=nq: v.tensor_copy(
                        qdup[:, 0:nq].rearrange("p (a b c) -> p a b c", a=2,
                                                b=nq // 4), lq),
                        reads=[b_qsb], writes=[b_qdup])
                    P.op("tensor", lambda t, rk=rk, nq=nq:
                         t.matmul(psS[0:nq, 0:nq], qdup[:, 0:nq], rk,
                                  start=True, stop=True),
                         reads=[b_qdup, b_ksb], writes=[b_psS])
                    msk = mskA if nw == 32 else mskB
                    b_msk = b_mskA if nw == 32 else b_mskB
                    P.op("vector", lambda v, nq=nq, msk=msk:
                         v.scalar_tensor_tensor(ssb[0:nq, 0:nq], psS[0:nq, 0:nq],
                                                0.03125, msk[0:nq, 0:nq],
                                                ALU.mult, ALU.add),
                         reads=[b_psS, b_msk], writes=[b_ssb])
                    P.op("vector", lambda v, nq=nq: v.reduce_max(
                        stt[0:nq, 0:1], ssb[0:nq, 0:nq], axis=AX.X),
                        reads=[b_ssb], writes=[b_stt], flush=True)
                    P.op("vector", lambda v, nq=nq: v.tensor_scalar_mul(
                        stt[0:nq, 1:2], stt[0:nq, 0:1], -1.0),
                        reads=[b_stt], writes=[b_stt], flush=True)

                    def expacc(s, nq=nq):
                        s.activation(pbf[0:nq, 0:nq], ssb[0:nq, 0:nq], AF.Exp,
                                     bias=stt[0:nq, 1:2],
                                     accum_out=stt[0:nq, 2:3])
                        s.drain()
                        return s.mul(stt[0:nq, 3:4], stt[0:nq, 2:3], 1.0)
                    P.op("scalar", expacc, reads=[b_ssb, b_stt],
                         writes=[b_pbf, b_stt], flush=True)

                    def norm(v, nq=nq):
                        v.reciprocal(stt[0:nq, 2:3], stt[0:nq, 3:4])
                        v.drain()
                        return v.tensor_scalar(pbf[0:nq, 0:nq], pbf[0:nq, 0:nq],
                                               stt[0:nq, 2:3], None, ALU.mult)
                    P.op("vector", norm, reads=[b_stt, b_pbf], writes=[b_pbf, b_stt])
                    def scat0(v, keyw=keyw):
                        v.memset(peff[:, 0:keyw], 0.0)
                        return v.memset(pef2[:, 0:keyw], 0.0)
                    P.op("vector", scat0, writes=[b_peff, b_pef2], flush=True)

                    def scat1(v, nw=nw):
                        v.tensor_copy(mkap(peff[:, 0:1], 0, [[2, nw], [1, 2]]),
                                      mkap(pbf[:, 0:1], 0, [[4, nw], [1, 2]]))
                        return v.tensor_copy(
                            mkap(pef2[:, 0:1], 2, [[2, nw], [1, 2]]),
                            mkap(pbf[:, 0:1], 2, [[4, nw], [1, 2]]))
                    P.op("vector", scat1, reads=[b_pbf], writes=[b_peff, b_pef2],
                         flush=True)
                    P.op("vector", lambda v, keyw=keyw: v.tensor_add(
                        peS[:, 0:keyw], peff[:, 0:keyw], pef2[:, 0:keyw]),
                        reads=[b_peff, b_pef2], writes=[b_peS], flush=True)
                    P.op("tensor", lambda t, nq=nq, keyw=keyw: t.transpose(
                        psT[0:keyw, 0:nq], peS[0:nq, 0:keyw], G.identb[0:nq, 0:nq]),
                        reads=[b_peS, G.identb_b], writes=[b_psT])
                    P.op("vector", lambda v, nq=nq, keyw=keyw: v.tensor_copy(
                        petb[0:keyw, 0:nq], psT[0:keyw, 0:nq]),
                        reads=[b_psT], writes=[b_petb])
                    P.dma("sync", out=vband[jv][0:keyw, :],
                          in_=vtm[0][bb, t0:t0 + keyw, h * 128:(h + 1) * 128],
                          reads=[vtm[1]], writes=[b_vband[jv]], slot=b_vband[jv])
                    P.op("tensor", lambda t, nq=nq, keyw=keyw, jv=jv: t.matmul(
                        psO[0:nq, 0:128], petb[0:keyw, 0:nq],
                        vband[jv][0:keyw, :], start=True, stop=True),
                        reads=[b_petb, b_vband[jv]], writes=[b_psO])
                    P.op("vector", lambda v, nq=nq: v.tensor_copy(
                        osb[0:nq, :], psO[0:nq, 0:128]),
                        reads=[b_psO], writes=[b_osb])
                    P.dma("gpsimd", out=plA[0][bb, t0:t0 + 2 * nw,
                                               h * 128:(h + 1) * 128],
                          in_=osb[0:2 * nw, :], reads=[b_osb], writes=[plA[1]],
                          slot=b_osb)
                    P.dma("gpsimd", out=plB[0][bb, t0 + 2:t0 + 2 + 2 * nw,
                                               h * 128:(h + 1) * 128],
                          in_=osb[2 * nw:4 * nw, :], reads=[b_osb],
                          writes=[plB[1]], slot=b_osb)
    P.barrier()

    with ExitStack() as es:
        wo, b_wo = load_w_natural(C, "l2a_wo", C.W["L2_attn_w_out"], E, E, es)
        with ExitStack() as es_tmp:
            ob_bc, b_ob_bc = bcast_row(C, G, "l2a_ob", C.W["L2_attn_b_out"],
                                       E, es, es_tmp=es_tmp)
        P.barrier()
        cnt, b_cnt = C.sb("l2a_cnt", [128, NT // B], F32, es=es)

        def mkcnt(g):
            g.memset(cnt[:], 2.0)
            g.affine_select(out=cnt[:], in_=cnt[:], compare_op=ALU.is_ge,
                            fill=1.0, base=-2, pattern=[[128, NT // B]],
                            channel_multiplier=1)
            return g.affine_select(out=cnt[:], in_=cnt[:], compare_op=ALU.is_ge,
                                   fill=1.0, base=2045,
                                   pattern=[[-128, NT // B]],
                                   channel_multiplier=-1)
        P.op("gpsimd", mkcnt, writes=[b_cnt], flush=True)

        oa, b_oa = [list(z) for z in zip(C.sb("l2a_o0", [128, E], F32, es=es),
                                         C.sb("l2a_o1", [128, E], F32, es=es))]
        obx, b_obx = [list(z) for z in zip(C.sb("l2a_ob0", [128, E], F32, es=es),
                                           C.sb("l2a_ob1", [128, E], F32, es=es))]
        obf, b_obf = [list(z) for z in zip(C.sb("l2a_obf0", [128, E], BF16, es=es),
                                           C.sb("l2a_obf1", [128, E], BF16, es=es))]
        ofc, b_ofc = [list(z) for z in zip(
            C.sb("l2a_ofc0", [128, EC, 128], BF16, es=es),
            C.sb("l2a_ofc1", [128, EC, 128], BF16, es=es))]
        yst, b_yst = [list(z) for z in zip(C.sb("l2a_y0", [128, 512], F32, es=es),
                                           C.sb("l2a_y1", [128, 512], F32, es=es))]
        psq2, b_psq2 = [list(z) for z in zip(C.ps("l2a_pq0", [128, 512], F32, es=es),
                                             C.ps("l2a_pq1", [128, 512], F32, es=es))]
        psE, b_psE = [list(z) for z in zip(C.ps("l2a_pe0", [128, E], BF16, es=es),
                                           C.ps("l2a_pe1", [128, E], BF16, es=es))]
        for i in range(NT):
            j = i % 2
            bb, tt = i // (NT // B), i % (NT // B)
            P.dma("sync", out=oa[j][:],
                  in_=plA[0][bb, tt * 128:(tt + 1) * 128, :],
                  reads=[plA[1]], writes=[b_oa[j]], slot=b_oa[j])
            P.dma("scalar", out=obx[j][:],
                  in_=plB[0][bb, tt * 128:(tt + 1) * 128, :],
                  reads=[plB[1]], writes=[b_obx[j]], slot=b_obx[j])
            P.op("vector", lambda v, j=j, oa=oa, obx=obx, obf=obf:
                 v.tensor_add(obf[j][:], oa[j][:], obx[j][:]),
                 reads=[b_oa[j], b_obx[j]], writes=[b_obf[j]])
            transpose_to_fm(C, G, obf[j][:], b_obf[j], psE[j][:], b_psE[j],
                            ofc[j][:], b_ofc[j])
            for ds in range(E // 512):
                jj = (i * (E // 512) + ds) % 2

                def mm(t, j=j, ds=ds, jj=jj, ofc=ofc, psq2=psq2, wo=wo):
                    last = None
                    for ec in range(EC):
                        last = t.matmul(
                            psq2[jj][:], ofc[j][:, ec, :],
                            wo[:, ec, ds * 512:(ds + 1) * 512],
                            start=(ec == 0), stop=(ec == EC - 1))
                    return last
                P.op("tensor", mm, reads=[b_ofc[j], b_wo], writes=[b_psq2[jj]])
                P.op("vector", lambda v, jj=jj, ds=ds, tt=tt, yst=yst,
                     psq2=psq2: v.scalar_tensor_tensor(
                         yst[jj][:], ob_bc[:, ds * 512:(ds + 1) * 512],
                         cnt[:, tt:tt + 1], psq2[jj][:], ALU.mult, ALU.add),
                     reads=[b_psq2[jj], b_ob_bc, b_cnt], writes=[b_yst[jj]])
                P.dma("gpsimd",
                      out=y[0][i * 128:(i + 1) * 128, ds * 512:(ds + 1) * 512],
                      in_=yst[jj][:], reads=[b_yst[jj]], writes=[y[1]],
                      slot=b_yst[jj])
    P.barrier()
    stage_post_ln_add(C, G, y[0], y[1], resid, b_resid)
    P.barrier()



def emit_L1_attn(C, G, resid, b_resid):
    P = C.P
    DK = 128
    xnfm = C.dram("l1a_xnfm", (B, EC, 128, S), BF16)
    cat = C.dram("l1a_cat", (B, S, 512), F32)
    y = C.dram("l1a_y", (T, E), F32)

    stage_prep_xn(C, G, resid, b_resid, want_fm=xnfm)
    P.barrier()

    with ExitStack() as es:
        with ExitStack() as es_tmp:
            wqT, b_wqT = prep_wT(C, G, "l1a_wqT",
                                 C.W["L1_attn_Wq"].rearrange("h k e -> (h k) e"),
                                 E, 512, es, es_tmp=es_tmp)
            wkT, b_wkT = prep_wT(C, G, "l1a_wkT",
                                 C.W["L1_attn_Wk"].rearrange("h k e -> (h k) e"),
                                 E, 512, es, es_tmp=es_tmp)
            wvT, b_wvT = prep_wT(C, G, "l1a_wvT",
                                 C.W["L1_attn_Wv"].rearrange("h k e -> (h k) e"),
                                 E, 512, es, es_tmp=es_tmp)
        P.barrier()

        zl, b_zl = C.sb("l1a_zl", [128, 512], F32, es=es)
        P.op("gpsimd", lambda g: g.memset(zl[:], 0.0), writes=[b_zl])
        xsb, b_xsb = C.sb("l1a_xsb", [128, EC, S], BF16, es=es)
        qL, b_qL = C.sb("l1a_qL", [128, 16, DK], F32, es=es)
        kL, b_kL = C.sb("l1a_kL", [128, 16, DK], F32, es=es)
        vL, b_vL = C.sb("l1a_vL", [128, 16, DK], F32, es=es)
        kq, b_kq = C.sb("l1a_kq", [128, 136], F32, es=es)
        pp, b_pp = C.sb("l1a_pp", [128, 136], F32, es=es)
        tmp, b_tmp = C.sb("l1a_tmp", [128, DK], F32, es=es)
        acc, b_acc = [list(z) for z in zip(C.sb("l1a_a0", [128, DK], F32, es=es),
                                           C.sb("l1a_a1", [128, DK], F32, es=es))]
        stt, b_stt = C.sb("l1a_stt", [128, 4], F32, es=es)
        psq, b_psq = C.ps("l1a_psq", [128, 3 * DK], F32, es=es)
        NORM = 1.0 / math.sqrt(DK)

        for bb in range(B):
            for zt in range(S // 128):
                P.dma("gpsimd", out=cat[0][bb, zt * 128:(zt + 1) * 128, :],
                      in_=zl[:], reads=[b_zl], writes=[cat[1]], slot=b_zl)
            P.dma("sync", out=xsb[:],
                  in_=xnfm[0][bb].rearrange("c p t -> p c t"),
                  reads=[xnfm[1]], writes=[b_xsb], slot=b_xsb)
            for h in range(4):
                dil = 1 << h
                L = 16 // dil
                for l in range(L):
                    blk = l * dil

                    def mmqkv(t, l=l, blk=blk, h=h):
                        last = None
                        for w_i, wT in enumerate((wqT, wkT, wvT)):
                            for ec in range(EC):
                                last = t.matmul(
                                    psq[:, w_i * DK:(w_i + 1) * DK],
                                    xsb[:, ec, blk * 128:(blk + 1) * 128],
                                    wT[:, ec, h * 128:(h + 1) * 128],
                                    start=(ec == 0), stop=(ec == EC - 1))
                        return last
                    P.op("tensor", mmqkv, reads=[b_xsb, b_wqT, b_wkT, b_wvT],
                         writes=[b_psq])

                    def ev(v, l=l):
                        v.tensor_copy(qL[:, l, :], psq[:, 0:DK])
                        v.tensor_copy(kL[:, l, :], psq[:, DK:2 * DK])
                        return v.tensor_copy(vL[:, l, :], psq[:, 2 * DK:3 * DK])
                    P.op("vector", ev, reads=[b_psq],
                         writes=[b_qL, b_kL, b_vL])
                base = [n * (n + 1) // 2 for n in range(L + 1)]
                for n in range(L):
                    def dots(v, n=n, base=base):
                        last = None
                        for m in range(n + 1):
                            v.tensor_mul(tmp[:], kL[:, n, :], qL[:, m, :])
                            last = v.reduce_sum(kq[:, base[n] + m:base[n] + m + 1],
                                                tmp[:], axis=AX.X)
                        return last
                    P.op("vector", dots, reads=[b_kL, b_qL],
                         writes=[b_tmp, b_kq], flush=True)
                    nm = n + 1
                    P.op("vector", lambda v, n=n, nm=nm, base=base: v.reduce_max(
                        stt[:, 0:1], kq[:, base[n]:base[n] + nm], axis=AX.X),
                        reads=[b_kq], writes=[b_stt], flush=True)
                    P.op("vector", lambda v: v.tensor_scalar_mul(
                        stt[:, 1:2], stt[:, 0:1], -NORM),
                        reads=[b_stt], writes=[b_stt], flush=True)

                    def expn(s, n=n, nm=nm, base=base):
                        s.activation(pp[:, base[n]:base[n] + nm],
                                     kq[:, base[n]:base[n] + nm], AF.Exp,
                                     bias=stt[:, 1:2], scale=NORM,
                                     accum_out=stt[:, 2:3])
                        s.drain()
                        return s.mul(stt[:, 3:4], stt[:, 2:3], 1.0)
                    P.op("scalar", expn, reads=[b_kq, b_stt],
                         writes=[b_pp, b_stt], flush=True)
                    P.op("vector", lambda v: v.reciprocal(stt[:, 2:3],
                                                          stt[:, 3:4]),
                         reads=[b_stt], writes=[b_stt], flush=True)
                    P.op("vector", lambda v, n=n, nm=nm, base=base:
                         v.tensor_scalar(pp[:, base[n]:base[n] + nm],
                                         pp[:, base[n]:base[n] + nm],
                                         stt[:, 2:3], None, ALU.mult),
                         reads=[b_pp, b_stt], writes=[b_pp], flush=True)
                for m in range(L):
                    steps = list(range(m, L))
                    cur = 0
                    P.op("vector", lambda v, m=m, base=base: v.tensor_scalar(
                        acc[0][:], vL[:, m, :], pp[:, base[m] + m:base[m] + m + 1],
                        None, ALU.mult),
                        reads=[b_vL, b_pp], writes=[b_acc[0]])
                    for n in steps[1:]:
                        nxt = 1 - cur
                        P.op("vector", lambda v, n=n, m=m, cur=cur, nxt=nxt,
                             base=base: v.scalar_tensor_tensor(
                                 acc[nxt][:], vL[:, n, :],
                                 pp[:, base[n] + m:base[n] + m + 1],
                                 acc[cur][:], ALU.mult, ALU.add),
                             reads=[b_vL, b_pp, b_acc[cur]],
                             writes=[b_acc[nxt]])
                        cur = nxt
                    out_ap = mkap(cat[0][bb, 0:1, 0:1].squeeze(0) if False else
                                  cat[0][bb], dil * m * 512 + h * 128,
                                  [[1, DK]])
                    out_ap = AP(cat[0].tensor,
                                cat[0].offset + bb * S * 512 +
                                dil * m * 512 + h * 128,
                                [[16 * 512, 128], [1, DK]])
                    P.dma("gpsimd", out=out_ap, in_=acc[cur][:],
                          reads=[b_acc[cur]], writes=[cat[1]],
                          slot=b_acc[cur])
    P.barrier()

    with ExitStack() as es:
        with ExitStack() as es_tmp:
            woT, b_woT = prep_wT(C, G, "l1a_woT", C.W["L1_attn_W_out"], 512,
                                 E, es, es_tmp=es_tmp)
            ob_bc, b_ob_bc = bcast_row(C, G, "l1a_ob", C.W["L1_attn_b_out"],
                                       E, es, es_tmp=es_tmp)
        P.barrier()
        ct, b_ct = [list(z) for z in zip(C.sb("l1a_c0", [128, 512], F32, es=es),
                                         C.sb("l1a_c1", [128, 512], F32, es=es))]
        cb, b_cb = [list(z) for z in zip(C.sb("l1a_cb0", [128, 512], BF16, es=es),
                                         C.sb("l1a_cb1", [128, 512], BF16, es=es))]
        cf, b_cf = [list(z) for z in zip(
            C.sb("l1a_cf0", [128, 4, 128], BF16, es=es),
            C.sb("l1a_cf1", [128, 4, 128], BF16, es=es))]
        yst, b_yst = [list(z) for z in zip(C.sb("l1a_y0", [128, 512], F32, es=es),
                                           C.sb("l1a_y1", [128, 512], F32, es=es))]
        psE, b_psE = [list(z) for z in zip(C.ps("l1a_pe0", [128, 512], BF16, es=es),
                                           C.ps("l1a_pe1", [128, 512], BF16, es=es))]
        psq2, b_psq2 = [list(z) for z in zip(C.ps("l1a_pq0", [128, 512], F32, es=es),
                                             C.ps("l1a_pq1", [128, 512], F32, es=es))]
        for i in range(NT):
            j = i % 2
            bb, tt = i // (NT // B), i % (NT // B)
            P.dma("sync", out=ct[j][:],
                  in_=cat[0][bb, tt * 128:(tt + 1) * 128, :],
                  reads=[cat[1]], writes=[b_ct[j]], slot=b_ct[j])
            P.op("vector", lambda v, j=j, ct=ct, cb=cb: v.tensor_copy(
                cb[j][:], ct[j][:]), reads=[b_ct[j]], writes=[b_cb[j]])
            transpose_to_fm(C, G, cb[j][:], b_cb[j], psE[j][:], b_psE[j],
                            cf[j][:], b_cf[j], nchunks=4)
            for ds in range(E // 512):
                jj = (i * (E // 512) + ds) % 2

                def mm(t, j=j, ds=ds, jj=jj, cf=cf, psq2=psq2, woT=woT):
                    last = None
                    for vc in range(4):
                        last = t.matmul(
                            psq2[jj][:], cf[j][:, vc, :],
                            woT[:, vc, ds * 512:(ds + 1) * 512],
                            start=(vc == 0), stop=(vc == 3))
                    return last
                P.op("tensor", mm, reads=[b_cf[j], b_woT], writes=[b_psq2[jj]])
                P.op("vector", lambda v, jj=jj, ds=ds, yst=yst, psq2=psq2:
                     v.tensor_add(yst[jj][:], psq2[jj][:],
                                  ob_bc[:, ds * 512:(ds + 1) * 512]),
                     reads=[b_psq2[jj], b_ob_bc], writes=[b_yst[jj]])
                P.dma("gpsimd",
                      out=y[0][i * 128:(i + 1) * 128, ds * 512:(ds + 1) * 512],
                      in_=yst[jj][:], reads=[b_yst[jj]], writes=[y[1]],
                      slot=b_yst[jj])
    P.barrier()
    stage_post_ln_add(C, G, y[0], y[1], resid, b_resid)
    P.barrier()



def emit_L0_ffn(C, G, resid, b_resid):
    P = C.P
    H, DH = 16, 64
    DI = 2730
    NJ = 22
    z0 = C.dram("l0f_z0", (B, S, E), F32)
    z1 = C.dram("l0f_z1", (B, S, E), F32)
    y = C.dram("l0f_y", (T, E), F32)

    stage_prep_xn(C, G, resid, b_resid,
                  want_tm32=(z0[0].rearrange("b s e -> (b s) e"), z0[1]))
    P.barrier()

    if True:
        def setup_tables(es, li=0):
            csq, b_csq = C.sb(f"l0f_csq{li}", [128, S // 128, DH], F32, es=es)
            snq, b_snq = C.sb(f"l0f_snq{li}", [128, S // 128, DH], F32, es=es)
            csk, b_csk = C.sb(f"l0f_csk{li}", [128, S // 128, DH], F32, es=es)
            snk, b_snk = C.sb(f"l0f_snk{li}", [128, S // 128, DH], F32, es=es)
            for nm, tile, buf in (("c_l0f_csq", csq, b_csq),
                                  ("c_l0f_snq", snq, b_snq),
                                  ("c_l0f_csk", csk, b_csk),
                                  ("c_l0f_snk", snk, b_snk)):
                P.dma("sync", out=tile[:],
                      in_=C.W[nm].rearrange("(c p) d -> p c d", p=128),
                      writes=[buf], slot=buf)
            msks, b_msks = {}, {}
            for off in range(2):
                mk, b_mk = C.sb(f"l0f_msk{off}_{li}", [128, 512], F32, es=es)

                def bld(g, off=off, mk=mk):
                    g.memset(mk[:], 0.0)
                    g.affine_select(out=mk[:], in_=mk[:], compare_op=ALU.is_ge,
                                    fill=-FMAX, base=128 * off + 256,
                                    pattern=[[-1, 512]], channel_multiplier=1)
                    return g.affine_select(out=mk[:], in_=mk[:],
                                           compare_op=ALU.is_ge, fill=-FMAX,
                                           base=-128 * off, pattern=[[1, 512]],
                                           channel_multiplier=-1)
                P.op("gpsimd", bld, writes=[b_mk], flush=True)
                msks[(off, 0)] = mk
                b_msks[(off, 0)] = b_mk
            return (csq, b_csq, snq, b_snq, csk, b_csk, snk, b_snk,
                    msks, b_msks)

        zbufs = [z0, z1]
        for li in range(6):
            zsA, zdA = zbufs[0], zbufs[1]
            with ExitStack() as es:
                (csq, b_csq, snq, b_snq, csk, b_csk, snk, b_snk,
                 msks, b_msks) = setup_tables(es, li)
                wqkv, b_wqkv = load_w_natural(C, f"l0fA_wqkv{li}",
                                              C.W["L0_ffn_Wqkv"][li], E, 3072,
                                              es)
                wo, b_wo = load_w_natural(C, f"l0fA_wo{li}",
                                          C.W["L0_ffn_Wo"][li], E, E, es)
                with ExitStack() as es_tmp:
                    qs_bc, b_qs_bc = bcast_row(C, G, f"l0fA_qs{li}",
                                               C.W["L0_ffn_q_scale"][li], DH,
                                               es, es_tmp=es_tmp)
                    ks_bc, b_ks_bc = bcast_row(C, G, f"l0fA_ks{li}",
                                               C.W["L0_ffn_k_scale"][li], DH,
                                               es, es_tmp=es_tmp)
                P.barrier()
                qsr, b_qsr = C.sb(f"l0fA_qsr_{li}", [128, DH], F32, es=es)
                ksr, b_ksr = C.sb(f"l0fA_ksr_{li}", [128, DH], F32, es=es)

                def rotscale(v):
                    v.tensor_scalar_mul(qsr[:, 0:32], qs_bc[:, 32:64], -1.0)
                    v.tensor_copy(qsr[:, 32:64], qs_bc[:, 0:32])
                    v.tensor_scalar_mul(ksr[:, 0:32], ks_bc[:, 32:64], -1.0)
                    return v.tensor_copy(ksr[:, 32:64], ks_bc[:, 0:32])
                P.op("vector", rotscale, reads=[b_qs_bc, b_ks_bc],
                     writes=[b_qsr, b_ksr], flush=True)

                xs, b_xs = C.sb(f"l0fA_xs_{li}", [128, E], F32, es=es)
                xb, b_xb = C.sb(f"l0fA_xb_{li}", [128, E], BF16, es=es)
                xf, b_xf = C.sb(f"l0fA_xf_{li}", [128, EC, 128], BF16, es=es)
                qkv, b_qkv = C.sb(f"l0fA_qkv_{li}", [128, 3072], F32, es=es)
                sqn, b_sqn = C.sb(f"l0fA_sqn_{li}", [128, 2048], F32, es=es)
                s2, b_s2 = C.sb(f"l0fA_s2_{li}", [128, 32], F32, es=es)
                rno, b_rno = C.sb(f"l0fA_rno_{li}", [128, 32], F32, es=es)
                tabA, b_tabA = C.sb(f"l0fA_tA_{li}", [128, DH], F32, es=es)
                tabB, b_tabB = C.sb(f"l0fA_tB_{li}", [128, DH], F32, es=es)
                tabC, b_tabC = C.sb(f"l0fA_tC_{li}", [128, DH], F32, es=es)
                tabD, b_tabD = C.sb(f"l0fA_tD_{li}", [128, DH], F32, es=es)
                qn, b_qn = C.sb(f"l0fA_qn_{li}", [128, 2048], F32, es=es)
                qsw, b_qsw = C.sb(f"l0fA_qsw_{li}", [128, 2048], F32, es=es)
                qt1, b_qt1 = C.sb(f"l0fA_qt1_{li}", [128, 2048], F32, es=es)
                qkb, b_qkb = C.sb(f"l0fA_qkb_{li}", [128, 2048], BF16, es=es)
                kR, b_kR = C.sb(f"l0fA_kR_{li}", [64, H, 4, 128], BF16, es=es)
                vR, b_vR = C.sb(f"l0fA_vR_{li}", [128, 4, H, DH], BF16, es=es)
                qfmr, b_qfmr = C.sb(f"l0fA_qfm_{li}", [64, H, 2, 128], BF16, es=es)
                ssb, b_ssb = C.sb(f"l0fA_ssb_{li}", [128, 512], F32, es=es)
                pbf, b_pbf = C.sb(f"l0fA_pbf_{li}", [128, 512], BF16, es=es)
                ptb, b_ptb = C.sb(f"l0fA_ptb_{li}", [128, 4, 128], BF16, es=es)
                afm, b_afm = C.sb(f"l0fA_afm_{li}", [128, EC, 2, 128], BF16, es=es)
                stt, b_stt = C.sb(f"l0fA_stt_{li}", [128, 4], F32, es=es)
                zres, b_zres = C.sb(f"l0fA_zres_{li}", [128, E], F32, es=es)
                zout, b_zout = C.sb(f"l0fA_zout_{li}", [128, E], F32, es=es)
                psq, b_psq = C.ps(f"l0fA_psq_{li}", [128, 512], F32, es=es)
                psF, b_psF = C.ps(f"l0fA_psF_{li}", [128, E], BF16, es=es)
                psH, b_psH = C.ps(f"l0fA_psH_{li}", [128, 1024], BF16, es=es)
                psS, b_psS = C.ps(f"l0fA_psS_{li}", [128, 512], F32, es=es)
                psT, b_psT = C.ps(f"l0fA_psT_{li}", [128, 512], BF16, es=es)
                psO, b_psO = C.ps(f"l0fA_psO_{li}", [64, 128], F32, es=es)
                psR, b_psR = C.ps(f"l0fA_psR_{li}", [128, 512], F32, es=es)

                g32 = lambda t_: t_.rearrange("p (g d) -> p g d", g=32)
                gH = lambda t_, o: t_[:, o:o + 1024].rearrange(
                    "p (h d) -> p h d", h=H)

                for bb in range(B):
                    for w in range(S // 256):
                        for tt in (2 * w, 2 * w + 1):
                            t0 = tt * 128
                            if tt == 0:
                                P.op("gpsimd", lambda g: g.memset(
                                    xs[0:1, 512:1024], 0.0), writes=[b_xs])
                                P.dma("sync", out=xs[:, 0:512],
                                      in_=zsA[0][bb, 0:128, 0:512],
                                      reads=[zsA[1]], writes=[b_xs], slot=b_xs)
                                P.dma("sync", out=xs[1:128, 512:1024],
                                      in_=zsA[0][bb, 0:127, 512:1024],
                                      reads=[zsA[1]], writes=[b_xs], slot=b_xs)
                            else:
                                P.dma("sync", out=xs[:, 0:512],
                                      in_=zsA[0][bb, t0:t0 + 128, 0:512],
                                      reads=[zsA[1]], writes=[b_xs], slot=b_xs)
                                P.dma("sync", out=xs[:, 512:1024],
                                      in_=zsA[0][bb, t0 - 1:t0 + 127, 512:1024],
                                      reads=[zsA[1]], writes=[b_xs], slot=b_xs)
                            P.op("vector", lambda v: v.tensor_copy(xb[:], xs[:]),
                                 reads=[b_xs], writes=[b_xb])
                            transpose_to_fm(C, G, xb[:], b_xb, psF[:], b_psF,
                                            xf[:], b_xf)
                            for sl in range(6):
                                def mmq(t, sl=sl):
                                    last = None
                                    for ec in range(EC):
                                        last = t.matmul(
                                            psq[:], xf[:, ec, :],
                                            wqkv[:, ec, sl * 512:(sl + 1) * 512],
                                            start=(ec == 0),
                                            stop=(ec == EC - 1))
                                    return last
                                P.op("tensor", mmq, reads=[b_xf, b_wqkv],
                                     writes=[b_psq])
                                P.op("vector", lambda v, sl=sl: v.tensor_copy(
                                    qkv[:, sl * 512:(sl + 1) * 512], psq[:]),
                                    reads=[b_psq], writes=[b_qkv])
                            P.op("scalar", lambda s: s.activation(
                                sqn[:], qkv[:, 0:2048], AF.Square),
                                reads=[b_qkv], writes=[b_sqn])
                            P.op("vector", lambda v: v.reduce_sum(
                                s2[:], g32(sqn), axis=AX.X),
                                reads=[b_sqn], writes=[b_s2], flush=True)
                            P.op("scalar", lambda s: s.activation(
                                rno[:], s2[:], AF.Sqrt),
                                reads=[b_s2], writes=[b_rno], flush=True)
                            P.op("vector", lambda v: v.tensor_scalar_max(
                                rno[:], rno[:], 1e-12),
                                reads=[b_rno], writes=[b_rno], flush=True)
                            P.op("vector", lambda v: v.reciprocal(
                                rno[:], rno[:]),
                                reads=[b_rno], writes=[b_rno], flush=True)
                            P.op("vector", lambda v: v.tensor_tensor(
                                g32(qn), g32(qkv[:, 0:2048]),
                                rno[:].unsqueeze(2).broadcast_to([128, 32, DH]),
                                ALU.mult),
                                reads=[b_qkv, b_rno], writes=[b_qn], flush=True)

                            def tabs(v, tt=tt):
                                v.tensor_mul(tabA[:], qs_bc[:], csq[:, tt, :])
                                v.tensor_mul(tabB[:], qsr[:], snq[:, tt, :])
                                v.tensor_mul(tabC[:], ks_bc[:], csk[:, tt, :])
                                return v.tensor_mul(tabD[:], ksr[:],
                                                    snk[:, tt, :])
                            P.op("vector", tabs,
                                 reads=[b_qs_bc, b_qsr, b_ks_bc, b_ksr,
                                        b_csq, b_snq, b_csk, b_snk],
                                 writes=[b_tabA, b_tabB, b_tabC, b_tabD],
                                 flush=True)

                            def swap(v):
                                v.tensor_copy(g32(qsw)[:, :, 0:32],
                                              g32(qn)[:, :, 32:64])
                                return v.tensor_copy(g32(qsw)[:, :, 32:64],
                                                     g32(qn)[:, :, 0:32])
                            P.op("vector", swap, reads=[b_qn], writes=[b_qsw],
                                 flush=True)

                            def term1(v):
                                v.tensor_tensor(
                                    gH(qt1, 0), gH(qn, 0),
                                    tabA[:].unsqueeze(1).broadcast_to([128, H, DH]),
                                    ALU.mult)
                                return v.tensor_tensor(
                                    gH(qt1, 1024), gH(qn, 1024),
                                    tabC[:].unsqueeze(1).broadcast_to([128, H, DH]),
                                    ALU.mult)
                            P.op("vector", term1,
                                 reads=[b_qn, b_tabA, b_tabC], writes=[b_qt1],
                                 flush=True)

                            def term2(v):
                                v.tensor_tensor(
                                    gH(sqn, 0), gH(qsw, 0),
                                    tabB[:].unsqueeze(1).broadcast_to([128, H, DH]),
                                    ALU.mult)
                                return v.tensor_tensor(
                                    gH(sqn, 1024), gH(qsw, 1024),
                                    tabD[:].unsqueeze(1).broadcast_to([128, H, DH]),
                                    ALU.mult)
                            P.op("vector", term2,
                                 reads=[b_qsw, b_tabB, b_tabD], writes=[b_sqn],
                                 flush=True)
                            P.op("vector", lambda v: v.tensor_add(
                                qkb[:], qt1[:], sqn[:]),
                                reads=[b_qt1, b_sqn], writes=[b_qkb],
                                flush=True)
                            P.op("vector", lambda v, tt=tt: v.tensor_copy(
                                vR[:, tt % 4, :, :],
                                qkv[:, 2048:3072].rearrange(
                                    "p (h d) -> p h d", h=H)),
                                reads=[b_qkv], writes=[b_vR])
                            for part, kofs in ((0, 0), (1, 1024)):
                                for h0 in (0, 8):
                                    def trqk(t, h0=h0, kofs=kofs):
                                        last = None
                                        for hh in range(8):
                                            last = t.transpose(
                                                psH[0:64, hh * 128:(hh + 1) * 128],
                                                qkb[:, kofs + (h0 + hh) * DH:
                                                    kofs + (h0 + hh + 1) * DH],
                                                G.identb[:])
                                        return last
                                    P.op("tensor", trqk,
                                         reads=[b_qkb, G.identb_b],
                                         writes=[b_psH])
                                    if kofs == 0:
                                        P.op("vector", lambda v, h0=h0, tt=tt:
                                             v.tensor_copy(
                                                 qfmr[0:64, h0:h0 + 8, tt % 2, :],
                                                 psH[0:64, :].rearrange(
                                                     "p (h t) -> p h t", h=8)),
                                             reads=[b_psH], writes=[b_qfmr])
                                    else:
                                        P.op("vector", lambda v, h0=h0, tt=tt:
                                             v.tensor_copy(
                                                 kR[0:64, h0:h0 + 8, tt % 4, :],
                                                 psH[0:64, :].rearrange(
                                                     "p (h t) -> p h t", h=8)),
                                             reads=[b_psH], writes=[b_kR])
                        for tt in (2 * w, 2 * w + 1):
                            off = tt % 2
                            w0 = 0
                            k0t = max(2 * w - 2, 0)
                            kw = 512 if w > 0 else 256
                            moff = 0 if w > 0 else 256
                            for h in range(H):
                                def mmsc(t, h=h, tt=tt, k0t=k0t, kw=kw):
                                    last = None
                                    for ci in range(kw // 128):
                                        last = t.matmul(
                                            psS[:, ci * 128:(ci + 1) * 128],
                                            qfmr[0:64, h, tt % 2, :],
                                            kR[0:64, h, (k0t + ci) % 4, :],
                                            start=True, stop=True)
                                    return last
                                P.op("tensor", mmsc, reads=[b_qfmr, b_kR],
                                     writes=[b_psS])
                                mk = msks[(off, w0)]
                                b_mk = b_msks[(off, w0)]
                                P.op("vector", lambda v, kw=kw, moff=moff,
                                     mk=mk: v.scalar_tensor_tensor(
                                         ssb[:, 0:kw], psS[:, 0:kw], 8.0,
                                         mk[:, moff:moff + kw], ALU.mult,
                                         ALU.add),
                                     reads=[b_psS, b_mk], writes=[b_ssb])
                                P.op("vector", lambda v, kw=kw: v.reduce_max(
                                    stt[:, 0:1], ssb[:, 0:kw], axis=AX.X),
                                    reads=[b_ssb], writes=[b_stt], flush=True)
                                P.op("vector", lambda v: v.tensor_scalar_mul(
                                    stt[:, 1:2], stt[:, 0:1], -1.0),
                                    reads=[b_stt], writes=[b_stt], flush=True)

                                def expa(s, kw=kw):
                                    s.activation(pbf[:, 0:kw], ssb[:, 0:kw],
                                                 AF.Exp, bias=stt[:, 1:2],
                                                 accum_out=stt[:, 2:3])
                                    s.drain()
                                    return s.mul(stt[:, 3:4], stt[:, 2:3], 1.0)
                                P.op("scalar", expa, reads=[b_ssb, b_stt],
                                     writes=[b_pbf, b_stt], flush=True)
                                P.op("vector", lambda v: v.reciprocal(
                                    stt[:, 2:3], stt[:, 3:4]),
                                    reads=[b_stt], writes=[b_stt], flush=True)
                                P.op("vector", lambda v, kw=kw: v.tensor_scalar(
                                    pbf[:, 0:kw], pbf[:, 0:kw], stt[:, 2:3],
                                    None, ALU.mult),
                                    reads=[b_pbf, b_stt], writes=[b_pbf])

                                def trp(t, kw=kw):
                                    last = None
                                    for c in range(kw // 128):
                                        last = t.transpose(
                                            psT[:, c * 128:(c + 1) * 128],
                                            pbf[:, c * 128:(c + 1) * 128],
                                            G.identb[:])
                                    return last
                                P.op("tensor", trp, reads=[b_pbf, G.identb_b],
                                     writes=[b_psT])
                                P.op("vector", lambda v, kw=kw: v.tensor_copy(
                                    ptb[:, 0:kw // 128, :], psT[:, 0:kw]),
                                    reads=[b_psT], writes=[b_ptb])

                                def pv(t, kw=kw, k0t=k0t, h=h):
                                    last = None
                                    for c in range(kw // 128):
                                        last = t.matmul(
                                            psO[:], vR[:, (k0t + c) % 4, h, :],
                                            ptb[:, c, :], start=(c == 0),
                                            stop=(c == kw // 128 - 1))
                                    return last
                                P.op("tensor", pv, reads=[b_vR, b_ptb],
                                     writes=[b_psO])
                                P.op("vector", lambda v, h=h, tt=tt:
                                     v.tensor_copy(
                                         afm[(h % 2) * 64:(h % 2) * 64 + 64,
                                             h // 2, tt % 2, :],
                                         psO[0:64, :]),
                                     reads=[b_psO], writes=[b_afm])
                            P.dma("scalar", out=zres[:],
                                  in_=zsA[0][bb, tt * 128:(tt + 1) * 128, :],
                                  reads=[zsA[1]], writes=[b_zres], slot=b_zres)
                            for ds in range(2):
                                def mmo(t, ds=ds, tt=tt):
                                    last = None
                                    for ic in range(EC):
                                        last = t.matmul(
                                            psR[:], afm[:, ic, tt % 2, :],
                                            wo[:, ic, ds * 512:(ds + 1) * 512],
                                            start=(ic == 0),
                                            stop=(ic == EC - 1))
                                    return last
                                P.op("tensor", mmo, reads=[b_afm, b_wo],
                                     writes=[b_psR])
                                P.op("vector", lambda v, ds=ds: v.tensor_add(
                                    zout[:, ds * 512:(ds + 1) * 512], psR[:],
                                    zres[:, ds * 512:(ds + 1) * 512]),
                                    reads=[b_psR, b_zres], writes=[b_zout])
                            P.dma("gpsimd",
                                  out=zdA[0][bb, tt * 128:(tt + 1) * 128, :],
                                  in_=zout[:], reads=[b_zout], writes=[zdA[1]],
                                  slot=b_zout)
            P.barrier()

            zsB, zdB = zbufs[1], zbufs[0]
            with ExitStack() as es:
                w1, b_w1 = load_w_natural(C, f"l0fB_w1{li}",
                                          C.W["L0_ffn_W1"][li], E, 5460, es)
                w2, b_w2 = C.sb(f"l0fB_w2{li}", [128, NJ, E], BF16, es=es)
                P.dma("gpsimd", out=w2[:, 0:21, :],
                      in_=C.W["L0_ffn_W2"][li][0:2688].rearrange(
                          "(c p) o -> p c o", p=128),
                      writes=[b_w2], slot=b_w2)
                P.dma("gpsimd", out=w2[0:42, 21, :],
                      in_=C.W["L0_ffn_W2"][li][2688:2730],
                      writes=[b_w2], slot=b_w2)
                with ExitStack() as es_tmp:
                    g_bc, b_g_bc = bcast_row(C, G, f"l0fB_g{li}",
                                             C.W["L0_ffn_ln_g"][li], E, es,
                                             es_tmp=es_tmp)
                    lb_bc, b_lb_bc = bcast_row(C, G, f"l0fB_b{li}",
                                               C.W["L0_ffn_ln_b"][li], E, es,
                                               es_tmp=es_tmp)
                    b2_bc, b_b2_bc = bcast_row(C, G, f"l0fB_b2{li}",
                                               C.W["L0_ffn_b2"][li], E, es,
                                               es_tmp=es_tmp)
                P.barrier()
                b1r, b_b1r = C.sb(f"l0fB_b1r_{li}", [1, 5460], F32, es=es)
                P.dma("sync", out=b1r[:], in_=C.W["L0_ffn_b1"][li].unsqueeze(0),
                      writes=[b_b1r], slot=b_b1r)
                b1a, b_b1a = C.sb(f"l0fB_b1a_{li}", [128, NJ], F32, es=es)
                b1g, b_b1g = C.sb(f"l0fB_b1g_{li}", [128, NJ], F32, es=es)
                psC, b_psC = C.ps(f"l0fB_psC_{li}", [128, 2 * NJ], F32, es=es)

                def trb(t):
                    last = None
                    for j in range(NJ):
                        pj = 128 if j < 21 else 42
                        last = t.transpose(psC[0:pj, j:j + 1],
                                           b1r[0:1, j * 128:j * 128 + pj],
                                           G.identf[0:1, 0:1])
                        last = t.transpose(psC[0:pj, NJ + j:NJ + j + 1],
                                           b1r[0:1, DI + j * 128:DI + j * 128 + pj],
                                           G.identf[0:1, 0:1])
                    return last
                P.op("tensor", trb, reads=[b_b1r, G.identf_b], writes=[b_psC])

                def evb(v):
                    v.tensor_copy(b1a[:], psC[:, 0:NJ])
                    return v.tensor_copy(b1g[:], psC[:, NJ:2 * NJ])
                P.op("vector", evb, reads=[b_psC], writes=[b_b1a, b_b1g],
                     flush=True)

                xs2, b_xs2 = C.sb(f"l0fB_xs_{li}", [128, E], F32, es=es)
                xn2, b_xn2 = C.sb(f"l0fB_xn_{li}", [128, E], F32, es=es)
                xg2, b_xg2 = C.sb(f"l0fB_xg_{li}", [128, E], F32, es=es)
                xb2, b_xb2 = C.sb(f"l0fB_xb_{li}", [128, E], BF16, es=es)
                xf2, b_xf2 = C.sb(f"l0fB_xf_{li}", [128, EC, 128], BF16, es=es)
                sq2, b_sq2 = C.sb(f"l0fB_sq_{li}", [128, E], F32, es=es)
                st2, b_st2 = C.sb(f"l0fB_st_{li}", [128, 8], F32, es=es)
                gact, b_gact = C.sb(f"l0fB_gact_{li}", [128, 128], F32, es=es)
                aact, b_aact = C.sb(f"l0fB_aact_{li}", [128, 128], F32, es=es)
                hfm, b_hfm = C.sb(f"l0fB_hfm_{li}", [128, NJ, 128], BF16, es=es)
                zres2, b_zres2 = C.sb(f"l0fB_zres_{li}", [128, E], F32, es=es)
                zout2, b_zout2 = C.sb(f"l0fB_zout_{li}", [128, E], F32, es=es)
                psF2, b_psF2 = C.ps(f"l0fB_psF_{li}", [128, E], BF16, es=es)
                psA, b_psA = C.ps(f"l0fB_psA_{li}", [128, 128], F32, es=es)
                psG, b_psG = C.ps(f"l0fB_psG_{li}", [128, 128], F32, es=es)
                psY, b_psY = [list(z) for z in zip(
                    C.ps(f"l0fB_pY0_{li}", [128, 512], F32, es=es),
                    C.ps(f"l0fB_pY1_{li}", [128, 512], F32, es=es))]
                for bb in range(B):
                    for tt in range(S // 128):
                        t0 = tt * 128
                        if tt == 0:
                            P.op("gpsimd", lambda g: g.memset(
                                xs2[0:1, 512:1024], 0.0), writes=[b_xs2])
                            P.dma("sync", out=xs2[:, 0:512],
                                  in_=zsB[0][bb, 0:128, 0:512],
                                  reads=[zsB[1]], writes=[b_xs2], slot=b_xs2)
                            P.dma("sync", out=xs2[1:128, 512:1024],
                                  in_=zsB[0][bb, 0:127, 512:1024],
                                  reads=[zsB[1]], writes=[b_xs2], slot=b_xs2)
                        else:
                            P.dma("sync", out=xs2[:, 0:512],
                                  in_=zsB[0][bb, t0:t0 + 128, 0:512],
                                  reads=[zsB[1]], writes=[b_xs2], slot=b_xs2)
                            P.dma("sync", out=xs2[:, 512:1024],
                                  in_=zsB[0][bb, t0 - 1:t0 + 127, 512:1024],
                                  reads=[zsB[1]], writes=[b_xs2], slot=b_xs2)
                        ln_tile(C, G, xs2[:], b_xs2, xn2[:], b_xn2, st2, b_st2,
                                sq2[:], b_sq2)

                        def affine(v):
                            v.tensor_mul(xg2[:], xn2[:], g_bc[:])
                            v.drain()
                            return v.tensor_add(xb2[:], xg2[:], lb_bc[:])
                        P.op("vector", affine,
                             reads=[b_xn2, b_g_bc, b_lb_bc],
                             writes=[b_xg2, b_xb2], flush=True)
                        transpose_to_fm(C, G, xb2[:], b_xb2, psF2[:], b_psF2,
                                        xf2[:], b_xf2)
                        for j in range(NJ):
                            pj = 128 if j < 21 else 42

                            def mma(t, j=j, pj=pj):
                                last = None
                                for ec in range(EC):
                                    last = t.matmul(
                                        psA[0:pj, :],
                                        w1[:, ec, j * 128:j * 128 + pj],
                                        xf2[:, ec, :], start=(ec == 0),
                                        stop=(ec == EC - 1))
                                return last
                            P.op("tensor", mma, reads=[b_xf2, b_w1],
                                 writes=[b_psA])

                            def mmg(t, j=j, pj=pj):
                                last = None
                                for ec in range(EC):
                                    last = t.matmul(
                                        psG[0:pj, :],
                                        w1[:, ec, DI + j * 128:DI + j * 128 + pj],
                                        xf2[:, ec, :], start=(ec == 0),
                                        stop=(ec == EC - 1))
                                return last
                            P.op("tensor", mmg, reads=[b_xf2, b_w1],
                                 writes=[b_psG])
                            P.op("scalar", lambda s, j=j, pj=pj: s.activation(
                                gact[0:pj, :], psG[0:pj, :], AF.Gelu,
                                bias=b1g[0:pj, j:j + 1]),
                                reads=[b_psG, b_b1g], writes=[b_gact])
                            P.op("vector", lambda v, j=j, pj=pj:
                                 v.tensor_scalar(aact[0:pj, :], psA[0:pj, :],
                                                 b1a[0:pj, j:j + 1], None,
                                                 ALU.add),
                                 reads=[b_psA, b_b1a], writes=[b_aact],
                                 flush=True)
                            P.op("vector", lambda v, j=j, pj=pj: v.tensor_mul(
                                hfm[0:pj, j, :], aact[0:pj, :], gact[0:pj, :]),
                                reads=[b_aact, b_gact], writes=[b_hfm])
                        P.dma("scalar", out=zres2[:],
                              in_=zsB[0][bb, t0:t0 + 128, :],
                              reads=[zsB[1]], writes=[b_zres2], slot=b_zres2)
                        for ds in range(2):
                            def mmy(t, ds=ds):
                                last = None
                                for j in range(NJ):
                                    pj = 128 if j < 21 else 42
                                    last = t.matmul(
                                        psY[ds][:], hfm[0:pj, j, :],
                                        w2[0:pj, j, ds * 512:(ds + 1) * 512],
                                        start=(j == 0), stop=(j == NJ - 1))
                                return last
                            P.op("tensor", mmy, reads=[b_hfm, b_w2],
                                 writes=[b_psY[ds]])

                            def evy(v, ds=ds):
                                v.tensor_add(
                                    zout2[:, ds * 512:(ds + 1) * 512],
                                    psY[ds][:],
                                    b2_bc[:, ds * 512:(ds + 1) * 512])
                                v.drain()
                                return v.tensor_add(
                                    zout2[:, ds * 512:(ds + 1) * 512],
                                    zout2[:, ds * 512:(ds + 1) * 512],
                                    zres2[:, ds * 512:(ds + 1) * 512])
                            P.op("vector", evy,
                                 reads=[b_psY[ds], b_b2_bc, b_zres2],
                                 writes=[b_zout2])
                        P.dma("gpsimd",
                              out=zdB[0][bb, t0:t0 + 128, :], in_=zout2[:],
                              reads=[b_zout2], writes=[zdB[1]], slot=b_zout2)
            P.barrier()
        P.dma("gpsimd", out=y[0], in_=z0[0].rearrange("b s e -> (b s) e"),
              reads=[z0[1]], writes=[y[1]])
    P.barrier()
    stage_post_ln_add(C, G, y[0], y[1], resid, b_resid)
    P.barrier()


EMITTERS = {"L0_attn": emit_L0_attn, "L3_attn": emit_L3_attn,
            "L2_ffn": emit_L2_ffn, "L1_ffn": emit_L1_ffn,
            "L3_ffn": emit_L3_ffn, "L2_attn": emit_L2_attn,
            "L1_attn": emit_L1_attn, "L0_ffn": emit_L0_ffn}


def input_specs():
    return {
        "L0_attn_q_w": (E, E), "L0_attn_k_w": (E, E), "L0_attn_v_w": (E, E),
        "L0_attn_out_w": (E, E), "L0_attn_out_b": (E,),
        "L0_ffn_Wqkv": (6, E, 3072), "L0_ffn_q_scale": (6, 64),
        "L0_ffn_k_scale": (6, 64), "L0_ffn_Wo": (6, E, E),
        "L0_ffn_ln_g": (6, E), "L0_ffn_ln_b": (6, E),
        "L0_ffn_W1": (6, E, 5460), "L0_ffn_b1": (6, 5460),
        "L0_ffn_W2": (6, 2730, E), "L0_ffn_b2": (6, E),
        "L1_attn_Wk": (4, 128, E), "L1_attn_Wq": (4, 128, E),
        "L1_attn_Wv": (4, 128, E), "L1_attn_W_out": (E, 512),
        "L1_attn_b_out": (E,),
        "L1_ffn_router_w": (16, E), "L1_ffn_correction_bias": (16,),
        "L1_ffn_gate_w": (16, E, 256), "L1_ffn_up_w": (16, E, 256),
        "L1_ffn_down_w": (16, 256, E), "L1_ffn_shared_gate_w": (2048, E),
        "L1_ffn_shared_up_w": (2048, E), "L1_ffn_shared_down_w": (E, 2048),
        "L2_attn_w_qkv": (E, 3072), "L2_attn_w_out": (E, E),
        "L2_attn_b_out": (E,),
        "L2_ffn_Wg": (E, 8), "L2_ffn_W1": (8, E, 1408),
        "L2_ffn_W2": (8, E, 1408), "L2_ffn_W3": (8, 1408, E),
        "L2_ffn_Ws1": (E, 2816), "L2_ffn_Ws2": (E, 2816),
        "L2_ffn_Ws3": (2816, E),
        "L3_attn_Wq": (E, E), "L3_attn_bq": (E,), "L3_attn_Wk": (E, E),
        "L3_attn_bk": (E,), "L3_attn_Wv": (E, E), "L3_attn_bv": (E,),
        "L3_attn_Wo": (E, E), "L3_attn_bo": (E,),
        "L3_ffn_base_W": (E, E), "L3_ffn_base_b": (E,),
        "L3_ffn_disc_mu": (8, E), "L3_ffn_disc_sigma": (8, E),
        "L3_ffn_down_W": (8, E, 64), "L3_ffn_down_b": (8, 64),
        "L3_ffn_up_W": (8, 64, E), "L3_ffn_up_b": (8, E),
        "L3_ffn_conv_W": (8, E, E), "L3_ffn_conv_b": (8, E),
    }


def host_consts():
    consts = {}
    for name, nw in (("c_l2a_maskA", 32), ("c_l2a_maskB", 31)):
        m = np.full((128, 128), -FMAX, np.float32)
        for i2 in range(2):
            for n in range(nw):
                for i in range(2):
                    r = i2 * 2 * nw + 2 * n + i
                    m[r, 4 * n:4 * n + 4] = 0.0
        consts[name] = m
    DH, WIN = 64, 256
    half = np.arange(0, DH, 2, dtype=np.float64)
    inv_freq = 1.0 / 10000.0 ** (half / DH)
    t = np.arange(S, dtype=np.float64)
    freqs = np.concatenate([t[:, None] * inv_freq[None, :]] * 2, axis=1)
    base_scale = (half + 0.4 * DH) / (1.4 * DH)
    power = (t - S // 2) / (WIN // 2)
    scale = np.concatenate([base_scale[None, :] ** power[:, None]] * 2, axis=1)
    cos, sin = np.cos(freqs), np.sin(freqs)
    consts["c_l0f_csq"] = (cos * scale).astype(np.float32)
    consts["c_l0f_snq"] = (sin * scale).astype(np.float32)
    consts["c_l0f_csk"] = (cos / scale).astype(np.float32)
    consts["c_l0f_snk"] = (sin / scale).astype(np.float32)
    return consts


def block_weight_names(block):
    pref = block + "_"
    return [k for k in input_specs() if k.startswith(pref)]


def build_core_program(blocks):
    nc = bass.Bass("TRN2", target_bir_lowering=False)
    x_in = nc.dram_tensor("x_in", [T, E], F32, kind="ExternalInput").ap()
    x_out = nc.dram_tensor("x_out", [T, E], F32, kind="ExternalOutput").ap()
    specs = input_specs()
    consts = host_consts()
    es = ExitStack()
    with es:
        P = Prog(nc)
        C = Ctx(nc, P, es)
        for blk in blocks:
            for name in block_weight_names(blk):
                if name not in C.W:
                    C.W[name] = nc.dram_tensor(
                        name, list(specs[name]), F32, kind="ExternalInput").ap()
        for name, arr in consts.items():
            C.W[name] = nc.dram_tensor(name, list(arr.shape), F32,
                                       kind="ExternalInput").ap()
        G = Shared(C)
        resid, b_resid0 = C.dram("resid", (T, E), F32)
        b_resid = b_resid0
        P.dma("sync", out=resid, in_=x_in, writes=[b_resid])
        P.barrier()
        for blk in blocks:
            EMITTERS[blk](C, G, resid, b_resid)
        outb = Buf("x_out")
        P.dma("gpsimd", out=x_out, in_=resid, reads=[b_resid], writes=[outb])
        for tname in DEBUG_TAPS:
            ap, bufd = C.dram_cache[tname]
            tap = nc.dram_tensor("tap_" + tname, list(ap.shape), ap.dtype,
                                 kind="ExternalOutput").ap()
            P.dma("gpsimd", out=tap, in_=ap, reads=[bufd], writes=[Buf()])
        P.barrier()
        P.op("sync", lambda e: e.nop())

        with ExitStack() as semstack:
            sems = {}
            for e in ENGINES:
                sems[("c", e)] = semstack.enter_context(nc.semaphore("c_" + e))
            for i in range(Prog.NDSEM + len(ENGINES)):
                sems[("s", i)] = semstack.enter_context(nc.semaphore(f"ds{i}"))
            block = semstack.enter_context(nc.Block())
            P.playback(block, sems)
    return nc


def run_blocks(x_full, inputs, blocks):
    nc = build_core_program(blocks)
    names = set()
    for blk in blocks:
        names.update(block_weight_names(blk))
    consts = host_consts()
    in_maps = []
    for c in range(NCORES):
        m = {"x_in": np.ascontiguousarray(
            x_full[c * B:(c + 1) * B].reshape(T, E))}
        for n in names:
            m[n] = inputs[n]
        m.update(consts)
        in_maps.append(m)
    res = run_bass_kernel_spmd(nc, in_maps, core_ids=list(range(NCORES)))
    out = np.concatenate([r["x_out"].reshape(B, S, E) for r in res.results],
                         axis=0)
    return out


ALL_BLOCKS = ["L0_attn", "L0_ffn", "L1_attn", "L1_ffn",
              "L2_attn", "L2_ffn", "L3_attn", "L3_ffn"]


def kernel(**inputs):
    x = np.asarray(inputs["L0_attn_x"], dtype=np.float32)
    w = {k: np.ascontiguousarray(np.asarray(v, dtype=np.float32))
         for k, v in inputs.items() if k != "L0_attn_x"}
    todo = [b for b in ALL_BLOCKS if b in EMITTERS]
    out = run_blocks(x, w, todo)
    return out.astype(np.float32)


# version 03 (final)
# speedup vs baseline: 1.0075x; 1.0075x over previous
import math
from contextlib import ExitStack

import numpy as np
import ml_dtypes

import concourse.bass as bass
import concourse.mybir as mybir
from concourse.bass_types import AP
from concourse.bass_core_rust import IndirectOffsetOnAxis
from concourse.bass_utils import run_bass_kernel_spmd

F32 = mybir.dt.float32
BF16 = mybir.dt.bfloat16
I32 = mybir.dt.int32
AF = mybir.ActivationFunctionType
ALU = mybir.AluOpType
AX = mybir.AxisListType

NCORES = 8
B = 2
S = 2048
E = 1024
T = B * S
EC = E // 128
NT = T // 128
FMAX = 3.4028235e38

ENGINES = ("sync", "gpsimd", "scalar", "vector", "tensor")

DEBUG_TAPS = []
L1F_RANGE = list(range(18))


class Buf:
    __slots__ = ("w", "rs", "name", "dsem")

    def __init__(self, name=""):
        self.w = None
        self.rs = []
        self.name = name
        self.dsem = None


class Op:
    __slots__ = ("eng", "fn", "dma", "seq", "waits", "awaited", "val", "flush",
                 "dsem")


class Prog:
    SERIAL_DMA = False
    NDSEM = 48

    def __init__(self, nc):
        self.nc = nc
        self.streams = {e: [] for e in ENGINES}
        self.seq = 0
        self.fence = []
        self.last_dma = {e: None for e in ENGINES}
        self.dsem_count = [0] * (Prog.NDSEM + len(ENGINES))
        self.dsem_free = list(range(Prog.NDSEM))
        self.dsem_assigned = []

    def _slot_sem(self, buf):
        if buf is None:
            return None
        if buf.dsem is None:
            if not self.dsem_free:
                raise RuntimeError("out of DMA slot sems; add a barrier or bump NDSEM")
            buf.dsem = self.dsem_free.pop()
            self.dsem_assigned.append(buf)
        return buf.dsem

    def _mk(self, eng, fn, dma, reads, writes, after, flush=False):
        deps = set()
        for b in reads:
            if b.w is not None:
                deps.add(b.w)
        for b in writes:
            if b.w is not None:
                deps.add(b.w)
            deps.update(b.rs)
        deps.update(after)
        deps.update(self.fence)
        o = Op()
        o.eng, o.fn, o.dma = eng, fn, dma
        o.flush = flush
        o.seq = self.seq
        self.seq += 1
        o.waits = list(deps)
        o.awaited = dma
        o.val = None
        self.streams[eng].append(o)
        for b in reads:
            b.rs.append(o)
        for b in writes:
            b.w = o
            b.rs = []
        return o

    def op(self, eng, fn, reads=(), writes=(), after=(), flush=False):
        return self._mk(eng, fn, False, reads, writes, after, flush)

    def dma(self, queue, out=None, in_=None, reads=(), writes=(), after=(),
            raw=None, slot=None, **kw):
        if raw is None:
            def fn(e, o=out, i=in_, kw=kw):
                return e.dma_start(out=o, in_=i, **kw)
        else:
            fn = raw
        if Prog.SERIAL_DMA and self.last_dma[queue] is not None:
            after = tuple(after) + (self.last_dma[queue],)
        o = self._mk(queue, fn, True, reads, writes, after)
        sid = self._slot_sem(slot)
        if sid is None:
            sid = Prog.NDSEM + ENGINES.index(queue)
        o.dsem = sid
        self.dsem_count[sid] += 16
        o.val = self.dsem_count[sid]
        self.last_dma[queue] = o
        return o

    def barrier(self):
        self.fence = [self.streams[e][-1] for e in ENGINES if self.streams[e]]
        for b in self.dsem_assigned:
            self.dsem_free.append(b.dsem)
            b.dsem = None
        self.dsem_assigned = []

    def playback(self, block, sems):
        for e in ENGINES:
            for o in self.streams[e]:
                for d in o.waits:
                    if d.eng != o.eng or d.dma:
                        d.awaited = True
        for e in ENGINES:
            c = 0
            for o in self.streams[e]:
                if not o.dma and o.awaited:
                    c += 1
                    o.val = c
        def emit(e):
            def run(engobj):
                seen = {}
                for o in self.streams[e]:
                    for d in sorted(o.waits, key=lambda d: d.seq):
                        if d.eng == e and not d.dma:
                            continue
                        key = ("s", d.dsem) if d.dma else ("c", d.eng)
                        if seen.get(key, 0) >= d.val:
                            continue
                        engobj.wait_ge(sems[key], d.val)
                        seen[key] = d.val
                    ins = o.fn(engobj)
                    if not o.dma and (o.flush or
                                      (o.awaited and e in ("vector", "scalar",
                                                          "gpsimd"))):
                        ins = engobj.drain(fusable=False)
                    if o.dma:
                        ins.then_inc(sems[("s", o.dsem)], 16)
                    elif o.awaited:
                        ins.then_inc(sems[("c", e)], 1)
            return run

        for e in ENGINES:
            if self.streams[e]:
                getattr(block, e)(emit(e))


class Ctx:

    def __init__(self, nc, P, es):
        self.nc = nc
        self.P = P
        self.es = es
        self.W = {}
        self.dram_cache = {}
        self.sb_scopes = []
        self._names = {}

    def dram(self, name, shape, dtype):
        key = name
        if key not in self.dram_cache:
            t = self.nc.dram_tensor(name, list(shape), dtype, kind="Internal")
            self.dram_cache[key] = (t.ap(), Buf(name))
        return self.dram_cache[key]

    def _uniq(self, name):
        n = self._names.get(name, 0)
        self._names[name] = n + 1
        return name if n == 0 else f"{name}__{n}"

    def sb(self, name, shape, dtype, es=None, side=None):
        name = self._uniq(name)
        t = (es or self.es).enter_context(
            self.nc.sbuf_tensor(name, list(shape), dtype, side=side))
        return t, Buf(name)

    def ps(self, name, shape, dtype=F32, es=None, side=None):
        name = self._uniq(name)
        t = (es or self.es).enter_context(
            self.nc.psum_tensor(name, list(shape), dtype, side=side))
        return t, Buf(name)


def mkap(base, offset_add, freedims):
    return AP(base.tensor, base.offset + offset_add,
              [list(base.ap[0])] + [list(f) for f in freedims])


class Shared:

    def __init__(self, C):
        nc, P = C.nc, C.P
        self.identb, self.identb_b = C.sb("g_identb", [128, 128], BF16)
        self.identf, self.identf_b = C.sb("g_identf", [128, 128], F32)
        self.onesrow, self.onesrow_b = C.sb("g_onesrow", [1, 128], F32)
        self.onesrow_bf, self.onesrow_bf_b = C.sb("g_onesrow_bf", [1, 128], BF16)
        self.onescol, self.onescol_b = C.sb("g_onescol", [128, 1], F32)
        self.eps, self.eps_b = C.sb("g_eps", [128, 4], F32)
        self.zero512, self.zero512_b = C.sb("g_zero512", [128, 512], F32)

        def setup(g):
            g.memset(self.identb[:], 1.0)
            g.affine_select(out=self.identb[:], in_=self.identb[:],
                            compare_op=ALU.is_equal, fill=0.0, base=0,
                            pattern=[[-1, 128]], channel_multiplier=1)
            g.memset(self.identf[:], 1.0)
            g.affine_select(out=self.identf[:], in_=self.identf[:],
                            compare_op=ALU.is_equal, fill=0.0, base=0,
                            pattern=[[-1, 128]], channel_multiplier=1)
            g.memset(self.onesrow[:], 1.0)
            g.memset(self.onesrow_bf[:], 1.0)
            g.memset(self.onescol[:], 1.0)
            g.memset(self.eps[:, 0:1], 1e-5)
            g.memset(self.eps[:, 1:2], 1e-12)
            g.memset(self.eps[:, 2:3], 0.0)
            g.memset(self.eps[:, 3:4], 1e-20)
            return g.memset(self.zero512[:], 0.0)

        wr = [self.identb_b, self.identf_b, self.onesrow_b, self.onesrow_bf_b,
              self.onescol_b, self.eps_b, self.zero512_b]
        P.op("gpsimd", setup, writes=wr)


def ln_tile(C, G, v_x, b_x, v_out, b_out, stat, b_stat, sq, b_sq, width=E,
            out_dtype_note=None):
    P = C.P
    P.op("vector", lambda v: v.reduce_sum(stat[:, 0:1], v_x, axis=AX.X),
         reads=[b_x], writes=[b_stat], flush=True)

    def sqacc(s):
        s.activation(sq, v_x, AF.Square, accum_out=stat[:, 2:3])
        s.drain()
        return s.mul(stat[:, 3:4], stat[:, 2:3], 1.0 / width)

    P.op("scalar", sqacc, reads=[b_x], writes=[b_sq, b_stat], flush=True)

    def meansq(v):
        v.tensor_scalar_mul(stat[:, 1:2], stat[:, 0:1], 1.0 / width)
        return v.scalar_tensor_tensor(stat[:, 7:8], stat[:, 0:1],
                                      1.0 / (width * width), stat[:, 0:1],
                                      ALU.mult, ALU.mult)

    P.op("vector", meansq, reads=[b_stat], writes=[b_stat], flush=True)
    P.op("vector", lambda v: v.tensor_sub(stat[:, 4:5], stat[:, 3:4],
                                          stat[:, 7:8]),
         reads=[b_stat], writes=[b_stat], flush=True)
    P.op("scalar", lambda s: s.activation(stat[:, 5:6], stat[:, 4:5], AF.Sqrt,
                                          bias=G.eps[:, 0:1]),
         reads=[b_stat, G.eps_b], writes=[b_stat], flush=True)
    P.op("vector", lambda v: v.reciprocal(stat[:, 6:7], stat[:, 5:6]),
         reads=[b_stat], writes=[b_stat], flush=True)
    P.op("vector", lambda v: v.tensor_scalar(v_out, v_x, stat[:, 1:2],
                                             stat[:, 6:7], ALU.subtract,
                                             ALU.mult),
         reads=[b_x, b_stat], writes=[b_out])


def transpose_to_fm(C, G, v_src_bf, b_src, pst, b_pst, v_dst, b_dst, nchunks=8):
    P = C.P

    def tr(t):
        last = None
        for c in range(nchunks):
            last = t.transpose(pst[:, c * 128:(c + 1) * 128],
                               v_src_bf[:, c * 128:(c + 1) * 128], G.identb[:])
        return last

    P.op("tensor", tr, reads=[b_src, G.identb_b], writes=[b_pst])
    P.op("vector", lambda v: v.tensor_copy(v_dst, pst[:, 0:nchunks * 128]),
         reads=[b_pst], writes=[b_dst])


def prep_wT(C, G, name, w_ap, din, dout, es, dtype=BF16, es_tmp=None):
    P = C.P
    wt, b_wt = C.sb(name, [128, din // 128, dout], dtype, es=es)
    stg, b_stg = C.sb(name + "_stg", [128, 512], dtype, es=es_tmp or es,
                      side="right" if es_tmp else None)
    pst, b_pst = C.ps(name + "_pst", [128, 512], dtype, es=es_tmp or es,
                      side="right" if es_tmp else None)
    ident = G.identb if dtype == BF16 else G.identf
    b_ident = G.identb_b if dtype == BF16 else G.identf_b
    queue = "gpsimd" if dtype == BF16 else "sync"
    for o0 in range(0, dout, 128):
        on = min(128, dout - o0)
        for c0 in range(0, din, 512):
            cn = min(512, din - c0)
            P.dma(queue, out=stg[0:on, 0:cn], in_=w_ap[o0:o0 + on, c0:c0 + cn],
                  writes=[b_stg], slot=b_stg)

            def tr(t, on=on, cn=cn):
                last = None
                for cc in range(0, cn, 128):
                    last = t.transpose(pst[:, cc:cc + on],
                                       stg[0:on, cc:cc + 128],
                                       ident[0:on, 0:on])
                return last
            P.op("tensor", tr, reads=[b_stg, b_ident], writes=[b_pst])

            def ev(v, o0=o0, on=on, c0=c0, cn=cn):
                last = None
                for cc in range(0, cn, 128):
                    last = v.tensor_copy(wt[:, (c0 + cc) // 128, o0:o0 + on],
                                         pst[:, cc:cc + on])
                return last
            P.op("vector", ev, reads=[b_pst], writes=[b_wt])
    return wt, b_wt


def load_w_natural(C, name, w_ap, din, dout, es, dtype=BF16):
    P = C.P
    wt, b_wt = C.sb(name, [128, din // 128, dout], dtype, es=es)
    P.dma("gpsimd", out=wt[:],
          in_=w_ap.rearrange("(c p) o -> p c o", p=128), writes=[b_wt], slot=b_wt)
    return wt, b_wt


def bcast_row(C, G, name, row_ap, n, es, dtype=F32, es_tmp=None):
    P = C.P
    rw, b_rw = C.sb(name + "_r", [1, n], F32, es=es_tmp or es,
                    side="right" if es_tmp else None)
    out, b_out = C.sb(name, [128, n], dtype, es=es)
    P.dma("sync", out=rw[:], in_=row_ap.unsqueeze(0) if row_ap.ndim == 1 else row_ap,
          writes=[b_rw], slot=b_rw)
    pst, b_pst = C.ps(name + "_ps", [128, 512], F32, es=es_tmp or es,
                      side="right" if es_tmp else None)
    for c0 in range(0, n, 512):
        cn = min(512, n - c0)
        P.op("tensor", lambda t, c0=c0, cn=cn: t.matmul(
            pst[:, 0:cn], G.onesrow[0:1, :], rw[0:1, c0:c0 + cn],
            start=True, stop=True),
            reads=[b_rw, G.onesrow_b], writes=[b_pst])
        P.op("vector", lambda v, c0=c0, cn=cn: v.tensor_copy(
            out[:, c0:c0 + cn], pst[:, 0:cn]), reads=[b_pst], writes=[b_out])
    return out, b_out


def stage_prep_xn(C, G, resid, b_resid, want_fm=None, want_tm32=None,
                  want_tmbf=None, want_fm32=None):
    P = C.P
    with ExitStack() as es:
        x, b_x = [list(z) for z in zip(C.sb("px_x0", [128, E], F32, es=es),
                                       C.sb("px_x1", [128, E], F32, es=es))]
        xn, b_xn = [list(z) for z in zip(C.sb("px_n0", [128, E], F32, es=es),
                                         C.sb("px_n1", [128, E], F32, es=es))]
        xb, b_xb = [list(z) for z in zip(C.sb("px_b0", [128, E], BF16, es=es),
                                         C.sb("px_b1", [128, E], BF16, es=es))]
        xf, b_xf = [list(z) for z in zip(C.sb("px_f0", [128, EC, 128], BF16, es=es),
                                         C.sb("px_f1", [128, EC, 128], BF16, es=es))]
        sq, b_sq = C.sb("px_sq", [128, E], F32, es=es)
        st, b_st = [list(z) for z in zip(C.sb("px_s0", [128, 8], F32, es=es),
                                         C.sb("px_s1", [128, 8], F32, es=es))]
        pst, b_pst = [list(z) for z in zip(C.ps("px_p0", [128, E], BF16, es=es),
                                           C.ps("px_p1", [128, E], BF16, es=es))]
        if want_fm32 is not None:
            xf32, b_xf32 = [list(z) for z in zip(
                C.sb("px_g0", [128, EC, 128], F32, es=es),
                C.sb("px_g1", [128, EC, 128], F32, es=es))]
            pst32, b_pst32 = [list(z) for z in zip(
                C.ps("px_q0", [128, E], F32, es=es),
                C.ps("px_q1", [128, E], F32, es=es))]
        for i in range(NT):
            j = i % 2
            P.dma("sync", out=x[j][:], in_=resid[i * 128:(i + 1) * 128, :],
                  reads=[b_resid], writes=[b_x[j]], slot=b_x[j])
            ln_tile(C, G, x[j][:], b_x[j], xn[j][:], b_xn[j], st[j], b_st[j],
                    sq[:], b_sq)
            P.op("vector", lambda v, j=j: v.tensor_copy(xb[j][:], xn[j][:]),
                 reads=[b_xn[j]], writes=[b_xb[j]])
            if want_tm32 is not None:
                P.dma("gpsimd", out=want_tm32[0][i * 128:(i + 1) * 128, :],
                      in_=xn[j][:], reads=[b_xn[j]], writes=[want_tm32[1]], slot=b_xn[j])
            if want_tmbf is not None:
                P.dma("gpsimd", out=want_tmbf[0][i * 128:(i + 1) * 128, :],
                      in_=xb[j][:], reads=[b_xb[j]], writes=[want_tmbf[1]], slot=b_xb[j])
            bb = i // (S // 128)
            t0 = (i % (S // 128)) * 128
            if want_fm is not None:
                transpose_to_fm(C, G, xb[j][:], b_xb[j], pst[j][:], b_pst[j],
                                xf[j][:], b_xf[j])
                P.dma("gpsimd",
                      out=want_fm[0][bb, :, :, t0:t0 + 128].rearrange("c p t -> p c t"),
                      in_=xf[j][:], reads=[b_xf[j]], writes=[want_fm[1]], slot=b_xf[j])
            if want_fm32 is not None:
                def trf(t, j=j):
                    last = None
                    for c in range(EC):
                        last = t.transpose(pst32[j][:, c * 128:(c + 1) * 128],
                                           xn[j][:, c * 128:(c + 1) * 128],
                                           G.identf[:])
                    return last
                P.op("tensor", trf, reads=[b_xn[j], G.identf_b],
                     writes=[b_pst32[j]])
                P.op("vector", lambda v, j=j: v.tensor_copy(
                    xf32[j][:], pst32[j][:]), reads=[b_pst32[j]],
                    writes=[b_xf32[j]])
                P.dma("gpsimd",
                      out=want_fm32[0][bb, :, :, t0:t0 + 128].rearrange("c p t -> p c t"),
                      in_=xf32[j][:], reads=[b_xf32[j]], writes=[want_fm32[1]],
                      slot=b_xf32[j])


def stage_post_ln_add(C, G, y, b_y, resid, b_resid):
    P = C.P
    with ExitStack() as es:
        yv, b_yv = [list(z) for z in zip(C.sb("po_y0", [128, E], F32, es=es),
                                         C.sb("po_y1", [128, E], F32, es=es))]
        rv, b_rv = [list(z) for z in zip(C.sb("po_r0", [128, E], F32, es=es),
                                         C.sb("po_r1", [128, E], F32, es=es))]
        yn, b_yn = [list(z) for z in zip(C.sb("po_n0", [128, E], F32, es=es),
                                         C.sb("po_n1", [128, E], F32, es=es))]
        rs, b_rs = [list(z) for z in zip(C.sb("po_o0", [128, E], F32, es=es),
                                         C.sb("po_o1", [128, E], F32, es=es))]
        sq, b_sq = C.sb("po_sq", [128, E], F32, es=es)
        st, b_st = [list(z) for z in zip(C.sb("po_s0", [128, 8], F32, es=es),
                                         C.sb("po_s1", [128, 8], F32, es=es))]
        for i in range(NT):
            j = i % 2
            P.dma("sync", out=yv[j][:], in_=y[i * 128:(i + 1) * 128, :],
                  reads=[b_y], writes=[b_yv[j]], slot=b_yv[j])
            P.dma("sync", out=rv[j][:], in_=resid[i * 128:(i + 1) * 128, :],
                  reads=[b_resid], writes=[b_rv[j]], slot=b_rv[j])
            ln_tile(C, G, yv[j][:], b_yv[j], yn[j][:], b_yn[j], st[j], b_st[j],
                    sq[:], b_sq)
            P.op("vector", lambda v, j=j: v.tensor_add(rs[j][:], rv[j][:], yn[j][:]),
                 reads=[b_yn[j], b_rv[j]], writes=[b_rs[j]])
            P.dma("gpsimd", out=resid[i * 128:(i + 1) * 128, :], in_=rs[j][:],
                  reads=[b_rs[j]], writes=[b_resid], slot=b_rs[j])


def emit_L0_attn(C, G, resid, b_resid):
    P = C.P
    H, HD = 16, 64
    xnfm = C.dram("l0a_xnfm", (B, EC, 128, S), BF16)
    qfm = C.dram("l0a_q", (B, EC, 128, S), BF16)
    kfm = C.dram("l0a_k", (B, EC, 128, S), BF16)
    vtm = C.dram("l0a_v", (B, S, E), BF16)
    ofm = C.dram("l0a_o", (B, EC, 128, S), BF16)
    y = C.dram("l0a_y", (T, E), F32)

    stage_prep_xn(C, G, resid, b_resid, want_fm=xnfm)
    P.barrier()

    with ExitStack() as es:
        qwT, b_qwT = prep_wT(C, G, "l0a_qwT", C.W["L0_attn_q_w"], E, E, es)
        kwT, b_kwT = prep_wT(C, G, "l0a_kwT", C.W["L0_attn_k_w"], E, E, es)
        vwT, b_vwT = prep_wT(C, G, "l0a_vwT", C.W["L0_attn_v_w"], E, E, es)
        xsb, b_xsb = C.sb("l0a_xsb", [128, EC, S], BF16, es=es)
        stg, b_stg = [list(z) for z in zip(C.sb("l0a_st0", [128, 512], BF16, es=es),
                                           C.sb("l0a_st1", [128, 512], BF16, es=es))]
        psq, b_psq = [list(z) for z in zip(C.ps("l0a_ps0", [128, 512], F32, es=es),
                                           C.ps("l0a_ps1", [128, 512], F32, es=es))]
        for bb in range(B):
            P.dma("sync", out=xsb[:],
                  in_=xnfm[0][bb].rearrange("c p t -> p c t"),
                  reads=[xnfm[1]], writes=[b_xsb], slot=b_xsb)
            for which, wt, b_wt, dst in (("q", qwT, b_qwT, qfm),
                                         ("k", kwT, b_kwT, kfm)):
                for ic in range(EC):
                    for ts in range(S // 512):
                        j = (ic * (S // 512) + ts) % 2

                        def mm(t, ic=ic, ts=ts, j=j, wt=wt, psq=psq):
                            last = None
                            for ec in range(EC):
                                last = t.matmul(
                                    psq[j][:],
                                    wt[:, ec, ic * 128:(ic + 1) * 128],
                                    xsb[:, ec, ts * 512:(ts + 1) * 512],
                                    start=(ec == 0), stop=(ec == EC - 1))
                            return last
                        P.op("tensor", mm, reads=[b_wt, b_xsb], writes=[b_psq[j]])
                        P.op("vector", lambda v, j=j, stg=stg, psq=psq: v.tensor_copy(
                            stg[j][:], psq[j][:]), reads=[b_psq[j]],
                            writes=[b_stg[j]])
                        P.dma("gpsimd",
                              out=dst[0][bb, ic, :, ts * 512:(ts + 1) * 512],
                              in_=stg[j][:], reads=[b_stg[j]], writes=[dst[1]],
                              slot=b_stg[j])
            for tt in range(S // 128):
                for ds in range(E // 512):
                    j = (tt * (E // 512) + ds) % 2

                    def mmv(t, tt=tt, ds=ds, j=j, psq=psq):
                        last = None
                        for ec in range(EC):
                            last = t.matmul(
                                psq[j][:],
                                xsb[:, ec, tt * 128:(tt + 1) * 128],
                                vwT[:, ec, ds * 512:(ds + 1) * 512],
                                start=(ec == 0), stop=(ec == EC - 1))
                        return last
                    P.op("tensor", mmv, reads=[b_vwT, b_xsb], writes=[b_psq[j]])
                    P.op("vector", lambda v, j=j, stg=stg, psq=psq: v.tensor_copy(
                        stg[j][:], psq[j][:]), reads=[b_psq[j]], writes=[b_stg[j]])
                    P.dma("gpsimd",
                          out=vtm[0][bb, tt * 128:(tt + 1) * 128,
                                     ds * 512:(ds + 1) * 512],
                          in_=stg[j][:], reads=[b_stg[j]], writes=[vtm[1]],
                          slot=b_stg[j])
    P.barrier()

    NKC = S // 512
    with ExitStack() as es:
        qsb, b_qsb = C.sb("l0a_at_q", [64, S], BF16, es=es)
        ksb, b_ksb = C.sb("l0a_at_k", [64, S], BF16, es=es)
        vsb, b_vsb = C.sb("l0a_at_v", [128, S // 128, HD], BF16, es=es)
        ssbR = [C.sb("l0a_at_s", [128, S], F32, es=es) for _ in range(2)]
        pbfR = [C.sb("l0a_at_p", [128, S], BF16, es=es) for _ in range(2)]
        ptbR = [C.sb("l0a_at_pt", [128, S // 128, 128], BF16, es=es)
                for _ in range(2)]
        osbR = [C.sb("l0a_at_o", [64, 128], BF16, es=es) for _ in range(2)]
        sttR = [C.sb("l0a_at_st", [128, 8], F32, es=es) for _ in range(2)]
        psS = [C.ps(f"l0a_at_psS{i}", [128, 512], F32, es=es) for i in range(4)]
        psT = [C.ps(f"l0a_at_psT{i}", [128, 1024], BF16, es=es) for i in range(2)]
        psOR = [C.ps("l0a_at_psO", [64, 128], F32, es=es) for _ in range(2)]
        for bb in range(B):
            for h in range(H):
                ic, off = h // 2, (h % 2) * 64
                P.dma("sync", out=qsb[:], in_=qfm[0][bb, ic, off:off + 64, :],
                      reads=[qfm[1]], writes=[b_qsb], slot=b_qsb)
                P.dma("sync", out=ksb[:], in_=kfm[0][bb, ic, off:off + 64, :],
                      reads=[kfm[1]], writes=[b_ksb], slot=b_ksb)
                P.dma("sync", out=vsb[:],
                      in_=vtm[0][bb, :, h * 64:(h + 1) * 64].rearrange(
                          "(c p) d -> p c d", p=128),
                      reads=[vtm[1]], writes=[b_vsb], slot=b_vsb)
                def qt_iter(qt, ssb, b_ssb, pbf, b_pbf, ptb, b_ptb, osb,
                            b_osb, stt, b_stt, psO, b_psO):
                    for kc in range(NKC):
                        P.op("tensor", lambda t, kc=kc, qt=qt: t.matmul(
                            psS[kc][0][:], qsb[0:64, qt * 128:(qt + 1) * 128],
                            ksb[0:64, kc * 512:(kc + 1) * 512],
                            start=True, stop=True),
                            reads=[b_qsb, b_ksb], writes=[psS[kc][1]])
                    def ev(s):
                        last = None
                        for kc in range(4):
                            last = s.copy(ssb[:, kc * 512:(kc + 1) * 512],
                                          psS[kc][0][:])
                        return last
                    P.op("scalar", ev, reads=[psS[i][1] for i in range(4)],
                         writes=[b_ssb])
                    def mx1(v):
                        v.reduce_max(stt[:, 5:6], ssb[:, 0:1984], axis=AX.X)
                        return v.reduce_max(stt[:, 4:5], ssb[:, 1984:2048],
                                            axis=AX.X)
                    P.op("vector", mx1, reads=[b_ssb], writes=[b_stt], flush=True)
                    P.op("vector", lambda v: v.tensor_tensor(
                        stt[:, 6:7], stt[:, 5:6], stt[:, 4:5], ALU.max),
                        reads=[b_stt], writes=[b_stt], flush=True)
                    def pre(v):
                        v.tensor_scalar_mul(stt[:, 7:8], stt[:, 6:7], -0.125)
                        return v.tensor_sub(stt[:, 1:2], stt[:, 5:6],
                                            stt[:, 6:7])
                    P.op("vector", pre, reads=[b_stt], writes=[b_stt], flush=True)

                    def exp64(s):
                        s.activation(pbf[:, 1984:2048], ssb[:, 1984:2048],
                                     AF.Exp, bias=stt[:, 7:8], scale=0.125,
                                     accum_out=stt[:, 0:1])
                        s.drain()
                        s.mul(stt[:, 3:4], stt[:, 0:1], 1.0)
                        return s.activation(stt[:, 2:3], stt[:, 1:2], AF.Exp,
                                            scale=0.125)
                    P.op("scalar", exp64, reads=[b_ssb, b_stt],
                         writes=[b_pbf, b_stt], flush=True)
                    P.op("vector", lambda v: v.tensor_scalar_add(
                        stt[:, 5:6], stt[:, 2:3], 1.0),
                        reads=[b_stt], writes=[b_stt], flush=True)
                    P.op("vector", lambda v: v.tensor_tensor(
                        stt[:, 6:7], stt[:, 3:4], stt[:, 5:6], ALU.mult),
                        reads=[b_stt], writes=[b_stt], flush=True)
                    P.op("scalar", lambda s: s.activation(
                        stt[:, 4:5], stt[:, 6:7], AF.Ln),
                        reads=[b_stt], writes=[b_stt], flush=True)
                    P.op("vector", lambda v: v.tensor_sub(
                        stt[:, 7:8], stt[:, 7:8], stt[:, 4:5]),
                        reads=[b_stt], writes=[b_stt], flush=True)
                    P.op("scalar", lambda s: s.activation(
                        pbf[:, 0:1984], ssb[:, 0:1984], AF.Exp,
                        bias=stt[:, 7:8], scale=0.125),
                        reads=[b_ssb, b_stt], writes=[b_pbf])
                    P.op("scalar", lambda s: s.activation(
                        pbf[:, 1984:2048], ssb[:, 1984:2048], AF.Exp,
                        bias=stt[:, 7:8], scale=0.125),
                        reads=[b_ssb, b_stt], writes=[b_pbf])

                    for half in range(2):
                        def trp(t, half=half):
                            last = None
                            for c in range(8):
                                last = t.transpose(
                                    psT[half][0][:, c * 128:(c + 1) * 128],
                                    pbf[:, (half * 8 + c) * 128:(half * 8 + c + 1) * 128],
                                    G.identb[:])
                            return last
                        P.op("tensor", trp, reads=[b_pbf, G.identb_b],
                             writes=[psT[half][1]])
                        P.op("vector", lambda v, half=half: v.tensor_copy(
                            ptb[:, half * 8:(half + 1) * 8, :], psT[half][0][:]),
                            reads=[psT[half][1]], writes=[b_ptb])

                    def pv(t):
                        last = None
                        for kc in range(S // 128):
                            last = t.matmul(psO[:], vsb[:, kc, :],
                                            ptb[:, kc, :],
                                            start=(kc == 0),
                                            stop=(kc == S // 128 - 1))
                        return last
                    P.op("tensor", pv, reads=[b_vsb, b_ptb], writes=[b_psO])
                    P.op("vector", lambda v: v.tensor_copy(osb[0:64, :], psO[0:64, :]),
                         reads=[b_psO], writes=[b_osb])
                    P.dma("gpsimd",
                          out=ofm[0][bb, ic, off:off + 64,
                                     qt * 128:(qt + 1) * 128],
                          in_=osb[:], reads=[b_osb], writes=[ofm[1]],
                          slot=b_osb)

                for qt in range(S // 128):
                    rr = qt % 2
                    qt_iter(qt, ssbR[rr][0], ssbR[rr][1], pbfR[rr][0],
                            pbfR[rr][1], ptbR[rr][0], ptbR[rr][1],
                            osbR[rr][0], osbR[rr][1], sttR[rr][0],
                            sttR[rr][1], psOR[rr][0], psOR[rr][1])
    P.barrier()

    with ExitStack() as es:
        owT, b_owT = prep_wT(C, G, "l0a_owT", C.W["L0_attn_out_w"], E, E, es)
        ob_bc, b_ob_bc = bcast_row(C, G, "l0a_ob", C.W["L0_attn_out_b"], E, es)
        poxb, b_poxb = C.sb("l0a_po_x", [128, EC, S], BF16, es=es)
        yst, b_yst = [list(z) for z in zip(C.sb("l0a_po_y0", [128, 512], F32, es=es),
                                           C.sb("l0a_po_y1", [128, 512], F32, es=es))]
        pops, b_pops = [list(z) for z in zip(C.ps("l0a_po_p0", [128, 512], F32, es=es),
                                             C.ps("l0a_po_p1", [128, 512], F32, es=es))]
        for bb in range(B):
            P.dma("sync", out=poxb[:], in_=ofm[0][bb].rearrange("c p t -> p c t"),
                  reads=[ofm[1]], writes=[b_poxb], slot=b_poxb)
            for tt in range(S // 128):
                for ds in range(E // 512):
                    j = (tt * (E // 512) + ds) % 2

                    def mm5(t, tt=tt, ds=ds, j=j, pops=pops, poxb=poxb, owT=owT):
                        last = None
                        for ic in range(EC):
                            last = t.matmul(
                                pops[j][:], poxb[:, ic, tt * 128:(tt + 1) * 128],
                                owT[:, ic, ds * 512:(ds + 1) * 512],
                                start=(ic == 0), stop=(ic == EC - 1))
                        return last
                    P.op("tensor", mm5, reads=[b_owT, b_poxb], writes=[b_pops[j]])
                    P.op("vector", lambda v, j=j, ds=ds, yst=yst, pops=pops, ob_bc=ob_bc: v.tensor_add(
                        yst[j][:], pops[j][:], ob_bc[:, ds * 512:(ds + 1) * 512]),
                        reads=[b_pops[j], b_ob_bc], writes=[b_yst[j]])
                    P.dma("gpsimd",
                          out=y[0][bb * S + tt * 128: bb * S + (tt + 1) * 128,
                                   ds * 512:(ds + 1) * 512],
                          in_=yst[j][:], reads=[b_yst[j]], writes=[y[1]],
                          slot=b_yst[j])
    P.barrier()
    stage_post_ln_add(C, G, y[0], y[1], resid, b_resid)
    P.barrier()



def col_vec(C, G, name, vec_ap, n, es):
    P = C.P
    nchunks = n // 128
    rw, b_rw = C.sb(name + "_r", [1, n], F32, es=es)
    out, b_out = C.sb(name, [128, nchunks], F32, es=es)
    pst, b_pst = C.ps(name + "_ps", [128, nchunks], F32, es=es)
    P.dma("sync", out=rw[:], in_=vec_ap.unsqueeze(0) if vec_ap.ndim == 1 else vec_ap,
          writes=[b_rw], slot=b_rw)

    def tr(t):
        last = None
        for c in range(nchunks):
            last = t.transpose(pst[:, c:c + 1], rw[0:1, c * 128:(c + 1) * 128],
                               G.identf[0:1, 0:1])
        return last
    P.op("tensor", tr, reads=[b_rw, G.identf_b], writes=[b_pst])
    P.op("vector", lambda v: v.tensor_copy(out[:], pst[:]),
         reads=[b_pst], writes=[b_out])
    return out, b_out


def emit_L3_attn(C, G, resid, b_resid):
    P = C.P
    H, HD = 16, 64
    xnfm = C.dram("l3a_xnfm", (B, EC, 128, S), BF16)
    qfm = C.dram("l3a_q", (B, EC, 128, S), BF16)
    kfm = C.dram("l3a_k", (B, EC, 128, S), BF16)
    vtm = C.dram("l3a_v", (B, S, E), BF16)
    ofm = C.dram("l3a_o", (B, EC, 128, S), BF16)
    y = C.dram("l3a_y", (T, E), F32)

    stage_prep_xn(C, G, resid, b_resid, want_fm=xnfm)
    P.barrier()

    with ExitStack() as es:
        qwT, b_qwT = prep_wT(C, G, "l3a_qwT", C.W["L3_attn_Wq"], E, E, es)
        kwT, b_kwT = prep_wT(C, G, "l3a_kwT", C.W["L3_attn_Wk"], E, E, es)
        vwT, b_vwT = prep_wT(C, G, "l3a_vwT", C.W["L3_attn_Wv"], E, E, es)
        bqc, b_bqc = col_vec(C, G, "l3a_bq", C.W["L3_attn_bq"], E, es)
        bkc, b_bkc = col_vec(C, G, "l3a_bk", C.W["L3_attn_bk"], E, es)
        bv_bc, b_bv_bc = bcast_row(C, G, "l3a_bv", C.W["L3_attn_bv"], E, es)
        xsb, b_xsb = C.sb("l3a_xsb", [128, EC, S], BF16, es=es)
        stg, b_stg = [list(z) for z in zip(C.sb("l3a_st0", [128, 512], BF16, es=es),
                                           C.sb("l3a_st1", [128, 512], BF16, es=es))]
        psq, b_psq = [list(z) for z in zip(C.ps("l3a_ps0", [128, 512], F32, es=es),
                                           C.ps("l3a_ps1", [128, 512], F32, es=es))]
        for bb in range(B):
            P.dma("sync", out=xsb[:],
                  in_=xnfm[0][bb].rearrange("c p t -> p c t"),
                  reads=[xnfm[1]], writes=[b_xsb], slot=b_xsb)
            for which, wt, b_wt, bc, b_bc, dst in (
                    ("q", qwT, b_qwT, bqc, b_bqc, qfm),
                    ("k", kwT, b_kwT, bkc, b_bkc, kfm)):
                for ic in range(EC):
                    for ts in range(S // 512):
                        j = (ic * (S // 512) + ts) % 2

                        def mm(t, ic=ic, ts=ts, j=j, wt=wt, psq=psq, xsb=xsb):
                            last = None
                            for ec in range(EC):
                                last = t.matmul(
                                    psq[j][:],
                                    wt[:, ec, ic * 128:(ic + 1) * 128],
                                    xsb[:, ec, ts * 512:(ts + 1) * 512],
                                    start=(ec == 0), stop=(ec == EC - 1))
                            return last
                        P.op("tensor", mm, reads=[b_wt, b_xsb], writes=[b_psq[j]])
                        P.op("vector", lambda v, j=j, ic=ic, stg=stg, psq=psq,
                             bc=bc: v.tensor_scalar_add(
                                 stg[j][:], psq[j][:], bc[:, ic:ic + 1]),
                             reads=[b_psq[j], b_bc], writes=[b_stg[j]])
                        P.dma("gpsimd",
                              out=dst[0][bb, ic, :, ts * 512:(ts + 1) * 512],
                              in_=stg[j][:], reads=[b_stg[j]], writes=[dst[1]],
                              slot=b_stg[j])
            for tt in range(S // 128):
                for ds in range(E // 512):
                    j = (tt * (E // 512) + ds) % 2

                    def mmv(t, tt=tt, ds=ds, j=j, psq=psq, xsb=xsb, vwT=vwT):
                        last = None
                        for ec in range(EC):
                            last = t.matmul(
                                psq[j][:],
                                xsb[:, ec, tt * 128:(tt + 1) * 128],
                                vwT[:, ec, ds * 512:(ds + 1) * 512],
                                start=(ec == 0), stop=(ec == EC - 1))
                        return last
                    P.op("tensor", mmv, reads=[b_vwT, b_xsb], writes=[b_psq[j]])
                    P.op("vector", lambda v, j=j, ds=ds, stg=stg, psq=psq:
                         v.tensor_add(stg[j][:], psq[j][:],
                                      bv_bc[:, ds * 512:(ds + 1) * 512]),
                         reads=[b_psq[j], b_bv_bc], writes=[b_stg[j]])
                    P.dma("gpsimd",
                          out=vtm[0][bb, tt * 128:(tt + 1) * 128,
                                     ds * 512:(ds + 1) * 512],
                          in_=stg[j][:], reads=[b_stg[j]], writes=[vtm[1]],
                          slot=b_stg[j])
    P.barrier()

    with ExitStack() as es:
        qsb, b_qsb = C.sb("l3a_at_q", [64, S], BF16, es=es)
        ksb, b_ksb = C.sb("l3a_at_k", [64, S], BF16, es=es)
        vsb, b_vsb = C.sb("l3a_at_v", [128, S // 128, HD], BF16, es=es)
        ssbR = [C.sb("l3a_at_s", [128, 512], F32, es=es) for _ in range(2)]
        pbfR = [C.sb("l3a_at_p", [128, 512], BF16, es=es) for _ in range(2)]
        ptbR = [C.sb("l3a_at_pt", [128, 4, 128], BF16, es=es) for _ in range(2)]
        osbR = [C.sb("l3a_at_o", [64, 128], BF16, es=es) for _ in range(2)]
        sttR = [C.sb("l3a_at_st", [128, 8], F32, es=es) for _ in range(2)]
        msk, b_msk = C.sb("l3a_msk", [128, 512], F32, es=es)
        psSR = [C.ps("l3a_at_psS", [128, 512], F32, es=es) for _ in range(2)]
        psTR = [C.ps("l3a_at_psT", [128, 512], BF16, es=es) for _ in range(2)]
        psOR = [C.ps("l3a_at_psO", [64, 128], F32, es=es) for _ in range(2)]

        def mkmask(g):
            g.iota(msk[:], pattern=[[1, 512]], base=-384, channel_multiplier=-1,
                   allow_small_or_imprecise_dtypes=True)
            return g.affine_select(out=msk[:], in_=msk[:], compare_op=ALU.is_ge,
                                   fill=-FMAX, base=384,
                                   pattern=[[-1, 512]], channel_multiplier=1)
        P.op("gpsimd", mkmask, writes=[b_msk], flush=True)

        NQ = S // 128
        for bb in range(B):
            for h in range(H):
                ic, off = h // 2, (h % 2) * 64
                P.dma("sync", out=qsb[:], in_=qfm[0][bb, ic, off:off + 64, :],
                      reads=[qfm[1]], writes=[b_qsb], slot=b_qsb)
                P.dma("sync", out=ksb[:], in_=kfm[0][bb, ic, off:off + 64, :],
                      reads=[kfm[1]], writes=[b_ksb], slot=b_ksb)
                P.dma("sync", out=vsb[:],
                      in_=vtm[0][bb, :, h * 64:(h + 1) * 64].rearrange(
                          "(c p) d -> p c d", p=128),
                      reads=[vtm[1]], writes=[b_vsb], slot=b_vsb)
                def qt_iter3(qt, ssb, b_ssb, pbf, b_pbf, ptb, b_ptb, osb,
                             b_osb, stt, b_stt, psS, b_psS, psT, b_psT, psO,
                             b_psO):
                    nk = min(qt + 1, 4)
                    k0 = qt + 1 - nk
                    w = nk * 128
                    moff = 512 - w
                    P.op("tensor", lambda t, qt=qt, k0=k0, w=w:
                         t.matmul(psS[:, 0:w],
                                  qsb[0:64, qt * 128:(qt + 1) * 128],
                                  ksb[0:64, k0 * 128:k0 * 128 + w],
                                  start=True, stop=True),
                         reads=[b_qsb, b_ksb], writes=[b_psS])
                    P.op("vector", lambda v, w=w, moff=moff:
                         v.scalar_tensor_tensor(ssb[:, 0:w], psS[:, 0:w],
                                                0.125, msk[:, moff:moff + w],
                                                ALU.mult, ALU.add),
                         reads=[b_psS, b_msk], writes=[b_ssb])
                    P.op("vector", lambda v, w=w: v.reduce_max(
                        stt[:, 0:1], ssb[:, 0:w], axis=AX.X),
                        reads=[b_ssb], writes=[b_stt], flush=True)
                    P.op("vector", lambda v: v.tensor_scalar_mul(
                        stt[:, 1:2], stt[:, 0:1], -1.0),
                        reads=[b_stt], writes=[b_stt], flush=True)

                    def expacc(s, w=w):
                        s.activation(pbf[:, 0:w], ssb[:, 0:w], AF.Exp,
                                     bias=stt[:, 1:2], accum_out=stt[:, 2:3])
                        s.drain()
                        return s.activation(stt[:, 3:4], stt[:, 2:3], AF.Ln)
                    P.op("scalar", expacc, reads=[b_ssb, b_stt],
                         writes=[b_pbf, b_stt], flush=True)
                    P.op("vector", lambda v: v.tensor_sub(
                        stt[:, 1:2], stt[:, 1:2], stt[:, 3:4]),
                        reads=[b_stt], writes=[b_stt], flush=True)
                    P.op("scalar", lambda s, w=w: s.activation(
                        pbf[:, 0:w], ssb[:, 0:w], AF.Exp, bias=stt[:, 1:2]),
                        reads=[b_ssb, b_stt], writes=[b_pbf])

                    def trp(t, nk=nk):
                        last = None
                        for c in range(nk):
                            last = t.transpose(
                                psT[:, c * 128:(c + 1) * 128],
                                pbf[:, c * 128:(c + 1) * 128], G.identb[:])
                        return last
                    P.op("tensor", trp, reads=[b_pbf, G.identb_b],
                         writes=[b_psT])
                    P.op("vector", lambda v, nk=nk: v.tensor_copy(
                        ptb[:, 0:nk, :], psT[:, 0:nk * 128]),
                        reads=[b_psT], writes=[b_ptb])

                    def pv(t, nk=nk, k0=k0):
                        last = None
                        for c in range(nk):
                            last = t.matmul(psO[:], vsb[:, k0 + c, :],
                                            ptb[:, c, :], start=(c == 0),
                                            stop=(c == nk - 1))
                        return last
                    P.op("tensor", pv, reads=[b_vsb, b_ptb], writes=[b_psO])
                    P.op("vector", lambda v: v.tensor_copy(osb[0:64, :],
                                                           psO[0:64, :]),
                         reads=[b_psO], writes=[b_osb])
                    P.dma("gpsimd",
                          out=ofm[0][bb, ic, off:off + 64,
                                     qt * 128:(qt + 1) * 128],
                          in_=osb[:], reads=[b_osb], writes=[ofm[1]],
                          slot=b_osb)

                for qt in range(NQ):
                    rr = qt % 2
                    qt_iter3(qt, ssbR[rr][0], ssbR[rr][1], pbfR[rr][0],
                             pbfR[rr][1], ptbR[rr][0], ptbR[rr][1],
                             osbR[rr][0], osbR[rr][1], sttR[rr][0],
                             sttR[rr][1], psSR[rr][0], psSR[rr][1],
                             psTR[rr][0], psTR[rr][1], psOR[rr][0],
                             psOR[rr][1])
    P.barrier()

    with ExitStack() as es:
        owT, b_owT = prep_wT(C, G, "l3a_owT", C.W["L3_attn_Wo"], E, E, es)
        ob_bc, b_ob_bc = bcast_row(C, G, "l3a_ob", C.W["L3_attn_bo"], E, es)
        poxb, b_poxb = C.sb("l3a_po_x", [128, EC, S], BF16, es=es)
        yst, b_yst = [list(z) for z in zip(C.sb("l3a_po_y0", [128, 512], F32, es=es),
                                           C.sb("l3a_po_y1", [128, 512], F32, es=es))]
        pops, b_pops = [list(z) for z in zip(C.ps("l3a_po_p0", [128, 512], F32, es=es),
                                             C.ps("l3a_po_p1", [128, 512], F32, es=es))]
        for bb in range(B):
            P.dma("sync", out=poxb[:], in_=ofm[0][bb].rearrange("c p t -> p c t"),
                  reads=[ofm[1]], writes=[b_poxb], slot=b_poxb)
            for tt in range(S // 128):
                for ds in range(E // 512):
                    j = (tt * (E // 512) + ds) % 2

                    def mm5(t, tt=tt, ds=ds, j=j, pops=pops, poxb=poxb, owT=owT):
                        last = None
                        for icc in range(EC):
                            last = t.matmul(
                                pops[j][:], poxb[:, icc, tt * 128:(tt + 1) * 128],
                                owT[:, icc, ds * 512:(ds + 1) * 512],
                                start=(icc == 0), stop=(icc == EC - 1))
                        return last
                    P.op("tensor", mm5, reads=[b_owT, b_poxb], writes=[b_pops[j]])
                    P.op("vector", lambda v, j=j, ds=ds, yst=yst, pops=pops,
                         ob_bc=ob_bc: v.tensor_add(
                             yst[j][:], pops[j][:],
                             ob_bc[:, ds * 512:(ds + 1) * 512]),
                         reads=[b_pops[j], b_ob_bc], writes=[b_yst[j]])
                    P.dma("gpsimd",
                          out=y[0][bb * S + tt * 128: bb * S + (tt + 1) * 128,
                                   ds * 512:(ds + 1) * 512],
                          in_=yst[j][:], reads=[b_yst[j]], writes=[y[1]],
                          slot=b_yst[j])
    P.barrier()
    stage_post_ln_add(C, G, y[0], y[1], resid, b_resid)
    P.barrier()



def emit_L2_ffn(C, G, resid, b_resid):
    P = C.P
    NE, HH = 8, 1408
    xnfm = C.dram("l2f_xnfm", (B, EC, 128, S), BF16)
    xnfm32 = C.dram("l2f_xnfm32", (B, EC, 128, S), F32)
    y = C.dram("l2f_y", (T, E), F32)

    stage_prep_xn(C, G, resid, b_resid, want_fm=xnfm, want_fm32=xnfm32)
    P.barrier()

    NST = 8
    TPS = 4
    with ExitStack() as es:
        wg, b_wg = C.sb("l2f_wg", [128, EC, NE], F32, es=es)
        P.dma("sync", out=wg[:], in_=C.W["L2_ffn_Wg"].rearrange(
            "(c p) e -> p c e", p=128), writes=[b_wg], slot=b_wg)
        negbig, b_negbig = C.sb("l2f_nb", [128, NE], F32, es=es)
        P.op("gpsimd", lambda g: g.memset(negbig[:], -1e30), writes=[b_negbig])

        xt, b_xt = [list(z) for z in zip(
            C.sb("l2f_x0", [128, EC, 512], BF16, es=es),
            C.sb("l2f_x1", [128, EC, 512], BF16, es=es))]
        xt32, b_xt32 = C.sb("l2f_x32", [128, EC, 512], F32, es=es)
        wmat, b_wmat = C.sb("l2f_wmat", [128, TPS, NE], F32, es=es)
        rt, b_rt = C.sb("l2f_rt", [128, 8], F32, es=es)
        oh1, b_oh1 = C.sb("l2f_oh1", [128, 8], F32, es=es)
        oh2, b_oh2 = C.sb("l2f_oh2", [128, 8], F32, es=es)
        m8, b_m8 = C.sb("l2f_m8", [128, 8], F32, es=es)
        sstat, b_sstat = C.sb("l2f_ss", [128, 8], F32, es=es)
        yacc, b_yacc = C.sb("l2f_yacc", [128, TPS, E], F32, es=es)
        hbf, b_hbf = C.sb("l2f_hbf", [128, HH], BF16, es=es)
        hfm, b_hfm = C.sb("l2f_hfm", [128, 11, 128], BF16, es=es)
        silu_t, b_silu = C.sb("l2f_silu", [128, 512], F32, es=es)
        ew = []
        for sl in range(2):
            w1, bw1 = C.sb(f"l2f_w1_{sl}", [128, EC, HH], BF16, es=es)
            w2, bw2 = C.sb(f"l2f_w2_{sl}", [128, EC, HH], BF16, es=es)
            w3, bw3 = C.sb(f"l2f_w3_{sl}", [128, 11, E], BF16, es=es)
            ew.append((w1, bw1, w2, bw2, w3, bw3))
        psr, b_psr = C.ps("l2f_psr", [128, NE], F32, es=es)
        psg, b_psg = C.ps("l2f_psg", [128, 512], F32, es=es)
        psu, b_psu = C.ps("l2f_psu", [128, 512], F32, es=es)
        psT, b_psT = C.ps("l2f_psT", [128, 1024], BF16, es=es)
        psy, b_psy = [list(z) for z in zip(C.ps("l2f_py0", [128, 512], F32, es=es),
                                           C.ps("l2f_py1", [128, 512], F32, es=es))]

        def wsrc(e):
            if e < NE:
                return (C.W["L2_ffn_W1"][e], C.W["L2_ffn_W2"][e],
                        C.W["L2_ffn_W3"][e])
            h0 = (e - NE) * HH
            return (C.W["L2_ffn_Ws1"][:, h0:h0 + HH],
                    C.W["L2_ffn_Ws2"][:, h0:h0 + HH],
                    C.W["L2_ffn_Ws3"][h0:h0 + HH, :])

        def load_expert(e, sl):
            w1, bw1, w2, bw2, w3, bw3 = ew[sl]
            a1, a2, a3 = wsrc(e)
            P.dma("gpsimd", out=w1[:], in_=a1.rearrange("(c p) h -> p c h", p=128),
                  writes=[bw1], slot=bw1)
            P.dma("gpsimd", out=w2[:], in_=a2.rearrange("(c p) h -> p c h", p=128),
                  writes=[bw2], slot=bw2)
            P.dma("gpsimd", out=w3[:], in_=a3.rearrange("(c p) o -> p c o", p=128),
                  writes=[bw3], slot=bw3)

        for st in range(NST):
            stoff = st * 512
            jx = st % 2
            bb, t0 = (stoff // S), (stoff % S)
            P.dma("sync", out=xt[jx][:],
                  in_=xnfm[0][bb, :, :, t0:t0 + 512].rearrange("c p t -> p c t"),
                  reads=[xnfm[1]], writes=[b_xt[jx]], slot=b_xt[jx])
            P.dma("gpsimd", out=xt32[:],
                  in_=xnfm32[0][bb, :, :, t0:t0 + 512].rearrange("c p t -> p c t"),
                  reads=[xnfm32[1]], writes=[b_xt32], slot=b_xt32)
            P.op("gpsimd", lambda g: g.memset(yacc[:], 0.0),
                 reads=[], writes=[b_yacc])
            for tt in range(TPS):
                def mmr(t, tt=tt):
                    last = None
                    for ec in range(EC):
                        last = t.matmul(psr[:], xt32[:, ec, tt * 128:(tt + 1) * 128],
                                        wg[:, ec, :], start=(ec == 0),
                                        stop=(ec == EC - 1))
                    return last
                P.op("tensor", mmr, reads=[b_xt32, b_wg], writes=[b_psr])
                P.op("vector", lambda v: v.tensor_copy(rt[:], psr[:]),
                     reads=[b_psr], writes=[b_rt], flush=True)
                P.op("vector", lambda v: v.max(out=m8[:], in_=rt[:]),
                     reads=[b_rt], writes=[b_m8], flush=True)
                P.op("vector", lambda v: v.tensor_sub(sstat[:, 0:1], m8[:, 0:1],
                                                      m8[:, 1:2]),
                     reads=[b_m8], writes=[b_sstat], flush=True)

                def sig(s):
                    s.activation(sstat[:, 1:2], sstat[:, 0:1], AF.Sigmoid)
                    return s.activation(sstat[:, 2:3], sstat[:, 0:1], AF.Sigmoid,
                                        scale=-1.0)
                P.op("scalar", sig, reads=[b_sstat], writes=[b_sstat], flush=True)
                P.op("vector", lambda v, oh1=oh1: v.tensor_scalar(
                    oh1[:], rt[:], m8[:, 0:1], None, ALU.is_equal),
                    reads=[b_rt, b_m8], writes=[b_oh1], flush=True)
                P.op("vector", lambda v, oh1=oh1, oh2=oh2: v.scalar_tensor_tensor(
                    oh2[:], oh1[:], -1e30, rt[:], ALU.mult, ALU.add),
                    reads=[b_oh1, b_rt], writes=[b_oh2], flush=True)
                P.op("vector", lambda v, oh2=oh2: v.tensor_scalar(
                    oh2[:], oh2[:], m8[:, 1:2], None, ALU.is_equal),
                    reads=[b_oh2, b_m8], writes=[b_oh2], flush=True)
                P.op("vector", lambda v, tt=tt, oh1=oh1: v.tensor_scalar(
                    wmat[:, tt, :], oh1[:], sstat[:, 1:2], None, ALU.mult),
                    reads=[b_oh1, b_sstat], writes=[b_wmat], flush=True)
                P.op("vector", lambda v, tt=tt, oh2=oh2: v.scalar_tensor_tensor(
                    wmat[:, tt, :], oh2[:], sstat[:, 2:3], wmat[:, tt, :],
                    ALU.mult, ALU.add),
                    reads=[b_oh2, b_sstat, b_wmat], writes=[b_wmat], flush=True)

            NSL = [(0, 512), (512, 512), (1024, 384)]
            for e in range(10):
                sl = e % 2
                if st == 0 and e == 0:
                    load_expert(0, 0)
                if e + 1 < 10:
                    load_expert(e + 1, (e + 1) % 2)
                elif st + 1 < NST:
                    pass
                w1, bw1, w2, bw2, w3, bw3 = ew[sl]
                for tt in range(TPS):
                    for (h0, hn) in NSL:
                        def mmg(t, tt=tt, h0=h0, hn=hn, jx=jx, w=w1):
                            last = None
                            for ec in range(EC):
                                last = t.matmul(
                                    psg[:, 0:hn],
                                    xt[jx][:, ec, tt * 128:(tt + 1) * 128],
                                    w[:, ec, h0:h0 + hn],
                                    start=(ec == 0), stop=(ec == EC - 1))
                            return last
                        P.op("tensor", mmg, reads=[b_xt[jx], bw1], writes=[b_psg])

                        def mmu(t, tt=tt, h0=h0, hn=hn, jx=jx, w=w2):
                            last = None
                            for ec in range(EC):
                                last = t.matmul(
                                    psu[:, 0:hn],
                                    xt[jx][:, ec, tt * 128:(tt + 1) * 128],
                                    w[:, ec, h0:h0 + hn],
                                    start=(ec == 0), stop=(ec == EC - 1))
                            return last
                        P.op("tensor", mmu, reads=[b_xt[jx], bw2], writes=[b_psu])
                        P.op("scalar", lambda s, hn=hn: s.activation(
                            silu_t[:, 0:hn], psg[:, 0:hn], AF.Silu),
                            reads=[b_psg], writes=[b_silu])
                        if e < NE:
                            P.op("vector", lambda v, h0=h0, hn=hn, tt=tt, e=e:
                                 v.scalar_tensor_tensor(
                                     hbf[:, h0:h0 + hn], psu[:, 0:hn],
                                     wmat[:, tt, e:e + 1], silu_t[:, 0:hn],
                                     ALU.mult, ALU.mult),
                                 reads=[b_psu, b_wmat, b_silu], writes=[b_hbf])
                        else:
                            P.op("vector", lambda v, h0=h0, hn=hn:
                                 v.tensor_mul(hbf[:, h0:h0 + hn], psu[:, 0:hn],
                                              silu_t[:, 0:hn]),
                                 reads=[b_psu, b_silu], writes=[b_hbf])
                    def trh1(t):
                        last = None
                        for c in range(8):
                            last = t.transpose(psT[:, c * 128:(c + 1) * 128],
                                               hbf[:, c * 128:(c + 1) * 128],
                                               G.identb[:])
                        return last
                    P.op("tensor", trh1, reads=[b_hbf, G.identb_b], writes=[b_psT])
                    P.op("vector", lambda v: v.tensor_copy(hfm[:, 0:8, :], psT[:]),
                         reads=[b_psT], writes=[b_hfm])

                    def trh2(t):
                        last = None
                        for c in range(8, 11):
                            last = t.transpose(psT[:, (c - 8) * 128:(c - 7) * 128],
                                               hbf[:, c * 128:(c + 1) * 128],
                                               G.identb[:])
                        return last
                    P.op("tensor", trh2, reads=[b_hbf, G.identb_b], writes=[b_psT])
                    P.op("vector", lambda v: v.tensor_copy(hfm[:, 8:11, :],
                                                           psT[:, 0:384]),
                         reads=[b_psT], writes=[b_hfm])
                    for ds in range(2):
                        def mmd(t, ds=ds, w=w3):
                            last = None
                            for hc in range(11):
                                last = t.matmul(
                                    psy[ds][:], hfm[:, hc, :],
                                    w[:, hc, ds * 512:(ds + 1) * 512],
                                    start=(hc == 0), stop=(hc == 10))
                            return last
                        P.op("tensor", mmd, reads=[b_hfm, bw3], writes=[b_psy[ds]])
                        P.op("vector", lambda v, tt=tt, ds=ds: v.tensor_add(
                            yacc[:, tt, ds * 512:(ds + 1) * 512],
                            yacc[:, tt, ds * 512:(ds + 1) * 512], psy[ds][:]),
                            reads=[b_psy[ds], b_yacc], writes=[b_yacc])
            if st + 1 < NST:
                load_expert(0, 0)
            for tt in range(TPS):
                P.dma("gpsimd",
                      out=y[0][stoff + tt * 128: stoff + (tt + 1) * 128, :],
                      in_=yacc[:, tt, :], reads=[b_yacc], writes=[y[1]],
                      slot=b_yacc)
    P.barrier()
    stage_post_ln_add(C, G, y[0], y[1], resid, b_resid)
    P.barrier()



def emit_L1_ffn(C, G, resid, b_resid):
    P = C.P
    NE, HR = 16, 256
    xnfm = C.dram("l1f_xnfm", (B, EC, 128, S), BF16)
    xnfm32 = C.dram("l1f_xnfm32", (B, EC, 128, S), F32)
    y = C.dram("l1f_y", (T, E), F32)

    stage_prep_xn(C, G, resid, b_resid, want_fm=xnfm, want_fm32=xnfm32)
    P.barrier()

    NST = 8
    TPS = 4
    with ExitStack() as es:
        with ExitStack() as es_tmp:
            rwT, b_rwT = prep_wT(C, G, "l1f_rwT", C.W["L1_ffn_router_w"], E,
                                 NE, es, dtype=F32, es_tmp=es_tmp)
            sgwT, b_sgwT = prep_wT(C, G, "l1f_sgwT",
                                   C.W["L1_ffn_shared_gate_w"], E, 2048, es,
                                   es_tmp=es_tmp)
            suwT, b_suwT = prep_wT(C, G, "l1f_suwT",
                                   C.W["L1_ffn_shared_up_w"], E, 2048, es,
                                   es_tmp=es_tmp)
            sdwT, b_sdwT = prep_wT(C, G, "l1f_sdwT",
                                   C.W["L1_ffn_shared_down_w"], 2048, E, es,
                                   es_tmp=es_tmp)
            cb_bc, b_cb_bc = bcast_row(C, G, "l1f_cb",
                                       C.W["L1_ffn_correction_bias"], NE, es,
                                       es_tmp=es_tmp)
        P.barrier()

        xt, b_xt = [list(z) for z in zip(
            C.sb("l1f_x0", [128, EC, 512], BF16, es=es),
            C.sb("l1f_x1", [128, EC, 512], BF16, es=es))]
        xt32, b_xt32 = C.sb("l1f_x32", [128, EC, 512], F32, es=es)
        cmb, b_cmb = C.sb("l1f_cmb", [128, TPS, NE], F32, es=es)
        rt, b_rt = C.sb("l1f_rt", [128, NE], F32, es=es)
        sb_, b_sb = C.sb("l1f_sb", [128, NE], F32, es=es)
        t1_, b_t1 = C.sb("l1f_t1", [128, NE], F32, es=es)
        t2_, b_t2 = C.sb("l1f_t2", [128, NE], F32, es=es)
        m8, b_m8 = C.sb("l1f_m8", [128, 8], F32, es=es)
        gstat, b_gstat = C.sb("l1f_gs", [128, 16], F32, es=es)
        yacc, b_yacc = C.sb("l1f_yacc", [128, TPS, E], F32, es=es)
        hbf, b_hbf = C.sb("l1f_hbf", [128, 1024], BF16, es=es)
        hfm, b_hfm = C.sb("l1f_hfm", [128, 8, 128], BF16, es=es)
        silu_t, b_silu = C.sb("l1f_silu", [128, 512], F32, es=es)
        ew = []
        for sl in range(2):
            w1, bw1 = C.sb(f"l1f_w1_{sl}", [128, EC, HR], BF16, es=es)
            w2, bw2 = C.sb(f"l1f_w2_{sl}", [128, EC, HR], BF16, es=es)
            w3, bw3 = C.sb(f"l1f_w3_{sl}", [128, 2, E], BF16, es=es)
            ew.append((w1, bw1, w2, bw2, w3, bw3))
        psr, b_psr = C.ps("l1f_psr", [128, NE], F32, es=es)
        psg, b_psg = C.ps("l1f_psg", [128, 512], F32, es=es)
        psu, b_psu = C.ps("l1f_psu", [128, 512], F32, es=es)
        psT, b_psT = C.ps("l1f_psT", [128, 1024], BF16, es=es)
        psy, b_psy = [list(z) for z in zip(C.ps("l1f_py0", [128, 512], F32, es=es),
                                           C.ps("l1f_py1", [128, 512], F32, es=es))]

        def load_expert(e, sl):
            if e >= NE:
                return
            w1, bw1, w2, bw2, w3, bw3 = ew[sl]
            P.dma("gpsimd", out=w1[:], in_=C.W["L1_ffn_gate_w"][e].rearrange(
                "(c p) h -> p c h", p=128), writes=[bw1], slot=bw1)
            P.dma("gpsimd", out=w2[:], in_=C.W["L1_ffn_up_w"][e].rearrange(
                "(c p) h -> p c h", p=128), writes=[bw2], slot=bw2)
            P.dma("gpsimd", out=w3[:], in_=C.W["L1_ffn_down_w"][e].rearrange(
                "(c p) o -> p c o", p=128), writes=[bw3], slot=bw3)

        for st in range(NST):
            stoff = st * 512
            jx = st % 2
            bb, t0 = (stoff // S), (stoff % S)
            P.dma("sync", out=xt[jx][:],
                  in_=xnfm[0][bb, :, :, t0:t0 + 512].rearrange("c p t -> p c t"),
                  reads=[xnfm[1]], writes=[b_xt[jx]], slot=b_xt[jx])
            P.dma("gpsimd", out=xt32[:],
                  in_=xnfm32[0][bb, :, :, t0:t0 + 512].rearrange("c p t -> p c t"),
                  reads=[xnfm32[1]], writes=[b_xt32], slot=b_xt32)
            P.op("gpsimd", lambda g: g.memset(yacc[:], 0.0), writes=[b_yacc])
            for tt in range(TPS):
                def mmr(t, tt=tt):
                    last = None
                    for ec in range(EC):
                        last = t.matmul(psr[:], xt32[:, ec, tt * 128:(tt + 1) * 128],
                                        rwT[:, ec, :], start=(ec == 0),
                                        stop=(ec == EC - 1))
                    return last
                P.op("tensor", mmr, reads=[b_xt32, b_rwT], writes=[b_psr])
                P.op("scalar", lambda s: s.activation(rt[:], psr[:], AF.Sigmoid),
                     reads=[b_psr], writes=[b_rt], flush=True)
                P.op("vector", lambda v: v.tensor_add(sb_[:], rt[:], cb_bc[:, 0:NE]),
                     reads=[b_rt, b_cb_bc], writes=[b_sb], flush=True)
                sb3 = sb_.rearrange("p (g j) -> p g j", g=4)
                P.op("vector", lambda v, sb3=sb3: v.tensor_reduce(
                    gstat[:, 0:4], sb3, axis=AX.X, op=ALU.max),
                    reads=[b_sb], writes=[b_gstat], flush=True)
                P.op("vector", lambda v, sb3=sb3: v.tensor_tensor(
                    t1_.rearrange("p (g j) -> p g j", g=4), sb3,
                    gstat[:, 0:4].unsqueeze(2).broadcast_to([128, 4, 4]),
                    ALU.is_equal),
                    reads=[b_sb, b_gstat], writes=[b_t1], flush=True)
                P.op("vector", lambda v: v.scalar_tensor_tensor(
                    t2_[:], t1_[:], -1e30, sb_[:], ALU.mult, ALU.add),
                    reads=[b_t1, b_sb], writes=[b_t2], flush=True)
                P.op("vector", lambda v: v.tensor_reduce(
                    gstat[:, 4:8], t2_.rearrange("p (g j) -> p g j", g=4),
                    axis=AX.X, op=ALU.max),
                    reads=[b_t2], writes=[b_gstat], flush=True)
                P.op("vector", lambda v: v.tensor_add(
                    gstat[:, 8:12], gstat[:, 0:4], gstat[:, 4:8]),
                    reads=[b_gstat], writes=[b_gstat], flush=True)
                P.op("vector", lambda v: v.reduce_max(
                    gstat[:, 12:13], gstat[:, 8:12], axis=AX.X),
                    reads=[b_gstat], writes=[b_gstat], flush=True)
                P.op("vector", lambda v: v.tensor_scalar(
                    gstat[:, 0:4], gstat[:, 8:12], gstat[:, 12:13], None,
                    ALU.is_equal),
                    reads=[b_gstat], writes=[b_gstat], flush=True)
                P.op("vector", lambda v: v.scalar_tensor_tensor(
                    gstat[:, 4:8], gstat[:, 0:4], -1e30, gstat[:, 8:12],
                    ALU.mult, ALU.add),
                    reads=[b_gstat], writes=[b_gstat], flush=True)
                P.op("vector", lambda v: v.reduce_max(
                    gstat[:, 13:14], gstat[:, 4:8], axis=AX.X),
                    reads=[b_gstat], writes=[b_gstat], flush=True)
                P.op("vector", lambda v: v.tensor_scalar(
                    gstat[:, 4:8], gstat[:, 4:8], gstat[:, 13:14], None,
                    ALU.is_equal),
                    reads=[b_gstat], writes=[b_gstat], flush=True)
                P.op("vector", lambda v: v.tensor_add(
                    gstat[:, 0:4], gstat[:, 0:4], gstat[:, 4:8]),
                    reads=[b_gstat], writes=[b_gstat], flush=True)
                P.op("vector", lambda v: v.tensor_tensor(
                    t1_.rearrange("p (g j) -> p g j", g=4),
                    sb_.rearrange("p (g j) -> p g j", g=4),
                    gstat[:, 0:4].unsqueeze(2).broadcast_to([128, 4, 4]),
                    ALU.mult),
                    reads=[b_sb, b_gstat], writes=[b_t1], flush=True)
                P.op("vector", lambda v: v.tensor_scalar(
                    gstat[:, 4:8], gstat[:, 0:4], 1.0, 1e30,
                    ALU.subtract, ALU.mult),
                    reads=[b_gstat], writes=[b_gstat], flush=True)
                P.op("vector", lambda v: v.tensor_tensor(
                    t2_.rearrange("p (g j) -> p g j", g=4),
                    t1_.rearrange("p (g j) -> p g j", g=4),
                    gstat[:, 4:8].unsqueeze(2).broadcast_to([128, 4, 4]),
                    ALU.add),
                    reads=[b_t1, b_gstat], writes=[b_t2], flush=True)
                P.op("vector", lambda v: v.max(out=m8[:], in_=t2_[:]),
                     reads=[b_t2], writes=[b_m8], flush=True)
                P.op("vector", lambda v: v.memset(m8[:, 4:8], -1e30),
                     reads=[], writes=[b_m8], flush=True)
                P.op("vector", lambda v: v.match_replace(
                    out=t1_[:], in_to_replace=m8[:], in_values=t2_[:],
                    imm_value=-1e30),
                    reads=[b_m8, b_t2], writes=[b_t1], flush=True)
                P.op("vector", lambda v: v.tensor_tensor(
                    t2_[:], t2_[:], t1_[:], ALU.not_equal),
                    reads=[b_t1, b_t2], writes=[b_t2], flush=True)
                P.op("vector", lambda v: v.tensor_mul(t1_[:], rt[:], t2_[:]),
                     reads=[b_rt, b_t2], writes=[b_t1], flush=True)
                P.op("vector", lambda v: v.reduce_sum(
                    gstat[:, 14:15], t1_[:], axis=AX.X),
                    reads=[b_t1], writes=[b_gstat], flush=True)
                P.op("vector", lambda v: v.tensor_scalar_add(
                    gstat[:, 15:16], gstat[:, 14:15], 1e-20),
                    reads=[b_gstat], writes=[b_gstat], flush=True)
                P.op("vector", lambda v: v.reciprocal(
                    gstat[:, 14:15], gstat[:, 15:16]),
                    reads=[b_gstat], writes=[b_gstat], flush=True)
                P.op("vector", lambda v, tt=tt: v.tensor_scalar(
                    cmb[:, tt, :], t1_[:], gstat[:, 14:15], None, ALU.mult),
                    reads=[b_t1, b_gstat], writes=[b_cmb], flush=True)

            cdump = C.dram("l1f_cmbdump", (NST, 128, TPS, NE), F32)
            P.dma("gpsimd", out=cdump[0][st], in_=cmb[:], reads=[b_cmb],
                  writes=[cdump[1]], slot=b_cmb)
            for e in L1F_RANGE:
                sl = e % 2
                if st == 0 and e == L1F_RANGE[0]:
                    load_expert(L1F_RANGE[0], L1F_RANGE[0] % 2)
                if e + 1 < 18:
                    load_expert(e + 1, (e + 1) % 2)
                w1, bw1, w2, bw2, w3, bw3 = ew[sl]
                routed = e < NE
                if routed:
                    slices = [(0, HR)]
                    nhc = 2
                else:
                    slices = [(0, 512), (512, 512)]
                    nhc = 8
                    p0 = (e - NE) * 1024
                for tt in range(TPS):
                    for (h0, hn) in slices:
                        def mmg(t, tt=tt, h0=h0, hn=hn, jx=jx, routed=routed,
                                w=w1, e=e, p0=(0 if routed else p0)):
                            last = None
                            for ec in range(EC):
                                rhs = (w[:, ec, h0:h0 + hn] if routed else
                                       sgwT[:, ec, p0 + h0:p0 + h0 + hn])
                                last = t.matmul(
                                    psg[:, 0:hn],
                                    xt[jx][:, ec, tt * 128:(tt + 1) * 128],
                                    rhs, start=(ec == 0), stop=(ec == EC - 1))
                            return last
                        P.op("tensor", mmg, reads=[b_xt[jx], bw1, b_sgwT],
                             writes=[b_psg])

                        def mmu(t, tt=tt, h0=h0, hn=hn, jx=jx, routed=routed,
                                w=w2, e=e, p0=(0 if routed else p0)):
                            last = None
                            for ec in range(EC):
                                rhs = (w[:, ec, h0:h0 + hn] if routed else
                                       suwT[:, ec, p0 + h0:p0 + h0 + hn])
                                last = t.matmul(
                                    psu[:, 0:hn],
                                    xt[jx][:, ec, tt * 128:(tt + 1) * 128],
                                    rhs, start=(ec == 0), stop=(ec == EC - 1))
                            return last
                        P.op("tensor", mmu, reads=[b_xt[jx], bw2, b_suwT],
                             writes=[b_psu])
                        P.op("scalar", lambda s, hn=hn: s.activation(
                            silu_t[:, 0:hn], psg[:, 0:hn], AF.Silu),
                            reads=[b_psg], writes=[b_silu])
                        if routed:
                            P.op("vector", lambda v, h0=h0, hn=hn, tt=tt, e=e:
                                 v.scalar_tensor_tensor(
                                     hbf[:, h0:h0 + hn], psu[:, 0:hn],
                                     cmb[:, tt, e:e + 1], silu_t[:, 0:hn],
                                     ALU.mult, ALU.mult),
                                 reads=[b_psu, b_cmb, b_silu], writes=[b_hbf])
                        else:
                            P.op("vector", lambda v, h0=h0, hn=hn:
                                 v.tensor_mul(hbf[:, h0:h0 + hn], psu[:, 0:hn],
                                              silu_t[:, 0:hn]),
                                 reads=[b_psu, b_silu], writes=[b_hbf])

                    def trh(t, nhc=nhc):
                        last = None
                        for c in range(nhc):
                            last = t.transpose(psT[:, c * 128:(c + 1) * 128],
                                               hbf[:, c * 128:(c + 1) * 128],
                                               G.identb[:])
                        return last
                    P.op("tensor", trh, reads=[b_hbf, G.identb_b], writes=[b_psT])
                    P.op("vector", lambda v, nhc=nhc: v.tensor_copy(
                        hfm[:, 0:nhc, :], psT[:, 0:nhc * 128]),
                        reads=[b_psT], writes=[b_hfm])
                    for ds in range(2):
                        def mmd(t, ds=ds, nhc=nhc, routed=routed, w=w3, e=e):
                            last = None
                            for hc in range(nhc):
                                rhs = (w[:, hc, ds * 512:(ds + 1) * 512]
                                       if routed else
                                       sdwT[:, (e - NE) * 8 + hc,
                                            ds * 512:(ds + 1) * 512])
                                last = t.matmul(psy[ds][:], hfm[:, hc, :],
                                                rhs, start=(hc == 0),
                                                stop=(hc == nhc - 1))
                            return last
                        P.op("tensor", mmd, reads=[b_hfm, bw3, b_sdwT],
                             writes=[b_psy[ds]])
                        P.op("vector", lambda v, tt=tt, ds=ds: v.tensor_add(
                            yacc[:, tt, ds * 512:(ds + 1) * 512],
                            yacc[:, tt, ds * 512:(ds + 1) * 512], psy[ds][:]),
                            reads=[b_psy[ds], b_yacc], writes=[b_yacc])
            if st + 1 < NST:
                load_expert(0, 0)
            for tt in range(TPS):
                P.dma("gpsimd",
                      out=y[0][stoff + tt * 128: stoff + (tt + 1) * 128, :],
                      in_=yacc[:, tt, :], reads=[b_yacc], writes=[y[1]],
                      slot=b_yacc)
    P.barrier()
    stage_post_ln_add(C, G, y[0], y[1], resid, b_resid)
    P.barrier()



def emit_L3_ffn(C, G, resid, b_resid):
    P = C.P
    NE, R = 8, 64
    xnfm = C.dram("l3f_xnfm", (B, EC, 128, S), BF16)
    xnfm32 = C.dram("l3f_xnfm32", (B, EC, 128, S), F32)
    y = C.dram("l3f_y", (T, E), F32)

    stage_prep_xn(C, G, resid, b_resid, want_fm=xnfm, want_fm32=xnfm32)
    P.barrier()

    with ExitStack() as es:
        with ExitStack() as es_tmp:
            bwT, b_bwT = prep_wT(C, G, "l3f_bwT", C.W["L3_ffn_base_W"], E, E,
                                 es, es_tmp=es_tmp)
            bb_bc, b_bb_bc = bcast_row(C, G, "l3f_bb", C.W["L3_ffn_base_b"],
                                       E, es, es_tmp=es_tmp)
        P.barrier()

        sg_, b_sg = C.sb("l3f_sig", [8, E], F32, es=es)
        mu_, b_mu = C.sb("l3f_mu", [8, E], F32, es=es)
        iv_, b_iv = C.sb("l3f_iv", [8, E], F32, es=es)
        muiv, b_muiv = C.sb("l3f_muiv", [8, E], F32, es=es)
        m2iv, b_m2iv = C.sb("l3f_m2iv", [8, E], F32, es=es)
        ivT, b_ivT = C.sb("l3f_ivT", [128, EC, 8], F32, es=es)
        muivT, b_muivT = C.sb("l3f_muivT", [128, EC, 8], F32, es=es)
        pstT, b_pstT = C.ps("l3f_pstT", [128, 64], F32, es=es)
        P.dma("sync", out=sg_[:], in_=C.W["L3_ffn_disc_sigma"], writes=[b_sg],
              slot=b_sg)
        P.dma("sync", out=mu_[:], in_=C.W["L3_ffn_disc_mu"], writes=[b_mu],
              slot=b_mu)

        def mkiv(v):
            v.tensor_mul(iv_[:], sg_[:], sg_[:])
            v.reciprocal(iv_[:], iv_[:])
            v.tensor_mul(muiv[:], mu_[:], iv_[:])
            return v.tensor_mul(m2iv[:], mu_[:], muiv[:])
        P.op("vector", mkiv, reads=[b_sg, b_mu], writes=[b_iv, b_muiv, b_m2iv],
             flush=True)
        t3c, b_t3c = C.sb("l3f_t3", [8, 2], F32, es=es)
        P.op("vector", lambda v: v.reduce_sum(t3c[:, 0:1], m2iv[:], axis=AX.X),
             reads=[b_m2iv], writes=[b_t3c], flush=True)

        def trv(t):
            last = None
            for c in range(EC):
                last = t.transpose(pstT[:, c * 8:(c + 1) * 8],
                                   iv_[0:8, c * 128:(c + 1) * 128],
                                   G.identf[0:8, 0:8])
            return last
        P.op("tensor", trv, reads=[b_iv, G.identf_b], writes=[b_pstT])
        P.op("vector", lambda v: v.tensor_copy(
            ivT.rearrange("p c e -> p (c e)"), pstT[:, 0:64]),
            reads=[b_pstT], writes=[b_ivT])

        def trv2(t):
            last = None
            for c in range(EC):
                last = t.transpose(pstT[:, c * 8:(c + 1) * 8],
                                   muiv[0:8, c * 128:(c + 1) * 128],
                                   G.identf[0:8, 0:8])
            return last
        P.op("tensor", trv2, reads=[b_muiv, G.identf_b], writes=[b_pstT])
        P.op("vector", lambda v: v.tensor_copy(
            muivT.rearrange("p c e -> p (c e)"), pstT[:, 0:64]),
            reads=[b_pstT], writes=[b_muivT])

        xc32, b_xc32 = C.sb("l3f_x32", [128, S], F32, es=es)
        sq, b_sq = C.sb("l3f_sq", [128, S], F32, es=es)
        m1c, b_m1c = C.sb("l3f_m1", [128, EC], F32, es=es)
        m2c, b_m2c = C.sb("l3f_m2", [128, EC], F32, es=es)
        lrow, b_lrow = C.sb("l3f_lrow", [1, 8], F32, es=es)
        selr, b_selr = C.sb("l3f_selr", [1, 2], F32, es=es)
        iotar, b_iotar = C.sb("l3f_iotar", [1, 8], F32, es=es)
        idxf, b_idxf = C.sb("l3f_idxf", [1, 1], F32, es=es)
        iotac, b_iotac = C.sb("l3f_iotac", [128, 1], F32, es=es)
        idxb, b_idxb = C.sb("l3f_idxb", [128, 2], F32, es=es)
        idxi, b_idxi = C.sb("l3f_idxi", [128, 10], I32, es=es)
        psL, b_psL = C.ps("l3f_psL", [8, 8], F32, es=es)
        psB, b_psB = C.ps("l3f_psB", [128, 8], F32, es=es)

        def mkio(g):
            g.iota(iotar[:], pattern=[[1, 8]], base=0, channel_multiplier=0,
                   allow_small_or_imprecise_dtypes=True)
            return g.iota(iotac[:], pattern=[[1, 1]], base=0,
                          channel_multiplier=1,
                          allow_small_or_imprecise_dtypes=True)
        P.op("gpsimd", mkio, writes=[b_iotar, b_iotac], flush=True)

        dwg, b_dwg = C.sb("l3f_dwg", [128, EC, R], F32, es=es)
        dwb, b_dwb = C.sb("l3f_dwb", [128, EC, R], BF16, es=es)
        uwg, b_uwg = C.sb("l3f_uwg", [64, E], F32, es=es)
        uwb, b_uwb = C.sb("l3f_uwb", [64, E], BF16, es=es)
        cwg, b_cwg = C.sb("l3f_cwg", [128, EC, E], F32, es=es)
        cwT, b_cwT = C.sb("l3f_cwT", [128, EC, E], BF16, es=es)
        cstg, b_cstg = C.sb("l3f_cstg", [128, E], BF16, es=es)
        rows3, b_rows3 = C.sb("l3f_rows", [2, 3 * E + R], F32, es=es)
        dbb, b_dbb = C.sb("l3f_dbb", [128, R], F32, es=es)
        ubb, b_ubb = C.sb("l3f_ubb", [128, E], F32, es=es)
        ybb, b_ybb = C.sb("l3f_ybb", [128, E], F32, es=es)
        psW, b_psW = C.ps("l3f_psW", [128, 1024], BF16, es=es)
        psh, b_psh = C.ps("l3f_psh", [128, R], F32, es=es)
        psa, b_psa = [list(z) for z in zip(C.ps("l3f_pa0", [128, 512], F32, es=es),
                                           C.ps("l3f_pa1", [128, 512], F32, es=es))]
        xt, b_xtile = C.sb("l3f_xt", [128, EC, 128], BF16, es=es)
        hh, b_hh = C.sb("l3f_hh", [128, R], BF16, es=es)
        hrelu, b_hrelu = C.sb("l3f_hrelu", [128, R], F32, es=es)
        hfmT, b_hfmT = C.sb("l3f_hfmT", [64, 128], BF16, es=es)
        aa, b_aa = C.sb("l3f_aa", [128, E], BF16, es=es)
        afm, b_afm = C.sb("l3f_afm", [128, EC, 128], BF16, es=es)
        yst, b_yst = C.sb("l3f_yst", [128, E], F32, es=es)
        psHT, b_psHT = C.ps("l3f_psHT", [64, 128], BF16, es=es)

        for bb in range(B):
            for c in range(EC):
                P.dma("sync", out=xc32[:], in_=xnfm32[0][bb, c],
                      reads=[xnfm32[1]], writes=[b_xc32], slot=b_xc32)
                P.op("vector", lambda v, c=c: v.reduce_sum(
                    m1c[:, c:c + 1], xc32[:], axis=AX.X),
                    reads=[b_xc32], writes=[b_m1c], flush=True)

                def sqa(s, c=c):
                    s.activation(sq[:], xc32[:], AF.Square,
                                 accum_out=m2c[:, c:c + 1])
                    s.drain()
                    return s.mul(m2c[:, c:c + 1], m2c[:, c:c + 1], 1.0 / S)
                P.op("scalar", sqa, reads=[b_xc32], writes=[b_sq, b_m2c],
                     flush=True)
            P.op("vector", lambda v: v.tensor_scalar(
                m1c[:], m1c[:], 1.0 / S, -2.0, ALU.mult, ALU.mult),
                reads=[b_m1c], writes=[b_m1c], flush=True)

            def mml(t):
                last = None
                for c in range(EC):
                    t.matmul(psL[:, 0:1], ivT[:, c, :], m2c[:, c:c + 1],
                             start=(c == 0), stop=False)
                for c in range(EC):
                    last = t.matmul(psL[:, 0:1], muivT[:, c, :],
                                    m1c[:, c:c + 1], start=False,
                                    stop=(c == EC - 1))
                return last
            P.op("tensor", mml, reads=[b_ivT, b_muivT, b_m1c, b_m2c],
                 writes=[b_psL])
            P.op("vector", lambda v: v.tensor_add(t3c[:, 1:2], psL[:, 0:1],
                                                  t3c[:, 0:1]),
                 reads=[b_psL, b_t3c], writes=[b_t3c], flush=True)
            P.op("tensor", lambda t: t.transpose(psL[0:1, 0:8], t3c[0:8, 1:2],
                                                 G.identf[0:8, 0:8]),
                 reads=[b_t3c, G.identf_b], writes=[b_psL])
            P.op("vector", lambda v: v.tensor_copy(lrow[:], psL[0:1, 0:8]),
                 reads=[b_psL], writes=[b_lrow], flush=True)
            P.op("vector", lambda v: v.tensor_reduce(
                selr[0:1, 0:1], lrow[:], axis=AX.X, op=ALU.min),
                reads=[b_lrow], writes=[b_selr], flush=True)
            P.op("vector", lambda v: v.tensor_scalar(
                lrow[:], lrow[:], selr[0:1, 0:1], None, ALU.is_equal),
                reads=[b_lrow, b_selr], writes=[b_lrow], flush=True)

            def mkidx(v):
                v.tensor_mul(lrow[:], lrow[:], iotar[:])
                v.drain()
                return v.reduce_sum(idxf[:], lrow[:], axis=AX.X)
            P.op("vector", mkidx, reads=[b_lrow, b_iotar],
                 writes=[b_lrow, b_idxf], flush=True)
            P.op("tensor", lambda t: t.matmul(psB[:, 0:1], G.onesrow[0:1, :],
                                              idxf[0:1, 0:1], start=True,
                                              stop=True),
                 reads=[b_idxf, G.onesrow_b], writes=[b_psB])

            def mkib(v):
                v.scalar_tensor_tensor(idxb[:, 0:1], psB[:, 0:1], 1024.0,
                                       iotac[:], ALU.mult, ALU.add)
                return v.scalar_tensor_tensor(idxb[:, 1:2], psB[:, 0:1], 64.0,
                                              iotac[:], ALU.mult, ALU.add)
            P.op("vector", mkib, reads=[b_psB, b_iotac], writes=[b_idxb],
                 flush=True)

            def mkidxi(v):
                for c in range(EC):
                    v.tensor_scalar_add(idxi[:, c:c + 1], idxb[:, 0:1],
                                        float(c * 128))
                v.tensor_copy(idxi[:, 8:9], idxb[:, 1:2])
                return v.tensor_copy(idxi[:, 9:10], psB[:, 0:1])
            P.op("vector", mkidxi, reads=[b_idxb, b_psB], writes=[b_idxi],
                 flush=True)

            dwf = C.W["L3_ffn_down_W"].rearrange("e d r -> (e d) r")
            cwf = C.W["L3_ffn_conv_W"].rearrange("e o d -> (e o) d")
            uwf = C.W["L3_ffn_up_W"].rearrange("e r d -> (e r) d")
            for c in range(EC):
                P.dma("gpsimd", raw=(lambda g, c=c, dwf=dwf: g.indirect_dma_start(
                    dwg[:, c, :], None, dwf,
                    IndirectOffsetOnAxis(idxi[:, c:c + 1], axis=0))),
                    reads=[b_idxi], writes=[b_dwg], slot=b_dwg)
                P.dma("gpsimd", raw=(lambda g, c=c, cwf=cwf: g.indirect_dma_start(
                    cwg[:, c, :], None, cwf,
                    IndirectOffsetOnAxis(idxi[:, c:c + 1], axis=0))),
                    reads=[b_idxi], writes=[b_cwg], slot=b_cwg)
            P.dma("gpsimd", raw=(lambda g, uwf=uwf: g.indirect_dma_start(
                uwg[:, :], None, uwf,
                IndirectOffsetOnAxis(idxi[0:64, 8:9], axis=0))),
                reads=[b_idxi], writes=[b_uwg], slot=b_uwg)
            P.dma("gpsimd", raw=(lambda g: g.indirect_dma_start(
                rows3[0:2, 0:R], None, C.W["L3_ffn_down_b"],
                IndirectOffsetOnAxis(idxi[0:2, 9:10], axis=0))),
                reads=[b_idxi], writes=[b_rows3], slot=b_rows3)
            P.dma("gpsimd", raw=(lambda g: g.indirect_dma_start(
                rows3[0:2, R:R + E], None, C.W["L3_ffn_up_b"],
                IndirectOffsetOnAxis(idxi[0:2, 9:10], axis=0))),
                reads=[b_idxi], writes=[b_rows3], slot=b_rows3)
            P.dma("gpsimd", raw=(lambda g: g.indirect_dma_start(
                rows3[0:2, R + E:R + 2 * E], None, C.W["L3_ffn_conv_b"],
                IndirectOffsetOnAxis(idxi[0:2, 9:10], axis=0))),
                reads=[b_idxi], writes=[b_rows3], slot=b_rows3)

            P.op("tensor", lambda t: t.matmul(psa[0][:, 0:R],
                                              G.onesrow[0:1, :],
                                              rows3[0:1, 0:R], start=True,
                                              stop=True),
                 reads=[b_rows3, G.onesrow_b], writes=[b_psa[0]])
            P.op("vector", lambda v: v.tensor_copy(dbb[:], psa[0][:, 0:R]),
                 reads=[b_psa[0]], writes=[b_dbb], flush=True)
            for half in range(2):
                P.op("tensor", lambda t, half=half: t.matmul(
                    psa[half][:], G.onesrow[0:1, :],
                    rows3[0:1, R + half * 512:R + (half + 1) * 512],
                    start=True, stop=True),
                    reads=[b_rows3, G.onesrow_b], writes=[b_psa[half]])
                P.op("vector", lambda v, half=half: v.tensor_copy(
                    ubb[:, half * 512:(half + 1) * 512], psa[half][:]),
                    reads=[b_psa[half]], writes=[b_ubb], flush=True)
                P.op("tensor", lambda t, half=half: t.matmul(
                    psa[half][:], G.onesrow[0:1, :],
                    rows3[0:1, R + E + half * 512:R + E + (half + 1) * 512],
                    start=True, stop=True),
                    reads=[b_rows3, G.onesrow_b], writes=[b_psa[half]])
                P.op("vector", lambda v, half=half: v.tensor_add(
                    ybb[:, half * 512:(half + 1) * 512], psa[half][:],
                    bb_bc[:, half * 512:(half + 1) * 512]),
                    reads=[b_psa[half], b_bb_bc], writes=[b_ybb], flush=True)

            P.op("vector", lambda v: v.tensor_copy(
                dwb.rearrange("p c r -> p (c r)"),
                dwg.rearrange("p c r -> p (c r)")),
                reads=[b_dwg], writes=[b_dwb])
            P.op("vector", lambda v: v.tensor_copy(uwb[:], uwg[:]),
                 reads=[b_uwg], writes=[b_uwb])
            for c in range(EC):
                P.op("vector", lambda v, c=c: v.tensor_copy(cstg[:], cwg[:, c, :]),
                     reads=[b_cwg], writes=[b_cstg])

                def trc(t, c=c):
                    last = None
                    for oc in range(EC):
                        last = t.transpose(psW[:, oc * 128:(oc + 1) * 128],
                                           cstg[:, oc * 128:(oc + 1) * 128],
                                           G.identb[:])
                    return last
                P.op("tensor", trc, reads=[b_cstg, G.identb_b], writes=[b_psW])
                P.op("vector", lambda v, c=c: v.tensor_copy(
                    mkap(cwT[:, 0, 0:1], c * 128, [[E, EC], [1, 128]]),
                    psW.rearrange("p (c t) -> p c t", c=EC)),
                    reads=[b_psW], writes=[b_cwT])

            for tt in range(S // 128):
                P.dma("sync", out=xt[:],
                      in_=xnfm[0][bb, :, :, tt * 128:(tt + 1) * 128].rearrange(
                          "c p t -> p c t"),
                      reads=[xnfm[1]], writes=[b_xtile], slot=b_xtile)

                def mmh(t):
                    last = None
                    for ec in range(EC):
                        last = t.matmul(psh[:], xt[:, ec, :], dwb[:, ec, :],
                                        start=(ec == 0), stop=(ec == EC - 1))
                    return last
                P.op("tensor", mmh, reads=[b_xtile, b_dwb], writes=[b_psh])

                P.op("vector", lambda v: v.tensor_add(hrelu[:], psh[:], dbb[:]),
                     reads=[b_psh, b_dbb], writes=[b_hrelu], flush=True)
                P.op("vector", lambda v: v.tensor_relu(hh[:], hrelu[:]),
                     reads=[b_hrelu], writes=[b_hh])
                P.op("tensor", lambda t: t.transpose(psHT[0:64, 0:128], hh[:],
                                                     G.identb[:]),
                     reads=[b_hh, G.identb_b], writes=[b_psHT])
                P.op("vector", lambda v: v.tensor_copy(hfmT[0:64, :],
                                                       psHT[0:64, 0:128]),
                     reads=[b_psHT], writes=[b_hfmT])
                for ds in range(2):
                    P.op("tensor", lambda t, ds=ds: t.matmul(
                        psa[ds][:], hfmT[0:64, :],
                        uwb[0:64, ds * 512:(ds + 1) * 512], start=True,
                        stop=True),
                        reads=[b_hfmT, b_uwb], writes=[b_psa[ds]])
                    P.op("vector", lambda v, ds=ds: v.tensor_add(
                        aa[:, ds * 512:(ds + 1) * 512], psa[ds][:],
                        ubb[:, ds * 512:(ds + 1) * 512]),
                        reads=[b_psa[ds], b_ubb], writes=[b_aa])

                def tra(t):
                    last = None
                    for c in range(EC):
                        last = t.transpose(psW[:, c * 128:(c + 1) * 128],
                                           aa[:, c * 128:(c + 1) * 128],
                                           G.identb[:])
                    return last
                P.op("tensor", tra, reads=[b_aa, G.identb_b], writes=[b_psW])
                P.op("vector", lambda v: v.tensor_copy(
                    afm.rearrange("p c t -> p (c t)"), psW[:]),
                    reads=[b_psW], writes=[b_afm])
                for ds in range(2):
                    def mmy(t, ds=ds):
                        last = None
                        for ec in range(EC):
                            t.matmul(psa[ds][:], xt[:, ec, :],
                                     bwT[:, ec, ds * 512:(ds + 1) * 512],
                                     start=(ec == 0), stop=False)
                        for ec in range(EC):
                            last = t.matmul(psa[ds][:], afm[:, ec, :],
                                            cwT[:, ec, ds * 512:(ds + 1) * 512],
                                            start=False, stop=(ec == EC - 1))
                        return last
                    P.op("tensor", mmy, reads=[b_xtile, b_afm, b_bwT, b_cwT],
                         writes=[b_psa[ds]])
                    P.op("vector", lambda v, ds=ds: v.tensor_add(
                        yst[:, ds * 512:(ds + 1) * 512], psa[ds][:],
                        ybb[:, ds * 512:(ds + 1) * 512]),
                        reads=[b_psa[ds], b_ybb], writes=[b_yst])
                P.dma("gpsimd",
                      out=y[0][bb * S + tt * 128: bb * S + (tt + 1) * 128, :],
                      in_=yst[:], reads=[b_yst], writes=[y[1]], slot=b_yst)
    P.barrier()
    stage_post_ln_add(C, G, y[0], y[1], resid, b_resid)
    P.barrier()



def emit_L2_attn(C, G, resid, b_resid):
    P = C.P
    H, DH = 8, 128
    NW = 32
    xnfm = C.dram("l2a_xnfm", (B, EC, 128, S), BF16)
    qfm = C.dram("l2a_q", (B, H, 128, S), BF16)
    kfm = C.dram("l2a_k", (B, H, 128, S), BF16)
    vtm = C.dram("l2a_v", (B, S, E), BF16)
    plA = C.dram("l2a_plA", (B, S, E), F32)
    plB = C.dram("l2a_plB", (B, S, E), F32)
    y = C.dram("l2a_y", (T, E), F32)

    stage_prep_xn(C, G, resid, b_resid, want_fm=xnfm)
    P.barrier()

    with ExitStack() as es:
        wq, b_wq = load_w_natural(C, "l2a_wqkv", C.W["L2_attn_w_qkv"], E,
                                  3072, es)
        xsb, b_xsb = C.sb("l2a_xsb", [128, EC, S], BF16, es=es)
        stg, b_stg = [list(z) for z in zip(C.sb("l2a_st0", [128, 512], BF16, es=es),
                                           C.sb("l2a_st1", [128, 512], BF16, es=es))]
        psq, b_psq = [list(z) for z in zip(C.ps("l2a_ps0", [128, 512], F32, es=es),
                                           C.ps("l2a_ps1", [128, 512], F32, es=es))]
        for bb in range(B):
            P.dma("sync", out=xsb[:],
                  in_=xnfm[0][bb].rearrange("c p t -> p c t"),
                  reads=[xnfm[1]], writes=[b_xsb], slot=b_xsb)
            for which, base, dst in (("q", 0, qfm), ("k", E, kfm)):
                for h in range(H):
                    for ts in range(S // 512):
                        j = (h * (S // 512) + ts) % 2

                        def mm(t, h=h, ts=ts, j=j, base=base, psq=psq, xsb=xsb,
                               wq=wq):
                            last = None
                            for ec in range(EC):
                                last = t.matmul(
                                    psq[j][:, 0:128] if False else psq[j][:, 0:512],
                                    wq[:, ec, base + h * 128: base + (h + 1) * 128],
                                    xsb[:, ec, ts * 512:(ts + 1) * 512],
                                    start=(ec == 0), stop=(ec == EC - 1))
                            return last
                        P.op("tensor", mm, reads=[b_wq, b_xsb], writes=[b_psq[j]])
                        P.op("vector", lambda v, j=j, stg=stg, psq=psq:
                             v.tensor_copy(stg[j][:], psq[j][:]),
                             reads=[b_psq[j]], writes=[b_stg[j]])
                        P.dma("gpsimd",
                              out=dst[0][bb, h, :, ts * 512:(ts + 1) * 512],
                              in_=stg[j][:], reads=[b_stg[j]], writes=[dst[1]],
                              slot=b_stg[j])
            for tt in range(S // 128):
                for ds in range(E // 512):
                    j = (tt * (E // 512) + ds) % 2

                    def mmv(t, tt=tt, ds=ds, j=j, psq=psq, xsb=xsb, wq=wq):
                        last = None
                        for ec in range(EC):
                            last = t.matmul(
                                psq[j][:],
                                xsb[:, ec, tt * 128:(tt + 1) * 128],
                                wq[:, ec, 2 * E + ds * 512:2 * E + (ds + 1) * 512],
                                start=(ec == 0), stop=(ec == EC - 1))
                        return last
                    P.op("tensor", mmv, reads=[b_wq, b_xsb], writes=[b_psq[j]])
                    P.op("vector", lambda v, j=j, stg=stg, psq=psq:
                         v.tensor_copy(stg[j][:], psq[j][:]),
                         reads=[b_psq[j]], writes=[b_stg[j]])
                    P.dma("gpsimd",
                          out=vtm[0][bb, tt * 128:(tt + 1) * 128,
                                     ds * 512:(ds + 1) * 512],
                          in_=stg[j][:], reads=[b_stg[j]], writes=[vtm[1]],
                          slot=b_stg[j])
    P.barrier()

    NTL = 32
    with ExitStack() as es:
        mskA, b_mskA = C.sb("l2a_mA", [128, 128], F32, es=es)
        mskB, b_mskB = C.sb("l2a_mB", [128, 128], F32, es=es)
        P.dma("sync", out=mskA[:], in_=C.W["c_l2a_maskA"], writes=[b_mskA],
              slot=b_mskA)
        P.dma("sync", out=mskB[:], in_=C.W["c_l2a_maskB"], writes=[b_mskB],
              slot=b_mskB)
        qsb, b_qsb = C.sb("l2a_at_q", [128, S], BF16, es=es)
        ksb, b_ksb = C.sb("l2a_at_k", [128, S], BF16, es=es)
        vband, b_vband = [list(z) for z in zip(
            C.sb("l2a_at_v0", [68, 128], BF16, es=es),
            C.sb("l2a_at_v1", [68, 128], BF16, es=es))]
        qdup, b_qdup = C.sb("l2a_at_qd", [128, 128], BF16, es=es)
        ssb, b_ssb = C.sb("l2a_at_s", [128, 128], F32, es=es)
        pbf, b_pbf = C.sb("l2a_at_p", [128, 128], BF16, es=es)
        peff, b_peff = C.sb("l2a_at_pe", [128, 68], BF16, es=es)
        pef2, b_pef2 = C.sb("l2a_at_pe2", [128, 68], BF16, es=es)
        peS, b_peS = C.sb("l2a_at_peS", [128, 68], BF16, es=es)
        petb, b_petb = C.sb("l2a_at_pet", [68, 128], BF16, es=es)
        osb, b_osb = C.sb("l2a_at_o", [128, 128], F32, es=es)
        stt, b_stt = C.sb("l2a_at_st", [128, 4], F32, es=es)
        psS, b_psS = C.ps("l2a_at_psS", [128, 128], F32, es=es)
        psT, b_psT = C.ps("l2a_at_psT", [128, 128], BF16, es=es)
        psO, b_psO = C.ps("l2a_at_psO", [128, 128], F32, es=es)
        zl, b_zl = C.sb("l2a_zl", [128, E], F32, es=es)
        P.op("gpsimd", lambda g: g.memset(zl[:], 0.0), writes=[b_zl])
        for bb in range(B):
            P.dma("gpsimd", out=plA[0][bb, 2046:2048, :], in_=zl[0:2, :],
                  reads=[b_zl], writes=[plA[1]], slot=b_zl)
            P.dma("gpsimd", out=plB[0][bb, 0:2, :], in_=zl[0:2, :],
                  reads=[b_zl], writes=[plB[1]], slot=b_zl)
            for h in range(H):
                P.dma("sync", out=qsb[:], in_=qfm[0][bb, h], reads=[qfm[1]],
                      writes=[b_qsb], slot=b_qsb)
                P.dma("sync", out=ksb[:], in_=kfm[0][bb, h], reads=[kfm[1]],
                      writes=[b_ksb], slot=b_ksb)
                for tl in range(NTL):
                    nw = 31 if tl == NTL - 1 else 32
                    nq = 4 * nw
                    t0 = 64 * tl
                    keyw = 2 * nw + 2
                    keyw = min(keyw, S - t0)
                    jv = tl % 2
                    lq = mkap(qsb[:, 0:1], t0, [[2, 2], [2, nw], [1, 2]])
                    rk = mkap(ksb[:, 0:1], t0, [[2, nw], [1, 4]])
                    P.op("vector", lambda v, lq=lq, nq=nq: v.tensor_copy(
                        qdup[:, 0:nq].rearrange("p (a b c) -> p a b c", a=2,
                                                b=nq // 4), lq),
                        reads=[b_qsb], writes=[b_qdup])
                    P.op("tensor", lambda t, rk=rk, nq=nq:
                         t.matmul(psS[0:nq, 0:nq], qdup[:, 0:nq], rk,
                                  start=True, stop=True),
                         reads=[b_qdup, b_ksb], writes=[b_psS])
                    msk = mskA if nw == 32 else mskB
                    b_msk = b_mskA if nw == 32 else b_mskB
                    P.op("vector", lambda v, nq=nq, msk=msk:
                         v.scalar_tensor_tensor(ssb[0:nq, 0:nq], psS[0:nq, 0:nq],
                                                0.03125, msk[0:nq, 0:nq],
                                                ALU.mult, ALU.add),
                         reads=[b_psS, b_msk], writes=[b_ssb])
                    P.op("vector", lambda v, nq=nq: v.reduce_max(
                        stt[0:nq, 0:1], ssb[0:nq, 0:nq], axis=AX.X),
                        reads=[b_ssb], writes=[b_stt], flush=True)
                    P.op("vector", lambda v, nq=nq: v.tensor_scalar_mul(
                        stt[0:nq, 1:2], stt[0:nq, 0:1], -1.0),
                        reads=[b_stt], writes=[b_stt], flush=True)

                    def expacc(s, nq=nq):
                        s.activation(pbf[0:nq, 0:nq], ssb[0:nq, 0:nq], AF.Exp,
                                     bias=stt[0:nq, 1:2],
                                     accum_out=stt[0:nq, 2:3])
                        s.drain()
                        return s.mul(stt[0:nq, 3:4], stt[0:nq, 2:3], 1.0)
                    P.op("scalar", expacc, reads=[b_ssb, b_stt],
                         writes=[b_pbf, b_stt], flush=True)

                    def norm(v, nq=nq):
                        v.reciprocal(stt[0:nq, 2:3], stt[0:nq, 3:4])
                        v.drain()
                        return v.tensor_scalar(pbf[0:nq, 0:nq], pbf[0:nq, 0:nq],
                                               stt[0:nq, 2:3], None, ALU.mult)
                    P.op("vector", norm, reads=[b_stt, b_pbf], writes=[b_pbf, b_stt])
                    def scat0(v, keyw=keyw):
                        v.memset(peff[:, 0:keyw], 0.0)
                        return v.memset(pef2[:, 0:keyw], 0.0)
                    P.op("vector", scat0, writes=[b_peff, b_pef2], flush=True)

                    def scat1(v, nw=nw):
                        v.tensor_copy(mkap(peff[:, 0:1], 0, [[2, nw], [1, 2]]),
                                      mkap(pbf[:, 0:1], 0, [[4, nw], [1, 2]]))
                        return v.tensor_copy(
                            mkap(pef2[:, 0:1], 2, [[2, nw], [1, 2]]),
                            mkap(pbf[:, 0:1], 2, [[4, nw], [1, 2]]))
                    P.op("vector", scat1, reads=[b_pbf], writes=[b_peff, b_pef2],
                         flush=True)
                    P.op("vector", lambda v, keyw=keyw: v.tensor_add(
                        peS[:, 0:keyw], peff[:, 0:keyw], pef2[:, 0:keyw]),
                        reads=[b_peff, b_pef2], writes=[b_peS], flush=True)
                    P.op("tensor", lambda t, nq=nq, keyw=keyw: t.transpose(
                        psT[0:keyw, 0:nq], peS[0:nq, 0:keyw], G.identb[0:nq, 0:nq]),
                        reads=[b_peS, G.identb_b], writes=[b_psT])
                    P.op("vector", lambda v, nq=nq, keyw=keyw: v.tensor_copy(
                        petb[0:keyw, 0:nq], psT[0:keyw, 0:nq]),
                        reads=[b_psT], writes=[b_petb])
                    P.dma("sync", out=vband[jv][0:keyw, :],
                          in_=vtm[0][bb, t0:t0 + keyw, h * 128:(h + 1) * 128],
                          reads=[vtm[1]], writes=[b_vband[jv]], slot=b_vband[jv])
                    P.op("tensor", lambda t, nq=nq, keyw=keyw, jv=jv: t.matmul(
                        psO[0:nq, 0:128], petb[0:keyw, 0:nq],
                        vband[jv][0:keyw, :], start=True, stop=True),
                        reads=[b_petb, b_vband[jv]], writes=[b_psO])
                    P.op("vector", lambda v, nq=nq: v.tensor_copy(
                        osb[0:nq, :], psO[0:nq, 0:128]),
                        reads=[b_psO], writes=[b_osb])
                    P.dma("gpsimd", out=plA[0][bb, t0:t0 + 2 * nw,
                                               h * 128:(h + 1) * 128],
                          in_=osb[0:2 * nw, :], reads=[b_osb], writes=[plA[1]],
                          slot=b_osb)
                    P.dma("gpsimd", out=plB[0][bb, t0 + 2:t0 + 2 + 2 * nw,
                                               h * 128:(h + 1) * 128],
                          in_=osb[2 * nw:4 * nw, :], reads=[b_osb],
                          writes=[plB[1]], slot=b_osb)
    P.barrier()

    with ExitStack() as es:
        wo, b_wo = load_w_natural(C, "l2a_wo", C.W["L2_attn_w_out"], E, E, es)
        with ExitStack() as es_tmp:
            ob_bc, b_ob_bc = bcast_row(C, G, "l2a_ob", C.W["L2_attn_b_out"],
                                       E, es, es_tmp=es_tmp)
        P.barrier()
        cnt, b_cnt = C.sb("l2a_cnt", [128, NT // B], F32, es=es)

        def mkcnt(g):
            g.memset(cnt[:], 2.0)
            g.affine_select(out=cnt[:], in_=cnt[:], compare_op=ALU.is_ge,
                            fill=1.0, base=-2, pattern=[[128, NT // B]],
                            channel_multiplier=1)
            return g.affine_select(out=cnt[:], in_=cnt[:], compare_op=ALU.is_ge,
                                   fill=1.0, base=2045,
                                   pattern=[[-128, NT // B]],
                                   channel_multiplier=-1)
        P.op("gpsimd", mkcnt, writes=[b_cnt], flush=True)

        oa, b_oa = [list(z) for z in zip(C.sb("l2a_o0", [128, E], F32, es=es),
                                         C.sb("l2a_o1", [128, E], F32, es=es))]
        obx, b_obx = [list(z) for z in zip(C.sb("l2a_ob0", [128, E], F32, es=es),
                                           C.sb("l2a_ob1", [128, E], F32, es=es))]
        obf, b_obf = [list(z) for z in zip(C.sb("l2a_obf0", [128, E], BF16, es=es),
                                           C.sb("l2a_obf1", [128, E], BF16, es=es))]
        ofc, b_ofc = [list(z) for z in zip(
            C.sb("l2a_ofc0", [128, EC, 128], BF16, es=es),
            C.sb("l2a_ofc1", [128, EC, 128], BF16, es=es))]
        yst, b_yst = [list(z) for z in zip(C.sb("l2a_y0", [128, 512], F32, es=es),
                                           C.sb("l2a_y1", [128, 512], F32, es=es))]
        psq2, b_psq2 = [list(z) for z in zip(C.ps("l2a_pq0", [128, 512], F32, es=es),
                                             C.ps("l2a_pq1", [128, 512], F32, es=es))]
        psE, b_psE = [list(z) for z in zip(C.ps("l2a_pe0", [128, E], BF16, es=es),
                                           C.ps("l2a_pe1", [128, E], BF16, es=es))]
        for i in range(NT):
            j = i % 2
            bb, tt = i // (NT // B), i % (NT // B)
            P.dma("sync", out=oa[j][:],
                  in_=plA[0][bb, tt * 128:(tt + 1) * 128, :],
                  reads=[plA[1]], writes=[b_oa[j]], slot=b_oa[j])
            P.dma("scalar", out=obx[j][:],
                  in_=plB[0][bb, tt * 128:(tt + 1) * 128, :],
                  reads=[plB[1]], writes=[b_obx[j]], slot=b_obx[j])
            P.op("vector", lambda v, j=j, oa=oa, obx=obx, obf=obf:
                 v.tensor_add(obf[j][:], oa[j][:], obx[j][:]),
                 reads=[b_oa[j], b_obx[j]], writes=[b_obf[j]])
            transpose_to_fm(C, G, obf[j][:], b_obf[j], psE[j][:], b_psE[j],
                            ofc[j][:], b_ofc[j])
            for ds in range(E // 512):
                jj = (i * (E // 512) + ds) % 2

                def mm(t, j=j, ds=ds, jj=jj, ofc=ofc, psq2=psq2, wo=wo):
                    last = None
                    for ec in range(EC):
                        last = t.matmul(
                            psq2[jj][:], ofc[j][:, ec, :],
                            wo[:, ec, ds * 512:(ds + 1) * 512],
                            start=(ec == 0), stop=(ec == EC - 1))
                    return last
                P.op("tensor", mm, reads=[b_ofc[j], b_wo], writes=[b_psq2[jj]])
                P.op("vector", lambda v, jj=jj, ds=ds, tt=tt, yst=yst,
                     psq2=psq2: v.scalar_tensor_tensor(
                         yst[jj][:], ob_bc[:, ds * 512:(ds + 1) * 512],
                         cnt[:, tt:tt + 1], psq2[jj][:], ALU.mult, ALU.add),
                     reads=[b_psq2[jj], b_ob_bc, b_cnt], writes=[b_yst[jj]])
                P.dma("gpsimd",
                      out=y[0][i * 128:(i + 1) * 128, ds * 512:(ds + 1) * 512],
                      in_=yst[jj][:], reads=[b_yst[jj]], writes=[y[1]],
                      slot=b_yst[jj])
    P.barrier()
    stage_post_ln_add(C, G, y[0], y[1], resid, b_resid)
    P.barrier()



def emit_L1_attn(C, G, resid, b_resid):
    P = C.P
    DK = 128
    xnfm = C.dram("l1a_xnfm", (B, EC, 128, S), BF16)
    cat = C.dram("l1a_cat", (B, S, 512), F32)
    y = C.dram("l1a_y", (T, E), F32)

    stage_prep_xn(C, G, resid, b_resid, want_fm=xnfm)
    P.barrier()

    with ExitStack() as es:
        with ExitStack() as es_tmp:
            wqT, b_wqT = prep_wT(C, G, "l1a_wqT",
                                 C.W["L1_attn_Wq"].rearrange("h k e -> (h k) e"),
                                 E, 512, es, es_tmp=es_tmp)
            wkT, b_wkT = prep_wT(C, G, "l1a_wkT",
                                 C.W["L1_attn_Wk"].rearrange("h k e -> (h k) e"),
                                 E, 512, es, es_tmp=es_tmp)
            wvT, b_wvT = prep_wT(C, G, "l1a_wvT",
                                 C.W["L1_attn_Wv"].rearrange("h k e -> (h k) e"),
                                 E, 512, es, es_tmp=es_tmp)
        P.barrier()

        zl, b_zl = C.sb("l1a_zl", [128, 512], F32, es=es)
        P.op("gpsimd", lambda g: g.memset(zl[:], 0.0), writes=[b_zl])
        xsb, b_xsb = C.sb("l1a_xsb", [128, EC, S], BF16, es=es)
        qL, b_qL = C.sb("l1a_qL", [128, 16, DK], F32, es=es)
        kL, b_kL = C.sb("l1a_kL", [128, 16, DK], F32, es=es)
        vL, b_vL = C.sb("l1a_vL", [128, 16, DK], F32, es=es)
        kq, b_kq = C.sb("l1a_kq", [128, 136], F32, es=es)
        pp, b_pp = C.sb("l1a_pp", [128, 136], F32, es=es)
        tmp, b_tmp = C.sb("l1a_tmp", [128, DK], F32, es=es)
        acc, b_acc = [list(z) for z in zip(C.sb("l1a_a0", [128, DK], F32, es=es),
                                           C.sb("l1a_a1", [128, DK], F32, es=es))]
        stt, b_stt = C.sb("l1a_stt", [128, 4], F32, es=es)
        psq, b_psq = C.ps("l1a_psq", [128, 3 * DK], F32, es=es)
        NORM = 1.0 / math.sqrt(DK)

        for bb in range(B):
            for zt in range(S // 128):
                P.dma("gpsimd", out=cat[0][bb, zt * 128:(zt + 1) * 128, :],
                      in_=zl[:], reads=[b_zl], writes=[cat[1]], slot=b_zl)
            P.dma("sync", out=xsb[:],
                  in_=xnfm[0][bb].rearrange("c p t -> p c t"),
                  reads=[xnfm[1]], writes=[b_xsb], slot=b_xsb)
            for h in range(4):
                dil = 1 << h
                L = 16 // dil
                for l in range(L):
                    blk = l * dil

                    def mmqkv(t, l=l, blk=blk, h=h):
                        last = None
                        for w_i, wT in enumerate((wqT, wkT, wvT)):
                            for ec in range(EC):
                                last = t.matmul(
                                    psq[:, w_i * DK:(w_i + 1) * DK],
                                    xsb[:, ec, blk * 128:(blk + 1) * 128],
                                    wT[:, ec, h * 128:(h + 1) * 128],
                                    start=(ec == 0), stop=(ec == EC - 1))
                        return last
                    P.op("tensor", mmqkv, reads=[b_xsb, b_wqT, b_wkT, b_wvT],
                         writes=[b_psq])

                    def ev(v, l=l):
                        v.tensor_copy(qL[:, l, :], psq[:, 0:DK])
                        v.tensor_copy(kL[:, l, :], psq[:, DK:2 * DK])
                        return v.tensor_copy(vL[:, l, :], psq[:, 2 * DK:3 * DK])
                    P.op("vector", ev, reads=[b_psq],
                         writes=[b_qL, b_kL, b_vL])
                base = [n * (n + 1) // 2 for n in range(L + 1)]
                for n in range(L):
                    def dots(v, n=n, base=base):
                        last = None
                        for m in range(n + 1):
                            v.tensor_mul(tmp[:], kL[:, n, :], qL[:, m, :])
                            last = v.reduce_sum(kq[:, base[n] + m:base[n] + m + 1],
                                                tmp[:], axis=AX.X)
                        return last
                    P.op("vector", dots, reads=[b_kL, b_qL],
                         writes=[b_tmp, b_kq], flush=True)
                    nm = n + 1
                    P.op("vector", lambda v, n=n, nm=nm, base=base: v.reduce_max(
                        stt[:, 0:1], kq[:, base[n]:base[n] + nm], axis=AX.X),
                        reads=[b_kq], writes=[b_stt], flush=True)
                    P.op("vector", lambda v: v.tensor_scalar_mul(
                        stt[:, 1:2], stt[:, 0:1], -NORM),
                        reads=[b_stt], writes=[b_stt], flush=True)

                    def expn(s, n=n, nm=nm, base=base):
                        s.activation(pp[:, base[n]:base[n] + nm],
                                     kq[:, base[n]:base[n] + nm], AF.Exp,
                                     bias=stt[:, 1:2], scale=NORM,
                                     accum_out=stt[:, 2:3])
                        s.drain()
                        return s.mul(stt[:, 3:4], stt[:, 2:3], 1.0)
                    P.op("scalar", expn, reads=[b_kq, b_stt],
                         writes=[b_pp, b_stt], flush=True)
                    P.op("vector", lambda v: v.reciprocal(stt[:, 2:3],
                                                          stt[:, 3:4]),
                         reads=[b_stt], writes=[b_stt], flush=True)
                    P.op("vector", lambda v, n=n, nm=nm, base=base:
                         v.tensor_scalar(pp[:, base[n]:base[n] + nm],
                                         pp[:, base[n]:base[n] + nm],
                                         stt[:, 2:3], None, ALU.mult),
                         reads=[b_pp, b_stt], writes=[b_pp], flush=True)
                for m in range(L):
                    steps = list(range(m, L))
                    cur = 0
                    P.op("vector", lambda v, m=m, base=base: v.tensor_scalar(
                        acc[0][:], vL[:, m, :], pp[:, base[m] + m:base[m] + m + 1],
                        None, ALU.mult),
                        reads=[b_vL, b_pp], writes=[b_acc[0]])
                    for n in steps[1:]:
                        nxt = 1 - cur
                        P.op("vector", lambda v, n=n, m=m, cur=cur, nxt=nxt,
                             base=base: v.scalar_tensor_tensor(
                                 acc[nxt][:], vL[:, n, :],
                                 pp[:, base[n] + m:base[n] + m + 1],
                                 acc[cur][:], ALU.mult, ALU.add),
                             reads=[b_vL, b_pp, b_acc[cur]],
                             writes=[b_acc[nxt]])
                        cur = nxt
                    out_ap = mkap(cat[0][bb, 0:1, 0:1].squeeze(0) if False else
                                  cat[0][bb], dil * m * 512 + h * 128,
                                  [[1, DK]])
                    out_ap = AP(cat[0].tensor,
                                cat[0].offset + bb * S * 512 +
                                dil * m * 512 + h * 128,
                                [[16 * 512, 128], [1, DK]])
                    P.dma("gpsimd", out=out_ap, in_=acc[cur][:],
                          reads=[b_acc[cur]], writes=[cat[1]],
                          slot=b_acc[cur])
    P.barrier()

    with ExitStack() as es:
        with ExitStack() as es_tmp:
            woT, b_woT = prep_wT(C, G, "l1a_woT", C.W["L1_attn_W_out"], 512,
                                 E, es, es_tmp=es_tmp)
            ob_bc, b_ob_bc = bcast_row(C, G, "l1a_ob", C.W["L1_attn_b_out"],
                                       E, es, es_tmp=es_tmp)
        P.barrier()
        ct, b_ct = [list(z) for z in zip(C.sb("l1a_c0", [128, 512], F32, es=es),
                                         C.sb("l1a_c1", [128, 512], F32, es=es))]
        cb, b_cb = [list(z) for z in zip(C.sb("l1a_cb0", [128, 512], BF16, es=es),
                                         C.sb("l1a_cb1", [128, 512], BF16, es=es))]
        cf, b_cf = [list(z) for z in zip(
            C.sb("l1a_cf0", [128, 4, 128], BF16, es=es),
            C.sb("l1a_cf1", [128, 4, 128], BF16, es=es))]
        yst, b_yst = [list(z) for z in zip(C.sb("l1a_y0", [128, 512], F32, es=es),
                                           C.sb("l1a_y1", [128, 512], F32, es=es))]
        psE, b_psE = [list(z) for z in zip(C.ps("l1a_pe0", [128, 512], BF16, es=es),
                                           C.ps("l1a_pe1", [128, 512], BF16, es=es))]
        psq2, b_psq2 = [list(z) for z in zip(C.ps("l1a_pq0", [128, 512], F32, es=es),
                                             C.ps("l1a_pq1", [128, 512], F32, es=es))]
        for i in range(NT):
            j = i % 2
            bb, tt = i // (NT // B), i % (NT // B)
            P.dma("sync", out=ct[j][:],
                  in_=cat[0][bb, tt * 128:(tt + 1) * 128, :],
                  reads=[cat[1]], writes=[b_ct[j]], slot=b_ct[j])
            P.op("vector", lambda v, j=j, ct=ct, cb=cb: v.tensor_copy(
                cb[j][:], ct[j][:]), reads=[b_ct[j]], writes=[b_cb[j]])
            transpose_to_fm(C, G, cb[j][:], b_cb[j], psE[j][:], b_psE[j],
                            cf[j][:], b_cf[j], nchunks=4)
            for ds in range(E // 512):
                jj = (i * (E // 512) + ds) % 2

                def mm(t, j=j, ds=ds, jj=jj, cf=cf, psq2=psq2, woT=woT):
                    last = None
                    for vc in range(4):
                        last = t.matmul(
                            psq2[jj][:], cf[j][:, vc, :],
                            woT[:, vc, ds * 512:(ds + 1) * 512],
                            start=(vc == 0), stop=(vc == 3))
                    return last
                P.op("tensor", mm, reads=[b_cf[j], b_woT], writes=[b_psq2[jj]])
                P.op("vector", lambda v, jj=jj, ds=ds, yst=yst, psq2=psq2:
                     v.tensor_add(yst[jj][:], psq2[jj][:],
                                  ob_bc[:, ds * 512:(ds + 1) * 512]),
                     reads=[b_psq2[jj], b_ob_bc], writes=[b_yst[jj]])
                P.dma("gpsimd",
                      out=y[0][i * 128:(i + 1) * 128, ds * 512:(ds + 1) * 512],
                      in_=yst[jj][:], reads=[b_yst[jj]], writes=[y[1]],
                      slot=b_yst[jj])
    P.barrier()
    stage_post_ln_add(C, G, y[0], y[1], resid, b_resid)
    P.barrier()



def emit_L0_ffn(C, G, resid, b_resid):
    P = C.P
    H, DH = 16, 64
    DI = 2730
    NJ = 22
    z0 = C.dram("l0f_z0", (B, S, E), F32)
    z1 = C.dram("l0f_z1", (B, S, E), F32)
    y = C.dram("l0f_y", (T, E), F32)

    stage_prep_xn(C, G, resid, b_resid,
                  want_tm32=(z0[0].rearrange("b s e -> (b s) e"), z0[1]))
    P.barrier()

    if True:
        def setup_tables(es, li=0):
            csq, b_csq = C.sb(f"l0f_csq{li}", [128, S // 128, DH], F32, es=es)
            snq, b_snq = C.sb(f"l0f_snq{li}", [128, S // 128, DH], F32, es=es)
            csk, b_csk = C.sb(f"l0f_csk{li}", [128, S // 128, DH], F32, es=es)
            snk, b_snk = C.sb(f"l0f_snk{li}", [128, S // 128, DH], F32, es=es)
            for nm, tile, buf in (("c_l0f_csq", csq, b_csq),
                                  ("c_l0f_snq", snq, b_snq),
                                  ("c_l0f_csk", csk, b_csk),
                                  ("c_l0f_snk", snk, b_snk)):
                P.dma("sync", out=tile[:],
                      in_=C.W[nm].rearrange("(c p) d -> p c d", p=128),
                      writes=[buf], slot=buf)
            msks, b_msks = {}, {}
            for off in range(2):
                mk, b_mk = C.sb(f"l0f_msk{off}_{li}", [128, 512], F32, es=es)

                def bld(g, off=off, mk=mk):
                    g.memset(mk[:], 0.0)
                    g.affine_select(out=mk[:], in_=mk[:], compare_op=ALU.is_ge,
                                    fill=-FMAX, base=128 * off + 256,
                                    pattern=[[-1, 512]], channel_multiplier=1)
                    return g.affine_select(out=mk[:], in_=mk[:],
                                           compare_op=ALU.is_ge, fill=-FMAX,
                                           base=-128 * off, pattern=[[1, 512]],
                                           channel_multiplier=-1)
                P.op("gpsimd", bld, writes=[b_mk], flush=True)
                msks[(off, 0)] = mk
                b_msks[(off, 0)] = b_mk
            return (csq, b_csq, snq, b_snq, csk, b_csk, snk, b_snk,
                    msks, b_msks)

        zbufs = [z0, z1]
        for li in range(6):
            zsA, zdA = zbufs[0], zbufs[1]
            with ExitStack() as es:
                (csq, b_csq, snq, b_snq, csk, b_csk, snk, b_snk,
                 msks, b_msks) = setup_tables(es, li)
                wqkv, b_wqkv = load_w_natural(C, f"l0fA_wqkv{li}",
                                              C.W["L0_ffn_Wqkv"][li], E, 3072,
                                              es)
                wo, b_wo = load_w_natural(C, f"l0fA_wo{li}",
                                          C.W["L0_ffn_Wo"][li], E, E, es)
                with ExitStack() as es_tmp:
                    qs_bc, b_qs_bc = bcast_row(C, G, f"l0fA_qs{li}",
                                               C.W["L0_ffn_q_scale"][li], DH,
                                               es, es_tmp=es_tmp)
                    ks_bc, b_ks_bc = bcast_row(C, G, f"l0fA_ks{li}",
                                               C.W["L0_ffn_k_scale"][li], DH,
                                               es, es_tmp=es_tmp)
                P.barrier()
                qsr, b_qsr = C.sb(f"l0fA_qsr_{li}", [128, DH], F32, es=es)
                ksr, b_ksr = C.sb(f"l0fA_ksr_{li}", [128, DH], F32, es=es)

                def rotscale(v):
                    v.tensor_scalar_mul(qsr[:, 0:32], qs_bc[:, 32:64], -1.0)
                    v.tensor_copy(qsr[:, 32:64], qs_bc[:, 0:32])
                    v.tensor_scalar_mul(ksr[:, 0:32], ks_bc[:, 32:64], -1.0)
                    return v.tensor_copy(ksr[:, 32:64], ks_bc[:, 0:32])
                P.op("vector", rotscale, reads=[b_qs_bc, b_ks_bc],
                     writes=[b_qsr, b_ksr], flush=True)

                xs, b_xs = C.sb(f"l0fA_xs_{li}", [128, E], F32, es=es)
                xb, b_xb = C.sb(f"l0fA_xb_{li}", [128, E], BF16, es=es)
                xf, b_xf = C.sb(f"l0fA_xf_{li}", [128, EC, 128], BF16, es=es)
                qkv, b_qkv = C.sb(f"l0fA_qkv_{li}", [128, 3072], F32, es=es)
                sqn, b_sqn = C.sb(f"l0fA_sqn_{li}", [128, 2048], F32, es=es)
                s2, b_s2 = C.sb(f"l0fA_s2_{li}", [128, 32], F32, es=es)
                rno, b_rno = C.sb(f"l0fA_rno_{li}", [128, 32], F32, es=es)
                tabA, b_tabA = C.sb(f"l0fA_tA_{li}", [128, DH], F32, es=es)
                tabB, b_tabB = C.sb(f"l0fA_tB_{li}", [128, DH], F32, es=es)
                tabC, b_tabC = C.sb(f"l0fA_tC_{li}", [128, DH], F32, es=es)
                tabD, b_tabD = C.sb(f"l0fA_tD_{li}", [128, DH], F32, es=es)
                qn, b_qn = C.sb(f"l0fA_qn_{li}", [128, 2048], F32, es=es)
                qsw, b_qsw = C.sb(f"l0fA_qsw_{li}", [128, 2048], F32, es=es)
                qt1, b_qt1 = C.sb(f"l0fA_qt1_{li}", [128, 2048], F32, es=es)
                qkb, b_qkb = C.sb(f"l0fA_qkb_{li}", [128, 2048], BF16, es=es)
                kR, b_kR = C.sb(f"l0fA_kR_{li}", [64, H, 4, 128], BF16, es=es)
                vR, b_vR = C.sb(f"l0fA_vR_{li}", [128, 4, H, DH], BF16, es=es)
                qfmr, b_qfmr = C.sb(f"l0fA_qfm_{li}", [64, H, 2, 128], BF16, es=es)
                ssbR = [C.sb(f"l0fA_ssb_{li}", [128, 512], F32, es=es)
                        for _ in range(2)]
                pbfR = [C.sb(f"l0fA_pbf_{li}", [128, 512], BF16, es=es)
                        for _ in range(2)]
                ptbR = [C.sb(f"l0fA_ptb_{li}", [128, 4, 128], BF16, es=es)
                        for _ in range(2)]
                sttR = [C.sb(f"l0fA_stt_{li}", [128, 4], F32, es=es)
                        for _ in range(2)]
                osbR = [C.sb(f"l0fA_osb_{li}", [64, 128], F32, es=es)
                        for _ in range(2)]
                afm, b_afm = C.sb(f"l0fA_afm_{li}", [128, EC, 2, 128], BF16, es=es)
                zres, b_zres = C.sb(f"l0fA_zres_{li}", [128, E], F32, es=es)
                zout, b_zout = C.sb(f"l0fA_zout_{li}", [128, E], F32, es=es)
                psq, b_psq = C.ps(f"l0fA_psq_{li}", [128, 512], F32, es=es)
                psH, b_psH = C.ps(f"l0fA_psH_{li}", [128, 1024], BF16, es=es)
                psF, b_psF = psH, b_psH
                psSR = [C.ps(f"l0fA_psS_{li}", [128, 512], F32, es=es)
                        for _ in range(2)]
                psT2, _bT = C.ps(f"l0fA_psT_{li}", [128, 1024], BF16, es=es)
                psTR = [(psT2[:, 0:512], Buf("psTa")),
                        (psT2[:, 512:1024], Buf("psTb"))]
                psO2, _bO = C.ps(f"l0fA_psO_{li}", [64, 256], F32, es=es)
                psOR = [(psO2[:, 0:128], Buf("psOa")),
                        (psO2[:, 128:256], Buf("psOb"))]
                psR, b_psR = C.ps(f"l0fA_psR_{li}", [128, 512], F32, es=es)

                g32 = lambda t_: t_.rearrange("p (g d) -> p g d", g=32)
                gH = lambda t_, o: t_[:, o:o + 1024].rearrange(
                    "p (h d) -> p h d", h=H)

                for bb in range(B):
                    for w in range(S // 256):
                        for tt in (2 * w, 2 * w + 1):
                            t0 = tt * 128
                            if tt == 0:
                                P.op("gpsimd", lambda g: g.memset(
                                    xs[0:1, 512:1024], 0.0), writes=[b_xs])
                                P.dma("sync", out=xs[:, 0:512],
                                      in_=zsA[0][bb, 0:128, 0:512],
                                      reads=[zsA[1]], writes=[b_xs], slot=b_xs)
                                P.dma("sync", out=xs[1:128, 512:1024],
                                      in_=zsA[0][bb, 0:127, 512:1024],
                                      reads=[zsA[1]], writes=[b_xs], slot=b_xs)
                            else:
                                P.dma("sync", out=xs[:, 0:512],
                                      in_=zsA[0][bb, t0:t0 + 128, 0:512],
                                      reads=[zsA[1]], writes=[b_xs], slot=b_xs)
                                P.dma("sync", out=xs[:, 512:1024],
                                      in_=zsA[0][bb, t0 - 1:t0 + 127, 512:1024],
                                      reads=[zsA[1]], writes=[b_xs], slot=b_xs)
                            P.op("vector", lambda v: v.tensor_copy(xb[:], xs[:]),
                                 reads=[b_xs], writes=[b_xb])
                            transpose_to_fm(C, G, xb[:], b_xb, psF[:], b_psF,
                                            xf[:], b_xf)
                            for sl in range(6):
                                def mmq(t, sl=sl):
                                    last = None
                                    for ec in range(EC):
                                        last = t.matmul(
                                            psq[:], xf[:, ec, :],
                                            wqkv[:, ec, sl * 512:(sl + 1) * 512],
                                            start=(ec == 0),
                                            stop=(ec == EC - 1))
                                    return last
                                P.op("tensor", mmq, reads=[b_xf, b_wqkv],
                                     writes=[b_psq])
                                P.op("vector", lambda v, sl=sl: v.tensor_copy(
                                    qkv[:, sl * 512:(sl + 1) * 512], psq[:]),
                                    reads=[b_psq], writes=[b_qkv])
                            P.op("scalar", lambda s: s.activation(
                                sqn[:], qkv[:, 0:2048], AF.Square),
                                reads=[b_qkv], writes=[b_sqn])
                            P.op("vector", lambda v: v.reduce_sum(
                                s2[:], g32(sqn), axis=AX.X),
                                reads=[b_sqn], writes=[b_s2], flush=True)
                            P.op("scalar", lambda s: s.activation(
                                rno[:], s2[:], AF.Sqrt),
                                reads=[b_s2], writes=[b_rno], flush=True)
                            P.op("vector", lambda v: v.tensor_scalar_max(
                                rno[:], rno[:], 1e-12),
                                reads=[b_rno], writes=[b_rno], flush=True)
                            P.op("vector", lambda v: v.reciprocal(
                                rno[:], rno[:]),
                                reads=[b_rno], writes=[b_rno], flush=True)
                            P.op("vector", lambda v: v.tensor_tensor(
                                g32(qn), g32(qkv[:, 0:2048]),
                                rno[:].unsqueeze(2).broadcast_to([128, 32, DH]),
                                ALU.mult),
                                reads=[b_qkv, b_rno], writes=[b_qn], flush=True)

                            def tabs(v, tt=tt):
                                v.tensor_mul(tabA[:], qs_bc[:], csq[:, tt, :])
                                v.tensor_mul(tabB[:], qsr[:], snq[:, tt, :])
                                v.tensor_mul(tabC[:], ks_bc[:], csk[:, tt, :])
                                return v.tensor_mul(tabD[:], ksr[:],
                                                    snk[:, tt, :])
                            P.op("vector", tabs,
                                 reads=[b_qs_bc, b_qsr, b_ks_bc, b_ksr,
                                        b_csq, b_snq, b_csk, b_snk],
                                 writes=[b_tabA, b_tabB, b_tabC, b_tabD],
                                 flush=True)

                            def swap(v):
                                v.tensor_copy(g32(qsw)[:, :, 0:32],
                                              g32(qn)[:, :, 32:64])
                                return v.tensor_copy(g32(qsw)[:, :, 32:64],
                                                     g32(qn)[:, :, 0:32])
                            P.op("vector", swap, reads=[b_qn], writes=[b_qsw],
                                 flush=True)

                            def term1(v):
                                v.tensor_tensor(
                                    gH(qt1, 0), gH(qn, 0),
                                    tabA[:].unsqueeze(1).broadcast_to([128, H, DH]),
                                    ALU.mult)
                                return v.tensor_tensor(
                                    gH(qt1, 1024), gH(qn, 1024),
                                    tabC[:].unsqueeze(1).broadcast_to([128, H, DH]),
                                    ALU.mult)
                            P.op("vector", term1,
                                 reads=[b_qn, b_tabA, b_tabC], writes=[b_qt1],
                                 flush=True)

                            def term2(v):
                                v.tensor_tensor(
                                    gH(sqn, 0), gH(qsw, 0),
                                    tabB[:].unsqueeze(1).broadcast_to([128, H, DH]),
                                    ALU.mult)
                                return v.tensor_tensor(
                                    gH(sqn, 1024), gH(qsw, 1024),
                                    tabD[:].unsqueeze(1).broadcast_to([128, H, DH]),
                                    ALU.mult)
                            P.op("vector", term2,
                                 reads=[b_qsw, b_tabB, b_tabD], writes=[b_sqn],
                                 flush=True)
                            P.op("vector", lambda v: v.tensor_add(
                                qkb[:], qt1[:], sqn[:]),
                                reads=[b_qt1, b_sqn], writes=[b_qkb],
                                flush=True)
                            P.op("vector", lambda v, tt=tt: v.tensor_copy(
                                vR[:, tt % 4, :, :],
                                qkv[:, 2048:3072].rearrange(
                                    "p (h d) -> p h d", h=H)),
                                reads=[b_qkv], writes=[b_vR])
                            for part, kofs in ((0, 0), (1, 1024)):
                                for h0 in (0, 8):
                                    def trqk(t, h0=h0, kofs=kofs):
                                        last = None
                                        for hh in range(8):
                                            last = t.transpose(
                                                psH[0:64, hh * 128:(hh + 1) * 128],
                                                qkb[:, kofs + (h0 + hh) * DH:
                                                    kofs + (h0 + hh + 1) * DH],
                                                G.identb[:])
                                        return last
                                    P.op("tensor", trqk,
                                         reads=[b_qkb, G.identb_b],
                                         writes=[b_psH])
                                    if kofs == 0:
                                        P.op("vector", lambda v, h0=h0, tt=tt:
                                             v.tensor_copy(
                                                 qfmr[0:64, h0:h0 + 8, tt % 2, :],
                                                 psH[0:64, :].rearrange(
                                                     "p (h t) -> p h t", h=8)),
                                             reads=[b_psH], writes=[b_qfmr])
                                    else:
                                        P.op("vector", lambda v, h0=h0, tt=tt:
                                             v.tensor_copy(
                                                 kR[0:64, h0:h0 + 8, tt % 4, :],
                                                 psH[0:64, :].rearrange(
                                                     "p (h t) -> p h t", h=8)),
                                             reads=[b_psH], writes=[b_kR])
                        def attn_iter(h, tt, k0t, kw, moff, mk, b_mk, ssb,
                                      b_ssb, stt, b_stt, pbf, b_pbf, ptb,
                                      b_ptb, psS, b_psS, psT, b_psT, psO,
                                      b_psO):
                            def mmsc(t):
                                last = None
                                for ci in range(kw // 128):
                                    last = t.matmul(
                                        psS[:, ci * 128:(ci + 1) * 128],
                                        qfmr[0:64, h, tt % 2, :],
                                        kR[0:64, h, (k0t + ci) % 4, :],
                                        start=True, stop=True)
                                return last
                            P.op("tensor", mmsc, reads=[b_qfmr, b_kR],
                                 writes=[b_psS])
                            P.op("vector", lambda v: v.scalar_tensor_tensor(
                                ssb[:, 0:kw], psS[:, 0:kw], 8.0,
                                mk[:, moff:moff + kw], ALU.mult, ALU.add),
                                reads=[b_psS, b_mk], writes=[b_ssb])
                            P.op("vector", lambda v: v.reduce_max(
                                stt[:, 0:1], ssb[:, 0:kw], axis=AX.X),
                                reads=[b_ssb], writes=[b_stt], flush=True)
                            P.op("vector", lambda v: v.tensor_scalar_mul(
                                stt[:, 1:2], stt[:, 0:1], -1.0),
                                reads=[b_stt], writes=[b_stt], flush=True)

                            def expa(s):
                                s.activation(pbf[:, 0:kw], ssb[:, 0:kw],
                                             AF.Exp, bias=stt[:, 1:2],
                                             accum_out=stt[:, 2:3])
                                s.drain()
                                return s.mul(stt[:, 3:4], stt[:, 2:3], 1.0)
                            P.op("scalar", expa, reads=[b_ssb, b_stt],
                                 writes=[b_pbf, b_stt], flush=True)
                            P.op("vector", lambda v: v.reciprocal(
                                stt[:, 2:3], stt[:, 3:4]),
                                reads=[b_stt], writes=[b_stt], flush=True)
                            P.op("vector", lambda v: v.tensor_scalar(
                                pbf[:, 0:kw], pbf[:, 0:kw], stt[:, 2:3],
                                None, ALU.mult),
                                reads=[b_pbf, b_stt], writes=[b_pbf])

                            def trp(t):
                                last = None
                                for c in range(kw // 128):
                                    last = t.transpose(
                                        psT[:, c * 128:(c + 1) * 128],
                                        pbf[:, c * 128:(c + 1) * 128],
                                        G.identb[:])
                                return last
                            P.op("tensor", trp, reads=[b_pbf, G.identb_b],
                                 writes=[b_psT])
                            P.op("vector", lambda v: v.tensor_copy(
                                ptb[:, 0:kw // 128, :], psT[:, 0:kw]),
                                reads=[b_psT], writes=[b_ptb])

                            def pv(t):
                                last = None
                                for c in range(kw // 128):
                                    last = t.matmul(
                                        psO[:], vR[:, (k0t + c) % 4, h, :],
                                        ptb[:, c, :], start=(c == 0),
                                        stop=(c == kw // 128 - 1))
                                return last
                            P.op("tensor", pv, reads=[b_vR, b_ptb],
                                 writes=[b_psO])
                            P.op("vector", lambda v: v.tensor_copy(
                                afm[(h % 2) * 64:(h % 2) * 64 + 64,
                                    h // 2, tt % 2, :], psO[0:64, :]),
                                reads=[b_psO], writes=[b_afm])

                        for tt in (2 * w, 2 * w + 1):
                            off = tt % 2
                            k0t = max(2 * w - 2, 0)
                            kw = 512 if w > 0 else 256
                            moff = 0 if w > 0 else 256
                            for h in range(H):
                                rr = h % 2
                                attn_iter(h, tt, k0t, kw, moff,
                                          msks[(off, 0)], b_msks[(off, 0)],
                                          ssbR[rr][0], ssbR[rr][1],
                                          sttR[rr][0], sttR[rr][1],
                                          pbfR[rr][0], pbfR[rr][1],
                                          ptbR[rr][0], ptbR[rr][1],
                                          psSR[rr][0], psSR[rr][1],
                                          psTR[rr][0], psTR[rr][1],
                                          psOR[rr][0], psOR[rr][1])
                            P.dma("scalar", out=zres[:],
                                  in_=zsA[0][bb, tt * 128:(tt + 1) * 128, :],
                                  reads=[zsA[1]], writes=[b_zres], slot=b_zres)
                            for ds in range(2):
                                def mmo(t, ds=ds, tt=tt):
                                    last = None
                                    for ic in range(EC):
                                        last = t.matmul(
                                            psR[:], afm[:, ic, tt % 2, :],
                                            wo[:, ic, ds * 512:(ds + 1) * 512],
                                            start=(ic == 0),
                                            stop=(ic == EC - 1))
                                    return last
                                P.op("tensor", mmo, reads=[b_afm, b_wo],
                                     writes=[b_psR])
                                P.op("vector", lambda v, ds=ds: v.tensor_add(
                                    zout[:, ds * 512:(ds + 1) * 512], psR[:],
                                    zres[:, ds * 512:(ds + 1) * 512]),
                                    reads=[b_psR, b_zres], writes=[b_zout])
                            P.dma("gpsimd",
                                  out=zdA[0][bb, tt * 128:(tt + 1) * 128, :],
                                  in_=zout[:], reads=[b_zout], writes=[zdA[1]],
                                  slot=b_zout)
            P.barrier()

            zsB, zdB = zbufs[1], zbufs[0]
            with ExitStack() as es:
                w1, b_w1 = load_w_natural(C, f"l0fB_w1{li}",
                                          C.W["L0_ffn_W1"][li], E, 5460, es)
                w2, b_w2 = C.sb(f"l0fB_w2{li}", [128, NJ, E], BF16, es=es)
                P.dma("gpsimd", out=w2[:, 0:21, :],
                      in_=C.W["L0_ffn_W2"][li][0:2688].rearrange(
                          "(c p) o -> p c o", p=128),
                      writes=[b_w2], slot=b_w2)
                P.dma("gpsimd", out=w2[0:42, 21, :],
                      in_=C.W["L0_ffn_W2"][li][2688:2730],
                      writes=[b_w2], slot=b_w2)
                with ExitStack() as es_tmp:
                    g_bc, b_g_bc = bcast_row(C, G, f"l0fB_g{li}",
                                             C.W["L0_ffn_ln_g"][li], E, es,
                                             es_tmp=es_tmp)
                    lb_bc, b_lb_bc = bcast_row(C, G, f"l0fB_b{li}",
                                               C.W["L0_ffn_ln_b"][li], E, es,
                                               es_tmp=es_tmp)
                    b2_bc, b_b2_bc = bcast_row(C, G, f"l0fB_b2{li}",
                                               C.W["L0_ffn_b2"][li], E, es,
                                               es_tmp=es_tmp)
                P.barrier()
                b1r, b_b1r = C.sb(f"l0fB_b1r_{li}", [1, 5460], F32, es=es)
                P.dma("sync", out=b1r[:], in_=C.W["L0_ffn_b1"][li].unsqueeze(0),
                      writes=[b_b1r], slot=b_b1r)
                b1a, b_b1a = C.sb(f"l0fB_b1a_{li}", [128, NJ], F32, es=es)
                b1g, b_b1g = C.sb(f"l0fB_b1g_{li}", [128, NJ], F32, es=es)
                psC, b_psC = C.ps(f"l0fB_psC_{li}", [128, 2 * NJ], F32, es=es)

                def trb(t):
                    last = None
                    for j in range(NJ):
                        pj = 128 if j < 21 else 42
                        last = t.transpose(psC[0:pj, j:j + 1],
                                           b1r[0:1, j * 128:j * 128 + pj],
                                           G.identf[0:1, 0:1])
                        last = t.transpose(psC[0:pj, NJ + j:NJ + j + 1],
                                           b1r[0:1, DI + j * 128:DI + j * 128 + pj],
                                           G.identf[0:1, 0:1])
                    return last
                P.op("tensor", trb, reads=[b_b1r, G.identf_b], writes=[b_psC])

                def evb(v):
                    v.tensor_copy(b1a[:], psC[:, 0:NJ])
                    return v.tensor_copy(b1g[:], psC[:, NJ:2 * NJ])
                P.op("vector", evb, reads=[b_psC], writes=[b_b1a, b_b1g],
                     flush=True)

                xs2, b_xs2 = C.sb(f"l0fB_xs_{li}", [128, E], F32, es=es)
                xn2, b_xn2 = C.sb(f"l0fB_xn_{li}", [128, E], F32, es=es)
                xg2, b_xg2 = C.sb(f"l0fB_xg_{li}", [128, E], F32, es=es)
                xb2, b_xb2 = C.sb(f"l0fB_xb_{li}", [128, E], BF16, es=es)
                xf2, b_xf2 = C.sb(f"l0fB_xf_{li}", [128, EC, 128], BF16, es=es)
                sq2, b_sq2 = C.sb(f"l0fB_sq_{li}", [128, E], F32, es=es)
                st2, b_st2 = C.sb(f"l0fB_st_{li}", [128, 8], F32, es=es)
                gact, b_gact = C.sb(f"l0fB_gact_{li}", [128, 128], F32, es=es)
                aact, b_aact = C.sb(f"l0fB_aact_{li}", [128, 128], F32, es=es)
                hfm, b_hfm = C.sb(f"l0fB_hfm_{li}", [128, NJ, 128], BF16, es=es)
                zres2, b_zres2 = C.sb(f"l0fB_zres_{li}", [128, E], F32, es=es)
                zout2, b_zout2 = C.sb(f"l0fB_zout_{li}", [128, E], F32, es=es)
                psF2, b_psF2 = C.ps(f"l0fB_psF_{li}", [128, E], BF16, es=es)
                psA, b_psA = C.ps(f"l0fB_psA_{li}", [128, 128], F32, es=es)
                psG, b_psG = C.ps(f"l0fB_psG_{li}", [128, 128], F32, es=es)
                psY, b_psY = [list(z) for z in zip(
                    C.ps(f"l0fB_pY0_{li}", [128, 512], F32, es=es),
                    C.ps(f"l0fB_pY1_{li}", [128, 512], F32, es=es))]
                for bb in range(B):
                    for tt in range(S // 128):
                        t0 = tt * 128
                        if tt == 0:
                            P.op("gpsimd", lambda g: g.memset(
                                xs2[0:1, 512:1024], 0.0), writes=[b_xs2])
                            P.dma("sync", out=xs2[:, 0:512],
                                  in_=zsB[0][bb, 0:128, 0:512],
                                  reads=[zsB[1]], writes=[b_xs2], slot=b_xs2)
                            P.dma("sync", out=xs2[1:128, 512:1024],
                                  in_=zsB[0][bb, 0:127, 512:1024],
                                  reads=[zsB[1]], writes=[b_xs2], slot=b_xs2)
                        else:
                            P.dma("sync", out=xs2[:, 0:512],
                                  in_=zsB[0][bb, t0:t0 + 128, 0:512],
                                  reads=[zsB[1]], writes=[b_xs2], slot=b_xs2)
                            P.dma("sync", out=xs2[:, 512:1024],
                                  in_=zsB[0][bb, t0 - 1:t0 + 127, 512:1024],
                                  reads=[zsB[1]], writes=[b_xs2], slot=b_xs2)
                        ln_tile(C, G, xs2[:], b_xs2, xn2[:], b_xn2, st2, b_st2,
                                sq2[:], b_sq2)

                        def affine(v):
                            v.tensor_mul(xg2[:], xn2[:], g_bc[:])
                            v.drain()
                            return v.tensor_add(xb2[:], xg2[:], lb_bc[:])
                        P.op("vector", affine,
                             reads=[b_xn2, b_g_bc, b_lb_bc],
                             writes=[b_xg2, b_xb2], flush=True)
                        transpose_to_fm(C, G, xb2[:], b_xb2, psF2[:], b_psF2,
                                        xf2[:], b_xf2)
                        for j in range(NJ):
                            pj = 128 if j < 21 else 42

                            def mma(t, j=j, pj=pj):
                                last = None
                                for ec in range(EC):
                                    last = t.matmul(
                                        psA[0:pj, :],
                                        w1[:, ec, j * 128:j * 128 + pj],
                                        xf2[:, ec, :], start=(ec == 0),
                                        stop=(ec == EC - 1))
                                return last
                            P.op("tensor", mma, reads=[b_xf2, b_w1],
                                 writes=[b_psA])

                            def mmg(t, j=j, pj=pj):
                                last = None
                                for ec in range(EC):
                                    last = t.matmul(
                                        psG[0:pj, :],
                                        w1[:, ec, DI + j * 128:DI + j * 128 + pj],
                                        xf2[:, ec, :], start=(ec == 0),
                                        stop=(ec == EC - 1))
                                return last
                            P.op("tensor", mmg, reads=[b_xf2, b_w1],
                                 writes=[b_psG])
                            P.op("scalar", lambda s, j=j, pj=pj: s.activation(
                                gact[0:pj, :], psG[0:pj, :], AF.Gelu,
                                bias=b1g[0:pj, j:j + 1]),
                                reads=[b_psG, b_b1g], writes=[b_gact])
                            P.op("vector", lambda v, j=j, pj=pj:
                                 v.tensor_scalar(aact[0:pj, :], psA[0:pj, :],
                                                 b1a[0:pj, j:j + 1], None,
                                                 ALU.add),
                                 reads=[b_psA, b_b1a], writes=[b_aact],
                                 flush=True)
                            P.op("vector", lambda v, j=j, pj=pj: v.tensor_mul(
                                hfm[0:pj, j, :], aact[0:pj, :], gact[0:pj, :]),
                                reads=[b_aact, b_gact], writes=[b_hfm])
                        P.dma("scalar", out=zres2[:],
                              in_=zsB[0][bb, t0:t0 + 128, :],
                              reads=[zsB[1]], writes=[b_zres2], slot=b_zres2)
                        for ds in range(2):
                            def mmy(t, ds=ds):
                                last = None
                                for j in range(NJ):
                                    pj = 128 if j < 21 else 42
                                    last = t.matmul(
                                        psY[ds][:], hfm[0:pj, j, :],
                                        w2[0:pj, j, ds * 512:(ds + 1) * 512],
                                        start=(j == 0), stop=(j == NJ - 1))
                                return last
                            P.op("tensor", mmy, reads=[b_hfm, b_w2],
                                 writes=[b_psY[ds]])

                            def evy(v, ds=ds):
                                v.tensor_add(
                                    zout2[:, ds * 512:(ds + 1) * 512],
                                    psY[ds][:],
                                    b2_bc[:, ds * 512:(ds + 1) * 512])
                                v.drain()
                                return v.tensor_add(
                                    zout2[:, ds * 512:(ds + 1) * 512],
                                    zout2[:, ds * 512:(ds + 1) * 512],
                                    zres2[:, ds * 512:(ds + 1) * 512])
                            P.op("vector", evy,
                                 reads=[b_psY[ds], b_b2_bc, b_zres2],
                                 writes=[b_zout2])
                        P.dma("gpsimd",
                              out=zdB[0][bb, t0:t0 + 128, :], in_=zout2[:],
                              reads=[b_zout2], writes=[zdB[1]], slot=b_zout2)
            P.barrier()
        P.dma("gpsimd", out=y[0], in_=z0[0].rearrange("b s e -> (b s) e"),
              reads=[z0[1]], writes=[y[1]])
    P.barrier()
    stage_post_ln_add(C, G, y[0], y[1], resid, b_resid)
    P.barrier()


EMITTERS = {"L0_attn": emit_L0_attn, "L3_attn": emit_L3_attn,
            "L2_ffn": emit_L2_ffn, "L1_ffn": emit_L1_ffn,
            "L3_ffn": emit_L3_ffn, "L2_attn": emit_L2_attn,
            "L1_attn": emit_L1_attn, "L0_ffn": emit_L0_ffn}


def input_specs():
    return {
        "L0_attn_q_w": (E, E), "L0_attn_k_w": (E, E), "L0_attn_v_w": (E, E),
        "L0_attn_out_w": (E, E), "L0_attn_out_b": (E,),
        "L0_ffn_Wqkv": (6, E, 3072), "L0_ffn_q_scale": (6, 64),
        "L0_ffn_k_scale": (6, 64), "L0_ffn_Wo": (6, E, E),
        "L0_ffn_ln_g": (6, E), "L0_ffn_ln_b": (6, E),
        "L0_ffn_W1": (6, E, 5460), "L0_ffn_b1": (6, 5460),
        "L0_ffn_W2": (6, 2730, E), "L0_ffn_b2": (6, E),
        "L1_attn_Wk": (4, 128, E), "L1_attn_Wq": (4, 128, E),
        "L1_attn_Wv": (4, 128, E), "L1_attn_W_out": (E, 512),
        "L1_attn_b_out": (E,),
        "L1_ffn_router_w": (16, E), "L1_ffn_correction_bias": (16,),
        "L1_ffn_gate_w": (16, E, 256), "L1_ffn_up_w": (16, E, 256),
        "L1_ffn_down_w": (16, 256, E), "L1_ffn_shared_gate_w": (2048, E),
        "L1_ffn_shared_up_w": (2048, E), "L1_ffn_shared_down_w": (E, 2048),
        "L2_attn_w_qkv": (E, 3072), "L2_attn_w_out": (E, E),
        "L2_attn_b_out": (E,),
        "L2_ffn_Wg": (E, 8), "L2_ffn_W1": (8, E, 1408),
        "L2_ffn_W2": (8, E, 1408), "L2_ffn_W3": (8, 1408, E),
        "L2_ffn_Ws1": (E, 2816), "L2_ffn_Ws2": (E, 2816),
        "L2_ffn_Ws3": (2816, E),
        "L3_attn_Wq": (E, E), "L3_attn_bq": (E,), "L3_attn_Wk": (E, E),
        "L3_attn_bk": (E,), "L3_attn_Wv": (E, E), "L3_attn_bv": (E,),
        "L3_attn_Wo": (E, E), "L3_attn_bo": (E,),
        "L3_ffn_base_W": (E, E), "L3_ffn_base_b": (E,),
        "L3_ffn_disc_mu": (8, E), "L3_ffn_disc_sigma": (8, E),
        "L3_ffn_down_W": (8, E, 64), "L3_ffn_down_b": (8, 64),
        "L3_ffn_up_W": (8, 64, E), "L3_ffn_up_b": (8, E),
        "L3_ffn_conv_W": (8, E, E), "L3_ffn_conv_b": (8, E),
    }


def host_consts():
    consts = {}
    for name, nw in (("c_l2a_maskA", 32), ("c_l2a_maskB", 31)):
        m = np.full((128, 128), -FMAX, np.float32)
        for i2 in range(2):
            for n in range(nw):
                for i in range(2):
                    r = i2 * 2 * nw + 2 * n + i
                    m[r, 4 * n:4 * n + 4] = 0.0
        consts[name] = m
    DH, WIN = 64, 256
    half = np.arange(0, DH, 2, dtype=np.float64)
    inv_freq = 1.0 / 10000.0 ** (half / DH)
    t = np.arange(S, dtype=np.float64)
    freqs = np.concatenate([t[:, None] * inv_freq[None, :]] * 2, axis=1)
    base_scale = (half + 0.4 * DH) / (1.4 * DH)
    power = (t - S // 2) / (WIN // 2)
    scale = np.concatenate([base_scale[None, :] ** power[:, None]] * 2, axis=1)
    cos, sin = np.cos(freqs), np.sin(freqs)
    consts["c_l0f_csq"] = (cos * scale).astype(np.float32)
    consts["c_l0f_snq"] = (sin * scale).astype(np.float32)
    consts["c_l0f_csk"] = (cos / scale).astype(np.float32)
    consts["c_l0f_snk"] = (sin / scale).astype(np.float32)
    return consts


def block_weight_names(block):
    pref = block + "_"
    return [k for k in input_specs() if k.startswith(pref)]


def build_core_program(blocks):
    nc = bass.Bass("TRN2", target_bir_lowering=False)
    x_in = nc.dram_tensor("x_in", [T, E], F32, kind="ExternalInput").ap()
    x_out = nc.dram_tensor("x_out", [T, E], F32, kind="ExternalOutput").ap()
    specs = input_specs()
    consts = host_consts()
    es = ExitStack()
    with es:
        P = Prog(nc)
        C = Ctx(nc, P, es)
        for blk in blocks:
            for name in block_weight_names(blk):
                if name not in C.W:
                    C.W[name] = nc.dram_tensor(
                        name, list(specs[name]), F32, kind="ExternalInput").ap()
        for name, arr in consts.items():
            C.W[name] = nc.dram_tensor(name, list(arr.shape), F32,
                                       kind="ExternalInput").ap()
        G = Shared(C)
        resid, b_resid0 = C.dram("resid", (T, E), F32)
        b_resid = b_resid0
        P.dma("sync", out=resid, in_=x_in, writes=[b_resid])
        P.barrier()
        for blk in blocks:
            EMITTERS[blk](C, G, resid, b_resid)
        outb = Buf("x_out")
        P.dma("gpsimd", out=x_out, in_=resid, reads=[b_resid], writes=[outb])
        for tname in DEBUG_TAPS:
            ap, bufd = C.dram_cache[tname]
            tap = nc.dram_tensor("tap_" + tname, list(ap.shape), ap.dtype,
                                 kind="ExternalOutput").ap()
            P.dma("gpsimd", out=tap, in_=ap, reads=[bufd], writes=[Buf()])
        P.barrier()
        P.op("sync", lambda e: e.nop())

        with ExitStack() as semstack:
            sems = {}
            for e in ENGINES:
                sems[("c", e)] = semstack.enter_context(nc.semaphore("c_" + e))
            for i in range(Prog.NDSEM + len(ENGINES)):
                sems[("s", i)] = semstack.enter_context(nc.semaphore(f"ds{i}"))
            block = semstack.enter_context(nc.Block())
            P.playback(block, sems)
    return nc


def run_blocks(x_full, inputs, blocks):
    nc = build_core_program(blocks)
    names = set()
    for blk in blocks:
        names.update(block_weight_names(blk))
    consts = host_consts()
    in_maps = []
    for c in range(NCORES):
        m = {"x_in": np.ascontiguousarray(
            x_full[c * B:(c + 1) * B].reshape(T, E))}
        for n in names:
            m[n] = inputs[n]
        m.update(consts)
        in_maps.append(m)
    res = run_bass_kernel_spmd(nc, in_maps, core_ids=list(range(NCORES)))
    out = np.concatenate([r["x_out"].reshape(B, S, E) for r in res.results],
                         axis=0)
    return out


ALL_BLOCKS = ["L0_attn", "L0_ffn", "L1_attn", "L1_ffn",
              "L2_attn", "L2_ffn", "L3_attn", "L3_ffn"]


def kernel(**inputs):
    x = np.asarray(inputs["L0_attn_x"], dtype=np.float32)
    w = {k: np.ascontiguousarray(np.asarray(v, dtype=np.float32))
         for k, v in inputs.items() if k != "L0_attn_x"}
    todo = [b for b in ALL_BLOCKS if b in EMITTERS]
    out = run_blocks(x, w, todo)
    return out.astype(np.float32)
